# version 17
# baseline (speedup 1.0000x reference)
import numpy as np
from collections import deque
from contextlib import ExitStack
import concourse.bass as bass
import concourse.mybir as mybir
from concourse.bass_utils import run_bass_kernel_spmd

F32 = mybir.dt.float32
BF16 = mybir.dt.bfloat16
AF = mybir.ActivationFunctionType
ALU = mybir.AluOpType
AX = mybir.AxisListType

T = 2048
C = 1024
H = 16
DH = 64
INNER = 4096
NT = T // 128
IC = C // 128
EPS = 1e-5
ASCALE = 0.125
RC = 12582912.0
NG = 4
GT = NT // NG
BANK = 512


class Sem:
    __slots__ = ("h", "n", "name")

    def __init__(self, h, name):
        self.h = h
        self.n = 0
        self.name = name

    def bump(self, k):
        self.n += k
        return self.n


class Plan:
    def __init__(self):
        self.ops = {"sync": [], "gpsimd": [], "scalar": [], "vector": [],
                    "tensor": []}

    def emit(self, eng, fn):
        self.ops[eng].append(fn)

    def wait(self, eng, sem, val=None):
        v = sem.n if val is None else val
        if v is None or v <= 0:
            return
        self.emit(eng, lambda e, s=sem.h, v=v: e.wait_ge(s, v))

    def dma(self, eng, out, in_, sem, **kw):
        def fn(e, o=out, i=in_, s=sem.h, kw=dict(kw)):
            e.dma_start(o, i, **kw).then_inc(s, 16)
        self.emit(eng, fn)
        return sem.bump(16)

    def op(self, eng, fn, sem=None, k=1):
        if sem is None:
            self.emit(eng, fn)
            return None

        def wrapped(e, fn=fn, s=sem.h, k=k):
            fn(e).then_inc(s, k)
        self.emit(eng, wrapped)
        return sem.bump(k)


def build_nc(debug=False):
    nc = bass.Bass("TRN2", target_bir_lowering=False)

    x_h = nc.dram_tensor("x", [T, C], F32, kind="ExternalInput").ap()
    wq_h = nc.dram_tensor("wq", [C, C], F32, kind="ExternalInput").ap()
    wk_h = nc.dram_tensor("wk", [C, C], F32, kind="ExternalInput").ap()
    wv_h = nc.dram_tensor("wv", [C, C], F32, kind="ExternalInput").ap()
    wp_h = nc.dram_tensor("wp", [C, C], F32, kind="ExternalInput").ap()
    wg_h = nc.dram_tensor("wg", [INNER, C], F32, kind="ExternalInput").ap()
    wu_h = nc.dram_tensor("wu", [INNER, C], F32, kind="ExternalInput").ap()
    wd_h = nc.dram_tensor("wd", [C, INNER], F32, kind="ExternalInput").ap()
    out_h = nc.dram_tensor("out", [T, C], F32, kind="ExternalOutput").ap()

    kd = "ExternalOutput" if debug else "Internal"
    kd2 = "ExternalOutput" if debug == 2 else kd
    wqb = nc.dram_tensor("wqb", [C, C], BF16, kind=kd).ap()
    wkb = nc.dram_tensor("wkb", [C, C], BF16, kind=kd).ap()
    wvb = nc.dram_tensor("wvb", [C, C], BF16, kind=kd).ap()
    wpb = nc.dram_tensor("wpb", [C, C], BF16, kind=kd).ap()
    x1d = nc.dram_tensor("x1d", [T, C], F32, kind=kd2).ap()
    xqd = nc.dram_tensor("xqd", [T, C], BF16, kind=kd2).ap()
    wgq = nc.dram_tensor("wgq", [INNER, C], BF16, kind=kd2).ap()
    wuq = nc.dram_tensor("wuq", [INNER, C], BF16, kind=kd2).ap()
    wdq = nc.dram_tensor("wdq", [C, INNER], BF16, kind=kd2).ap()
    if debug:
        dbgh = nc.dram_tensor("dbgh", [T, INNER], F32, kind=kd).ap()
        dbghq = nc.dram_tensor("dbghq", [T, INNER], BF16, kind=kd).ap()
        dbgffo = nc.dram_tensor("dbgffo", [T, C], F32, kind=kd).ap()
        dbgvA2 = nc.dram_tensor("dbgvA2", [T, H * 65], BF16, kind=kd).ap()
        dbgWv = nc.dram_tensor("dbgWv", [C, C], BF16, kind=kd).ap()
        dbgsc = nc.dram_tensor("dbgsc", [128, 8], F32, kind=kd).ap()
        dbgacc = nc.dram_tensor("dbgacc", [128, 192], F32, kind=kd).ap()
        dbgAT = nc.dram_tensor("dbgAT", [C, T], BF16, kind=kd).ap()
        dbgqT = nc.dram_tensor("dbgqT", [C, T], BF16, kind=kd).ap()
        dbgkT = nc.dram_tensor("dbgkT", [C, T], BF16, kind=kd).ap()
        dbgyT = nc.dram_tensor("dbgyT", [C, T], BF16, kind=kd).ap()
        dbgvA = nc.dram_tensor("dbgvA", [T, H * 65], BF16, kind=kd).ap()

    p = Plan()
    ctx = ExitStack()
    with ctx:
        EG = ctx.enter_context

        S = {}
        for name in [
            "g_cast", "g_cast1", "g_cast2", "g_cast3", "s_xld", "s_xld1",
            "s_xld2", "s_xld3",
            "g_x1st", "g_xqst", "g_xqst1", "g_xqst2", "g_xqst3", "g_xqst4",
            "g_xqst5", "g_xqst6", "g_xqst7", "wb0", "wb1", "wb2", "wb3",
            "s_wstr1", "s_xqtg1", "s_wdc1", "s_wdc2", "s_wdc3",
            "g_prepA", "g_prepB",
            "g_prepst", "g_outst", "s_wtr", "s_xqtg", "s_wstr", "s_wdc",
            "p1_st", "p1_rs", "p1_nm", "p1_ln", "p1_tr", "at_evD", "at_evA",
            "prj_mm", "prj_evA", "prj_evD", "c_smm", "c_expA", "c_msk",
            "c_avmm", "c_lcp", "c_rbf", "c_rmm", "c_rev", "c_yev",
            "d_aomm", "d_sg1", "d_sg2", "d_sg3", "d_sg4", "d_sg5", "d_sg6",
            "pp_abs", "pp_eop", "pp_fin", "pp_q1", "pp_q2", "pp_q3", "e_gmm", "e_sgA",
            "e_hD", "e_amx", "e_qA", "e_qD", "e_trm", "e_trev", "e_trev2",
            "e_dmm",
            "g_hqst", "s_hqt",
            "e_deA", "e_fin", "misc", "misc2",
        ]:
            S[name] = Sem(EG(nc.semaphore(name)), name)

        PS = EG(nc.psum_tensor("PS", [128, 8 * BANK], F32, unsafe=True))

        def bank(i, n=1):
            return PS[:, i * BANK:(i + n) * BANK]

        bank_w = [(None, 0)] * 8
        bank_r = [(None, 0)] * 8

        def pe_writes_bank(bi, n=1):
            for b in range(bi, bi + n):
                s, v = bank_r[b]
                if s is not None:
                    p.wait("tensor", s, v)

        def note_bank_write(bi, sem, val, n=1):
            for b in range(bi, bi + n):
                bank_w[b] = (sem, val)

        def reader_waits_bank(eng, bi, n=1):
            for b in range(bi, bi + n):
                s, v = bank_w[b]
                if s is not None:
                    p.wait(eng, s, v)

        def note_bank_read(bi, sem, val, n=1):
            for b in range(bi, bi + n):
                bank_r[b] = (sem, val)

        ident = EG(nc.sbuf_tensor("ident", [128, 128], BF16))
        ones_row_b = EG(nc.sbuf_tensor("ones_row_b", [1, 128], BF16))
        ones_row_f = EG(nc.sbuf_tensor("ones_row_f", [1, 128], F32))
        ones_col_f = EG(nc.sbuf_tensor("ones_col_f", [128, 1], F32))
        epsc = EG(nc.sbuf_tensor("epsc", [128, 1], F32))
        rcc = EG(nc.sbuf_tensor("rcc", [128, 1], F32))

        xt = [EG(nc.sbuf_tensor(f"xt{i}", [128, C], F32)) for i in range(2)]
        bn6 = [EG(nc.sbuf_tensor(f"bn6_{i}", [128, 12], F32)) for i in range(4)]
        sp = [EG(nc.sbuf_tensor(f"sp{i}", [128, 12], F32)) for i in range(4)]
        nmrc = [EG(nc.sbuf_tensor(f"nmrc{i}", [128, 1], F32)) for i in range(4)]
        rga_all = EG(nc.sbuf_tensor("rga_all", [128, NT], F32))
        scol_w = [EG(nc.sbuf_tensor(f"scolw{i}", [128, 1], F32)) for i in range(3)]
        rgw_col = [EG(nc.sbuf_tensor(f"rgwcol{i}", [128, 1], F32)) for i in range(3)]
        wstat = EG(nc.sbuf_tensor("wstat", [1, 4], F32))

        junkV = EG(nc.sbuf_tensor("junkV", [128, 1], F32))
        junkG = EG(nc.sbuf_tensor("junkG", [128, 1], BF16))
        junkA = EG(nc.sbuf_tensor("junkA", [128, 1], F32))
        fincol = EG(nc.sbuf_tensor("fincol", [128, 1], F32))
        maskb = EG(nc.sbuf_tensor("maskb", [128, 128], BF16))
        masknT = EG(nc.sbuf_tensor("masknT", [128, 128], BF16))
        sel4 = EG(nc.sbuf_tensor("sel4", [4, 256], BF16))
        sel4a = EG(nc.sbuf_tensor("sel4a", [4, 256], F32))
        sel4b = EG(nc.sbuf_tensor("sel4b", [4, 256], F32))

        rstd = [sp[i][:, 11:12] for i in range(4)]

        def jv():
            p.op("vector", lambda e: e.tensor_copy(junkV[:], epsc[:]))

        def ja():
            p.op("scalar", lambda e: e.activation(junkA[:], epsc[:],
                                                  AF.Identity))

        def vsp(sem):
            return p.op("vector", lambda e: e.tensor_copy(junkV[:], epsc[:]), sem)

        def asp(sem):
            return p.op("scalar", lambda e: e.activation(
                junkA[:], epsc[:], AF.Identity), sem)

        def init_consts(e):
            e.memset(ident[:], 1.0)
            e.affine_select(ident[:], ident[:], [[1, 128]], ALU.is_equal, 0.0,
                            channel_multiplier=-1)
            e.memset(ones_row_b[:], 1.0)
            e.memset(ones_row_f[:], 1.0)
            e.memset(epsc[:], EPS)
            e.memset(rcc[:], RC)

            e.memset(maskb[:], 1.0)
            e.affine_select(maskb[:], maskb[:], [[1, 128]], ALU.is_ge, 0.0,
                            channel_multiplier=-1)
            e.memset(masknT[:], -30000.0)
            e.affine_select(masknT[:], masknT[:], [[1, 128]], ALU.is_gt, 0.0,
                            channel_multiplier=-1)
            e.memset(sel4a[:], 1.0)
            e.affine_select(sel4a[:], sel4a[:], [[1, 256]], ALU.is_ge, 0.0,
                            channel_multiplier=-64)
            e.memset(sel4b[:], 1.0)
            e.affine_select(sel4b[:], sel4b[:], [[1, 256]], ALU.is_ge, 0.0,
                            base=-64, channel_multiplier=-64)
            return e.memset(ones_col_f[:], 1.0)
        p.op("gpsimd", init_consts, S["misc"])
        consts_done = S["misc"].n
        for _eng in ["sync", "scalar", "vector", "tensor"]:
            p.wait(_eng, S["misc"], consts_done)
        p.op("vector", lambda e: e.tensor_sub(sel4[:], sel4a[:], sel4b[:]))
        sel4_done = vsp(S["misc2"])
        p.wait("tensor", S["misc2"], sel4_done)

        cast_sems = [S["g_cast"], S["g_cast1"], S["g_cast2"], S["g_cast3"]]
        for ci, (csrc, cdst) in enumerate([(wq_h, wqb), (wk_h, wkb),
                                           (wv_h, wvb), (wp_h, wpb)]):
            p.dma("gpsimd", cdst, csrc, cast_sems[ci])

        def ln_stats(par, src_ap, sg_stats, sg_rstd, sg_nm, pre_waits_v=()):
            s = sp[par]
            b = bn6[par]
            m = [b[:, 1:2], b[:, 4:5], b[:, 7:8], b[:, 10:11]]
            M = [b[:, 2:3], b[:, 5:6], b[:, 8:9], b[:, 11:12]]
            for eng, sem, v in pre_waits_v:
                p.wait(eng, sem, v)
            p.op("vector", lambda e, b=b, src=src_ap: e.bn_stats(
                b[:, 0:6], src[:, 0:512]))
            p.op("vector", lambda e, b=b, src=src_ap: e.bn_stats(
                b[:, 6:12], src[:, 512:1024]))
            p.op("vector", lambda e, s=s, m=m: e.tensor_add(s[:, 0:1], m[0], m[1]))
            p.op("vector", lambda e, s=s, m=m: e.tensor_add(s[:, 1:2], m[2], m[3]))
            p.op("vector", lambda e, s=s, M=M: e.tensor_add(s[:, 4:5], M[0], M[1]))
            p.op("vector", lambda e, s=s, M=M: e.tensor_add(s[:, 5:6], M[2], M[3]))
            p.op("vector", lambda e, s=s, m=m: e.tensor_mul(s[:, 7:8], m[0], m[0]))
            p.op("vector", lambda e, s=s, m=m: e.tensor_mul(s[:, 8:9], m[1], m[1]))
            p.op("vector", lambda e, s=s: e.tensor_add(s[:, 2:3], s[:, 0:1], s[:, 1:2]))
            p.op("vector", lambda e, s=s: e.tensor_add(s[:, 6:7], s[:, 4:5], s[:, 5:6]))
            p.op("vector", lambda e, s=s: e.tensor_add(s[:, 7:8], s[:, 7:8], s[:, 8:9]))
            p.op("vector", lambda e, s=s, m=m: e.tensor_mul(s[:, 8:9], m[2], m[2]))
            p.op("vector", lambda e, s=s: e.tensor_scalar_mul(s[:, 3:4], s[:, 2:3], 0.25))
            p.op("vector", lambda e, s=s: e.tensor_add(s[:, 7:8], s[:, 7:8], s[:, 8:9]))
            p.op("vector", lambda e, s=s, m=m: e.tensor_mul(s[:, 8:9], m[3], m[3]))
            p.op("vector", lambda e, s=s: e.tensor_mul(s[:, 1:2], s[:, 3:4], s[:, 3:4]))
            p.op("vector", lambda e, s=s: e.tensor_add(s[:, 7:8], s[:, 7:8], s[:, 8:9]))
            jv()
            p.op("vector", lambda e, s=s: e.tensor_scalar_mul(s[:, 0:1], s[:, 7:8], 0.25))
            jv()
            p.op("vector", lambda e, s=s: e.scalar_tensor_tensor(
                s[:, 9:10], s[:, 6:7], 1.0 / C, s[:, 0:1], ALU.mult, ALU.add))
            jv()
            p.op("vector", lambda e, s=s: e.tensor_sub(
                s[:, 10:11], s[:, 9:10], s[:, 1:2]))
            st_v = vsp(sg_stats)
            p.wait("scalar", sg_stats, st_v)
            p.op("scalar", lambda e, s=s: e.activation(s[:, 9:10], s[:, 10:11],
                                                       AF.Ln, bias=epsc[:]))
            ja()
            p.op("scalar", lambda e, s=s, par=par: e.activation(
                rstd[par], s[:, 9:10], AF.Exp, scale=-0.5))
            rs_v = asp(sg_rstd)
            p.wait("vector", sg_rstd, rs_v)
            p.op("vector", lambda e, s=s, par=par: e.tensor_scalar(
                nmrc[par][:], s[:, 3:4], rstd[par], -1.0, ALU.mult, ALU.mult))
            nm_v = vsp(sg_nm)
            return rs_v, nm_v

        prep_srcs = []
        for mi, wh in [(0, wg_h), (1, wu_h)]:
            for ch in range(32):
                for hf in range(2):
                    prep_srcs.append(
                        (mi, wh[ch * 128:(ch + 1) * 128, hf * 512:(hf + 1) * 512],
                         None))
        for rr in range(8):
            for cpc in range(8):
                prep_srcs.append(
                    (2, wd_h[rr * 128:(rr + 1) * 128, cpc * 512:(cpc + 1) * 512],
                     (rr, cpc)))

        prep_abs_v = [0] * 192
        prep_q3_v = [0] * 192
        prep_fin_v = [0, 0, 0]
        prep_state = {"st": 0}

        s_prep = ExitStack()
        PE_ = s_prep.enter_context
        wbuf = [PE_(nc.sbuf_tensor(f"wbuf{i}", [128, 512], F32)) for i in range(4)]
        pstage = [PE_(nc.sbuf_tensor(f"pstage{i}", [128, 512], F32)) for i in range(2)]
        pout = [PE_(nc.sbuf_tensor(f"pout{i}", [128, 512], BF16)) for i in range(2)]
        acc_strip = PE_(nc.sbuf_tensor("acc_strip", [128, 192], F32))

        def prep1_thunk(idx):
            def emit():
                par = idx % 4
                wsems = [S["wb0"], S["wb1"], S["wb2"], S["wb3"]]
                if idx == 0:
                    for j in range(4):
                        p.dma("gpsimd", wbuf[j][:], prep_srcs[j][1], wsems[j])
                elif idx + 3 < 192:
                    p.wait("gpsimd", S["pp_abs"], idx)
                    p.dma("gpsimd", wbuf[(idx + 3) % 4][:],
                          prep_srcs[idx + 3][1], wsems[(idx + 3) % 4])
                p.wait("vector", wsems[par], 16 * (idx // 4 + 1))
                prep_abs_v[idx] = p.op("vector", lambda e, par=par, idx=idx: e.tensor_reduce(
                    acc_strip[:, idx:idx + 1], wbuf[par][:], AX.X, ALU.add,
                    apply_absolute_value=True), S["pp_abs"])
            return emit

        def prep2_dst(idx):
            mi, _, wdpos = prep_srcs[idx]
            if mi == 0:
                ch, hf = (idx % 64) // 2, idx % 2
                return wgq[ch * 128:(ch + 1) * 128, hf * 512:(hf + 1) * 512]
            if mi == 1:
                j = idx - 64
                ch, hf = j // 2, j % 2
                return wuq[ch * 128:(ch + 1) * 128, hf * 512:(hf + 1) * 512]
            rr, cpc = wdpos
            return wdq[rr * 128:(rr + 1) * 128, cpc * 512:(cpc + 1) * 512]

        def prep2_thunk(idx):
            def emit():
                mi = prep_srcs[idx][0]
                par = idx % 4
                spar = idx % 2
                wsems = [S["wb0"], S["wb1"], S["wb2"], S["wb3"]]
                if idx == 0:
                    p.wait("gpsimd", S["pp_abs"], 192)
                    for j in range(4):
                        p.dma("gpsimd", wbuf[j][:], prep_srcs[j][1], wsems[j])
                elif idx + 3 < 192:
                    p.wait("gpsimd", S["pp_q1"], idx)
                    p.dma("gpsimd", wbuf[(idx + 3) % 4][:],
                          prep_srcs[idx + 3][1], wsems[(idx + 3) % 4])
                p.wait("scalar", wsems[par], 16 * (48 + idx // 4 + 1))
                p.wait("scalar", S["pp_fin"], prep_fin_v[mi])
                if idx >= 2:
                    p.wait("scalar", S["pp_q3"], idx - 1)
                q1 = p.op("scalar", lambda e, par=par, mi=mi, spar=spar: e.activation(
                    pstage[spar][:], wbuf[par][:], AF.Identity,
                    scale=scol_w[mi][:], bias=rcc[:]), S["pp_q1"])
                p.wait("vector", S["pp_q1"], q1)
                p.op("vector", lambda e, spar=spar: e.tensor_scalar(
                    pstage[spar][:], pstage[spar][:], RC + 1.0, RC - 1.0,
                    ALU.min, ALU.max), S["pp_q2"])
                if idx >= 2:
                    p.wait("vector", S["g_prepst"], 16 * (idx - 1))
                q3 = p.op("vector", lambda e, spar=spar: e.tensor_scalar(
                    pout[spar][:], pstage[spar][:], -RC, None, ALU.add),
                    S["pp_q3"])
                prep_q3_v[idx] = q3
                p.wait("gpsimd", S["pp_q3"], q3)
                prep_state["st"] = p.dma("gpsimd", prep2_dst(idx),
                                         pout[spar][:], S["g_prepst"])
            return emit

        def prep_fin_thunk(mi):
            def emit():
                lo, hi = mi * 64, (mi + 1) * 64
                bi = 6
                p.wait("vector", S["pp_abs"], prep_abs_v[hi - 1])
                p.op("vector", lambda e: e.reduce_sum(
                    fincol[:], acc_strip[:, lo:hi], AX.X))
                jv()
                cs_v = vsp(S["pp_fin"])
                p.wait("tensor", S["pp_fin"], cs_v)
                pe_writes_bank(bi)
                mm_v = p.op("tensor", lambda e, bi=bi: e.matmul(
                    bank(bi)[0:1, 0:1], fincol[:], ones_col_f[:],
                    start=True, stop=True), S["pp_fin"])
                note_bank_write(bi, S["pp_fin"], mm_v)
                reader_waits_bank("vector", bi)
                jv()
                jv()
                p.op("vector", lambda e, bi=bi: e.tensor_copy(wstat[:, 0:1],
                                                              bank(bi)[0:1, 0:1]))
                jv()
                p.op("vector", lambda e: e.tensor_scalar(
                    wstat[:, 1:2], wstat[:, 0:1], 1.0 / (INNER * C), 1e-5,
                    ALU.mult, ALU.max))
                jv()
                p.op("vector", lambda e: e.reciprocal(
                    wstat[:, 2:3], wstat[:, 1:2]))
                rc_v = vsp(S["pp_fin"])
                note_bank_read(bi, S["pp_fin"], rc_v)
                p.wait("tensor", S["pp_fin"], rc_v)
                pe_writes_bank(bi)
                p.op("tensor", lambda e, bi=bi: e.matmul(
                    bank(bi)[:, 0:1], ones_row_f[:], wstat[:, 2:3],
                    start=True, stop=True), S["pp_fin"])
                mm3 = p.op("tensor", lambda e, bi=bi: e.matmul(
                    bank(bi)[:, 1:2], ones_row_f[:], wstat[:, 1:2],
                    start=True, stop=True), S["pp_fin"])
                note_bank_write(bi, S["pp_fin"], mm3)
                reader_waits_bank("vector", bi)
                jv()
                jv()
                p.op("vector", lambda e, mi=mi, bi=bi: e.tensor_copy(
                    scol_w[mi][:], bank(bi)[:, 0:1]))
                p.op("vector", lambda e, mi=mi, bi=bi: e.tensor_copy(
                    rgw_col[mi][:], bank(bi)[:, 1:2]))
                ev_v = vsp(S["pp_fin"])
                note_bank_read(bi, S["pp_fin"], ev_v)
                prep_fin_v[mi] = S["pp_fin"].n
            return emit

        def prep2_dst(idx):
            mi, _, wdpos = prep_srcs[idx]
            if mi == 0:
                ch, hf = (idx % 64) // 2, idx % 2
                return wgq[ch * 128:(ch + 1) * 128, hf * 512:(hf + 1) * 512]
            if mi == 1:
                j = idx - 64
                ch, hf = j // 2, j % 2
                return wuq[ch * 128:(ch + 1) * 128, hf * 512:(hf + 1) * 512]
            rr, cpc = wdpos
            return wdq[rr * 128:(rr + 1) * 128, cpc * 512:(cpc + 1) * 512]

        def prep2_all():
            H0, H1 = slice(0, 256), slice(256, 512)
            for idx in range(192):
                par = idx % 2
                mi = prep_srcs[idx][0]
                if idx >= 2:
                    p.wait("gpsimd", S["pp_q3"], prep_q3_v[idx - 2])
                ld = p.dma("gpsimd", wbuf[par][:], prep_srcs[idx][1],
                           S["g_prepB"])
                p.wait("gpsimd", S["g_prepB"], ld)
                p.wait("gpsimd", S["pp_fin"], prep_fin_v[mi])
                for hs in (H0, H1):
                    p.op("gpsimd", lambda e, par=par, mi=mi, hs=hs: e.tensor_tensor(
                        pstage[par][:, hs], wbuf[par][:, hs], scol256[mi][:],
                        ALU.mult))
                for hs in (H0, H1):
                    p.op("gpsimd", lambda e, par=par, hs=hs: e.tensor_tensor(
                        pst2[par][:, hs], pstage[par][:, hs], rc256[:],
                        ALU.add))
                for hs in (H0, H1):
                    p.op("gpsimd", lambda e, par=par, hs=hs: e.tensor_tensor(
                        pstage[par][:, hs], pst2[par][:, hs], rcp256[:],
                        ALU.min))
                for hs in (H0, H1):
                    p.op("gpsimd", lambda e, par=par, hs=hs: e.tensor_tensor(
                        pst2[par][:, hs], pstage[par][:, hs], rcm256[:],
                        ALU.max))
                if idx >= 2:
                    p.wait("gpsimd", S["g_prepst"], 16 * (idx - 1))
                q3 = None
                for hs in (H0, H1):
                    q3 = p.op("gpsimd", lambda e, par=par, hs=hs: e.tensor_tensor(
                        pout[par][:, hs], pst2[par][:, hs], nrc256[:],
                        ALU.add), S["pp_q3"])
                prep_q3_v[idx] = q3
                prep_state["st"] = p.dma("gpsimd", prep2_dst(idx),
                                         pout[par][:], S["g_prepst"])
            return

        prep_queue = deque()
        for i in range(64):
            prep_queue.append(prep1_thunk(i))
        prep_queue.append(prep_fin_thunk(0))
        for i in range(64, 128):
            prep_queue.append(prep1_thunk(i))
        prep_queue.append(prep_fin_thunk(1))
        for i in range(128, 192):
            prep_queue.append(prep1_thunk(i))
        prep_queue.append(prep_fin_thunk(2))
        for i in range(192):
            prep_queue.append(prep2_thunk(i))

        def drip(n):
            for _ in range(n):
                if prep_queue:
                    prep_queue.popleft()()

        s_qkv = ExitStack()
        Q_ = s_qkv.enter_context
        qT = [Q_(nc.sbuf_tensor(f"qT{i}", [128, T], BF16)) for i in range(IC)]
        kT = [Q_(nc.sbuf_tensor(f"kT{i}", [128, T], BF16)) for i in range(IC)]
        vA = [Q_(nc.sbuf_tensor(f"vA{i}", [128, H * 65], BF16)) for i in range(NT)]

        s_a = ExitStack()
        A_ = s_a.enter_context
        AT = [A_(nc.sbuf_tensor(f"AT{i}", [128, T], BF16)) for i in range(IC)]
        WTa = [[A_(nc.sbuf_tensor(f"WTa{j}_{i}", [128, C], BF16))
                for i in range(IC)] for j in range(2)]
        s_lnt = ExitStack()
        lnt = [s_lnt.enter_context(nc.sbuf_tensor(f"lnt{i}", [128, C], BF16))
               for i in range(2)]

        for i in range(NT):
            p.op("gpsimd", lambda e, t=vA[i]: e.memset(t[:], 1.0), S["misc"])
        vones_done = S["misc"].n

        p1_ln_v = [0] * NT
        for tt in range(NT):
            par = tt % 2
            if tt >= 2:
                p.wait("sync", S["p1_ln"], p1_ln_v[tt - 2])
            xsem = S["s_xld"] if par == 0 else S["s_xld1"]
            xld = p.dma("sync", xt[par][:], x_h[tt * 128:(tt + 1) * 128, :],
                        xsem)
            rs_v, nm_v = ln_stats(par, xt[par][:], S["p1_st"], S["p1_rs"],
                                  S["p1_nm"],
                                  pre_waits_v=[("vector", xsem, xld)])
            p.wait("scalar", S["p1_nm"], nm_v)
            if tt >= 2:
                p.wait("scalar", S["p1_tr"], (tt - 1) * IC)
            ln_v = p.op("scalar", lambda e, par=par: e.activation(
                lnt[par][:], xt[par][:], AF.Identity, bias=nmrc[par][:],
                scale=rstd[par]), S["p1_ln"])
            p1_ln_v[tt] = ln_v
            p.wait("tensor", S["p1_ln"], ln_v)
            if tt == 0:
                p.wait("tensor", S["misc"], consts_done)
            for ic in range(IC):
                bi = ic % 4
                pe_writes_bank(bi)
                mm_v = p.op("tensor", lambda e, par=par, ic=ic, bi=bi: e.matmul(
                    bank(bi)[:, 0:128], lnt[par][:, ic * 128:(ic + 1) * 128],
                    ident[:], start=True, stop=True), S["p1_tr"])
                note_bank_write(bi, S["p1_tr"], mm_v)
                eng, sem = (("vector", S["at_evD"]) if ic % 2 == 0
                            else ("scalar", S["at_evA"]))
                p.wait(eng, S["p1_tr"], mm_v + 1)
                reader_waits_bank(eng, bi)
                if eng == "vector":
                    ev_v = p.op(eng, lambda e, ic=ic, tt=tt, bi=bi: e.tensor_copy(
                        AT[ic][:, tt * 128:(tt + 1) * 128], bank(bi)[:, 0:128]),
                        sem)
                else:
                    ev_v = p.op(eng, lambda e, ic=ic, tt=tt, bi=bi: e.activation(
                        AT[ic][:, tt * 128:(tt + 1) * 128], bank(bi)[:, 0:128],
                        AF.Identity), sem)
                note_bank_read(bi, sem, ev_v)
            pe_writes_bank(6)
            dmy = p.op("tensor", lambda e: e.matmul(
                bank(6)[:, 0:128], ident[:], ident[:], start=True, stop=True),
                S["p1_tr"])
            note_bank_write(6, S["p1_tr"], dmy)
        vsp(S["at_evD"])
        asp(S["at_evA"])
        at_evD_done = S["at_evD"].n
        at_evA_done = S["at_evA"].n
        s_lnt.close()

        def load_wta(j, wb_dram, cast_idx, war_v):
            p.wait("sync", cast_sems[cast_idx], 16)
            if war_v:
                p.wait("sync", S["prj_mm"], war_v)
            last = 0
            for ic in range(IC):
                last = p.dma("sync", WTa[j][ic][:],
                             wb_dram[0:C, ic * 128:(ic + 1) * 128],
                             S["s_wtr"], transpose=True)
            return last

        def proj_fm(j, dstT, wtr_v, scale, evac_eng, evac_sem):
            p.wait("tensor", S["s_wtr"], wtr_v)
            for ts in range(4):
                p.wait("tensor", S["at_evD"], min(4 * (4 * ts + 4) + 1, at_evD_done))
                p.wait("tensor", S["at_evA"], min(4 * (4 * ts + 4) + 1, at_evA_done))
                for oc in range(IC):
                    bi = (oc * 4 + ts) % 4
                    pe_writes_bank(bi)
                    mm_v = None
                    for ic in range(IC):
                        r = p.op("tensor", lambda e, j=j, ic=ic, oc=oc, ts=ts, bi=bi: e.matmul(
                            bank(bi), WTa[j][ic][:, oc * 128:(oc + 1) * 128],
                            AT[ic][:, ts * 512:(ts + 1) * 512],
                            start=(ic == 0), stop=(ic == IC - 1)),
                            S["prj_mm"] if ic == IC - 1 else None)
                        if r is not None:
                            mm_v = r
                    note_bank_write(bi, S["prj_mm"], mm_v)
                    reader_waits_bank(evac_eng, bi)
                    if evac_eng == "scalar":
                        ev_v = p.op(evac_eng, lambda e, oc=oc, ts=ts, bi=bi, d=dstT, sc=scale: e.activation(
                            d[oc][:, ts * 512:(ts + 1) * 512], bank(bi),
                            AF.Identity, scale=sc), evac_sem)
                    else:
                        ev_v = p.op(evac_eng, lambda e, oc=oc, ts=ts, bi=bi, d=dstT: e.tensor_copy(
                            d[oc][:, ts * 512:(ts + 1) * 512], bank(bi)),
                            evac_sem)
                    note_bank_read(bi, evac_sem, ev_v)
                    drip(3)

        wtr_q = load_wta(0, wqb, 0, 0)
        wtr_k = load_wta(1, wkb, 1, 0)
        proj_fm(0, qT, wtr_q, ASCALE, "scalar", S["prj_evA"])
        proj_fm(1, kT, wtr_k, 1.0, "vector", S["prj_evD"])
        wtr_v = load_wta(0, wvb, 2, S["prj_mm"].n - 32)

        p.wait("tensor", S["s_wtr"], wtr_v)
        p.wait("vector", S["misc"], vones_done)
        p.wait("scalar", S["misc"], vones_done)
        for tt in range(NT):
            for oc2 in range(2):
                bi = 4 + (tt * 2 + oc2) % 2
                pe_writes_bank(bi)
                mm_v = None
                for ic in range(IC):
                    r = p.op("tensor", lambda e, ic=ic, tt=tt, oc2=oc2, bi=bi: e.matmul(
                        bank(bi), AT[ic][:, tt * 128:(tt + 1) * 128],
                        WTa[0][ic][:, oc2 * 512:(oc2 + 1) * 512],
                        start=(ic == 0), stop=(ic == IC - 1)),
                        S["prj_mm"] if ic == IC - 1 else None)
                    if r is not None:
                        mm_v = r
                note_bank_write(bi, S["prj_mm"], mm_v)
                eng, sem = (("vector", S["prj_evD"]) if oc2 == 0
                            else ("scalar", S["prj_evA"]))
                reader_waits_bank(eng, bi)
                ev_v = None
                for hh in range(8):
                    h = oc2 * 8 + hh
                    if eng == "vector":
                        ev_v = p.op(eng, lambda e, tt=tt, h=h, hh=hh, bi=bi: e.tensor_copy(
                            vA[tt][:, h * 65:h * 65 + 64],
                            bank(bi)[:, hh * 64:(hh + 1) * 64]), sem)
                    else:
                        ev_v = p.op(eng, lambda e, tt=tt, h=h, hh=hh, bi=bi: e.activation(
                            vA[tt][:, h * 65:h * 65 + 64],
                            bank(bi)[:, hh * 64:(hh + 1) * 64], AF.Identity), sem)
                note_bank_read(bi, sem, ev_v)
                drip(3)
        vsp(S["prj_evD"])
        asp(S["prj_evA"])
        v_doneD = S["prj_evD"].n
        v_doneA = S["prj_evA"].n

        if debug == 1:
            p.wait("gpsimd", S["at_evD"], at_evD_done)
            p.wait("gpsimd", S["at_evA"], at_evA_done)
            for ic in range(IC):
                p.dma("gpsimd", dbgAT[ic * 128:(ic + 1) * 128, :], AT[ic][:],
                      S["misc2"])
            p.wait("gpsimd", S["prj_evD"], v_doneD)
            p.wait("gpsimd", S["prj_evA"], v_doneA)
            for tt in range(NT):
                p.dma("gpsimd", dbgvA2[tt * 128:(tt + 1) * 128, :], vA[tt][:],
                      S["misc2"])
            for ic in range(IC):
                p.dma("gpsimd", dbgWv[ic * 128:(ic + 1) * 128, :],
                      WTa[0][ic][:], S["misc2"])
            p.wait("gpsimd", S["misc2"], S["misc2"].n)

        s_a.close()

        s_y = ExitStack()
        Y_ = s_y.enter_context
        yT = [Y_(nc.sbuf_tensor(f"yT{i}", [128, T], BF16, side="right"))
              for i in range(IC)]
        WTb = [Y_(nc.sbuf_tensor(f"WTb{i}", [128, C], BF16, side="right"))
               for i in range(IC)]

        p.wait("sync", S["g_cast3"], 16)
        p.wait("sync", S["prj_mm"], S["prj_mm"].n)
        wtr_p = 0
        for ic in range(IC):
            wtr_p = p.dma("sync", WTb[ic][:], wpb[0:C, ic * 128:(ic + 1) * 128],
                          S["s_wtr"], transpose=True)

        s_c = ExitStack()
        C_ = s_c.enter_context
        ptile = [C_(nc.sbuf_tensor(f"pt{i}", [128, BANK], BF16)) for i in range(4)]
        yu = [C_(nc.sbuf_tensor(f"yu{i}", [65, BANK], F32)) for i in range(4)]
        lall = C_(nc.sbuf_tensor("lall", [4, BANK], F32))
        rall = C_(nc.sbuf_tensor("rall", [4, BANK], F32))
        rallb = C_(nc.sbuf_tensor("rallb", [4, BANK], BF16))

        p.wait("tensor", S["prj_evA"], v_doneA)
        p.wait("tensor", S["prj_evD"], v_doneD)

        SBK = [0, 1, 6, 7]
        YBK = [2, 3, 4, 5]
        yev_done = [0, 0, 0, 0]
        av_prev = [None] * 4
        rb_prev = 0
        pending = []

        def emit_tail(st):
            si = st["si"]
            sbk = SBK[si]
            pe_writes_bank(sbk)
            if st.get("heat"):
                p.wait("tensor", S["c_lcp"], st["ev"])
                p.op("tensor", lambda e, sbk=sbk: e.matmul(
                    bank(sbk)[0:1, 0:1], ones_row_b[:, 0:1],
                    ones_row_b[:, 0:1], start=True, stop=True))
                p.wait("tensor", S["c_rev"], st["ld"])
                p.op("tensor", lambda e, sbk=sbk: e.matmul(
                    bank(sbk)[0:1, 0:1], ones_row_b[:, 0:1],
                    ones_row_b[:, 0:1], start=True, stop=True))
            p.wait("tensor", S["c_rbf"], st["rb"])
            rmm = p.op("tensor", lambda e, si=si, sbk=sbk: e.matmul(
                bank(sbk)[0:64, :], sel4[:, si * 64:(si + 1) * 64],
                rallb[:], start=True, stop=True), S["c_rmm"])
            note_bank_write(sbk, S["c_rmm"], rmm)
            reader_waits_bank("vector", sbk)
            yv = p.op("vector", lambda e, st=st, si=si, sbk=sbk: e.tensor_tensor(
                yT[st["hpt"]][st["prow"]:st["prow"] + 64,
                              st["tq0"]:st["tq0"] + 512],
                yu[si][0:64, :], bank(sbk)[0:64, 0:512], ALU.mult),
                S["c_yev"])
            note_bank_read(sbk, S["c_yev"], yv)
            yev_done[si] = yv

        for j in range(4):
            for tqs in range(4):
                tq0 = tqs * 512
                nk = 4 * tqs + 4
                avv = [[None] * nk for _ in range(4)]
                exv = [[None] * nk for _ in range(4)]

                for tkc in range(nk):
                    off = max(0, 128 * tkc - tq0)
                    npr = 512 - off
                    diag = tkc >= 4 * tqs
                    for si in range(4):
                        h = 4 * j + si
                        hpt, pr = h // 2, (h % 2) * 64
                        sbi = SBK[si]
                        pe_writes_bank(sbi)
                        if diag:
                            p.op("tensor", lambda e, sbi=sbi: e.matmul(
                                bank(sbi)[:, 0:128], masknT[:], ident[:],
                                start=True, stop=False))
                        smm = p.op("tensor", lambda e, tkc=tkc, off=off, npr=npr, sbi=sbi, diag=diag, hpt=hpt, pr=pr, tq0=tq0: e.matmul(
                            bank(sbi)[:, 0:npr],
                            kT[hpt][pr:pr + 64, tkc * 128:(tkc + 1) * 128],
                            qT[hpt][pr:pr + 64, tq0 + off:tq0 + 512],
                            start=not diag, stop=True), S["c_smm"])
                        note_bank_write(sbi, S["c_smm"], smm)
                        reader_waits_bank("scalar", sbi)
                        if tkc >= 1 and avv[si][tkc - 1] is not None:
                            p.wait("scalar", S["c_avmm"], avv[si][tkc - 1])
                        elif tkc == 0 and av_prev[si] is not None:
                            p.wait("scalar", S["c_avmm"], av_prev[si])
                        if npr <= 128:
                            ja()
                        ex = p.op("scalar", lambda e, si=si, npr=npr, sbi=sbi: e.activation(
                            ptile[si][:, 0:npr], bank(sbi)[:, 0:npr], AF.Exp),
                            S["c_expA"])
                        note_bank_read(sbi, S["c_expA"], ex)
                        exv[si][tkc] = ex
                    for si in range(4):
                        h = 4 * j + si
                        if tkc == 0:
                            pe_writes_bank(YBK[si])
                        p.wait("tensor", S["c_expA"], exv[si][tkc])
                        v = p.op("tensor", lambda e, si=si, tkc=tkc, off=off, npr=npr, h=h, ybi=YBK[si], first=(tkc == 0), last=(tkc == nk - 1): e.matmul(
                            bank(ybi)[0:65, off:off + npr],
                            vA[tkc][:, h * 65:(h + 1) * 65],
                            ptile[si][:, 0:npr],
                            start=first, stop=last), S["c_avmm"])
                        avv[si][tkc] = v
                    if tkc == 2 and pending:
                        for st in pending:
                            emit_tail(st)
                        pending = []

                for si in range(4):
                    eng = "scalar" if si % 2 == 0 else "vector"
                    av_prev[si] = avv[si][nk - 1]
                    note_bank_write(YBK[si], S["c_avmm"], avv[si][nk - 1])
                    reader_waits_bank(eng, YBK[si])
                    if yev_done[si]:
                        p.wait(eng, S["c_yev"], yev_done[si])
                    if si % 2 == 0:
                        ev = p.op("scalar", lambda e, si=si, b=YBK[si]: e.activation(
                            yu[si][:, :], bank(b)[0:65, 0:512], AF.Identity),
                            S["c_lcp"])
                    else:
                        ev = p.op("vector", lambda e, si=si, b=YBK[si]: e.tensor_copy(
                            yu[si][:, :], bank(b)[0:65, 0:512]), S["c_lcp"])
                    note_bank_read(YBK[si], S["c_lcp"], ev)
                asp(S["c_lcp"])
                ev_sp = vsp(S["c_lcp"])
                p.wait("gpsimd", S["c_lcp"], ev_sp)
                if rb_prev:
                    p.wait("gpsimd", S["c_rbf"], rb_prev)
                for si in range(4):
                    ldma = p.dma("gpsimd", lall[si:si + 1, :],
                                 yu[si][64:65, :], S["c_rev"])
                p.wait("vector", S["c_rev"], ldma)
                p.op("vector", lambda e: e.reciprocal(rall[:], lall[:]))
                jv()
                p.op("vector", lambda e: e.tensor_copy(rallb[:], rall[:]))
                rb = vsp(S["c_rbf"])
                rb_prev = rb
                pending = []
                for si in range(4):
                    h = 4 * j + si
                    pending.append(dict(si=si, hpt=h // 2, prow=(h % 2) * 64,
                                        tq0=tq0, rb=rb, ev=ev_sp, ld=ldma,
                                        heat=(si == 0)))
                drip(18)
        for st in pending:
            emit_tail(st)
        while prep_queue:
            drip(1)
        vsp(S["c_yev"])
        c_y_done = S["c_yev"].n
        prepst_done = prep_state["st"]

        s_c.close()
        s_qkv.close()
        s_prep.close()


        s_d = ExitStack()
        D_ = s_d.enter_context
        zt = [D_(nc.sbuf_tensor(f"zt{i}", [128, C], F32, side="right"))
              for i in range(2)]
        x1t = [D_(nc.sbuf_tensor(f"x1t{i}", [128, C], F32, side="right"))
               for i in range(4)]
        ln2t = [D_(nc.sbuf_tensor(f"ln2t{i}", [128, C], F32, side="right"))
                for i in range(2)]
        qstage = [D_(nc.sbuf_tensor(f"qstage{i}", [128, C], F32, side="right"))
                  for i in range(2)]
        xqt = [D_(nc.sbuf_tensor(f"xqt{i}", [128, C], BF16, side="right"))
               for i in range(2)]

        s_es = ExitStack()
        ES_ = s_es.enter_context
        xqTg = [[ES_(nc.sbuf_tensor(f"xqTg{j}_{i}", [128, 512], BF16))
                 for i in range(IC)] for j in range(2)]
        wgTs = [[ES_(nc.sbuf_tensor(f"wgTs{j}_{i}", [128, 1024], BF16))
                 for i in range(IC)] for j in range(2)]
        wuTs = [[ES_(nc.sbuf_tensor(f"wuTs{j}_{i}", [128, 1024], BF16))
                 for i in range(IC)] for j in range(2)]
        wdTc = [ES_(nc.sbuf_tensor(f"wdTc{j}", [128, C], BF16))
                for j in range(4)]

        if debug == 1:
            p.wait("gpsimd", S["c_yev"], c_y_done)
            for ic in range(IC):
                p.dma("gpsimd", dbgyT[ic * 128:(ic + 1) * 128, :], yT[ic][:],
                      S["misc2"])
            p.wait("gpsimd", S["misc2"], S["misc2"].n)

        p.wait("tensor", S["c_yev"], c_y_done)
        p.wait("tensor", S["s_wtr"], wtr_p)
        p.wait("gpsimd", S["p1_ln"], S["p1_ln"].n)
        x1st_v = [0] * NT
        xqst_v = [0] * NT
        xq_sems = [S["g_xqst"], S["g_xqst1"], S["g_xqst2"], S["g_xqst3"],
                   S["g_xqst4"], S["g_xqst5"], S["g_xqst6"], S["g_xqst7"]]
        xsems = [S["s_xld"], S["s_xld1"], S["s_xld2"], S["s_xld3"]]

        def emitD_A(tt):
            par = tt % 4
            aob = 4 * (tt % 2)
            pe_writes_bank(aob, 2)
            mm_v = None
            for oc2 in range(2):
                for ic in range(IC):
                    r = p.op("tensor", lambda e, ic=ic, tt=tt, oc2=oc2, aob=aob: e.matmul(
                        bank(aob + oc2), yT[ic][:, tt * 128:(tt + 1) * 128],
                        WTb[ic][:, oc2 * 512:(oc2 + 1) * 512],
                        start=(ic == 0), stop=(ic == IC - 1)),
                        S["d_aomm"] if ic == IC - 1 else None)
                    if r is not None:
                        mm_v = r
            note_bank_write(aob, S["d_aomm"], mm_v, 2)
            xp = tt % 2
            if tt >= 2:
                p.wait("gpsimd", S["d_sg5"], d_x1v[tt - 2])
            xld = p.dma("gpsimd", xt[xp][:], x_h[tt * 128:(tt + 1) * 128, :],
                        xsems[xp])
            ao_view = PS[:, aob * BANK:(aob + 2) * BANK]
            reader_waits_bank("vector", aob, 2)
            rs_v, nm_v = ln_stats(par, ao_view, S["d_sg1"], S["d_sg2"],
                                  S["d_sg3"])
            p.wait("scalar", S["d_sg3"], nm_v)
            if tt >= 2:
                p.wait("scalar", S["d_sg5"], d_x1v[tt - 2])
            z_v = p.op("scalar", lambda e, par=par, xp=xp, ao=ao_view: e.activation(
                zt[xp][:], ao, AF.Identity, scale=rstd[par], bias=nmrc[par][:]),
                S["d_sg4"])
            note_bank_read(aob, S["d_sg4"], z_v, 2)
            p.wait("vector", S["d_sg4"], z_v)
            p.wait("vector", xsems[xp], xld)
            if tt >= 4:
                p.wait("vector", S["g_x1st"], x1st_v[tt - 4])
            x1_v = p.op("vector", lambda e, par=par, xp=xp: e.tensor_add(
                x1t[par][:], zt[xp][:], xt[xp][:]), S["d_sg5"])
            p.wait("gpsimd", S["d_sg5"], x1_v)
            x1st_v[tt] = p.dma("gpsimd", x1d[tt * 128:(tt + 1) * 128, :],
                               x1t[par][:], S["g_x1st"])
            return x1_v

        def emitD_B(tt, x1_v):
            par = tt % 4
            xp = tt % 2
            rs2_v, nm2_v = ln_stats(par, x1t[par][:], S["d_sg1"], S["d_sg2"],
                                    S["d_sg3"],
                                    pre_waits_v=[("vector", S["d_sg5"], x1_v)])
            p.wait("scalar", S["d_sg3"], nm2_v)
            l2_v = p.op("scalar", lambda e, par=par, xp=xp: e.activation(
                ln2t[xp][:], x1t[par][:], AF.Identity, scale=rstd[par],
                bias=nmrc[par][:]), S["d_sg4"])
            p.wait("vector", S["d_sg4"], l2_v)
            p.op("vector", lambda e, par=par, xp=xp: e.tensor_reduce(
                sp[par][:, 0:1], ln2t[xp][:], AX.X, ALU.max,
                apply_absolute_value=True))
            jv()
            p.op("vector", lambda e, par=par: e.tensor_scalar_max(
                sp[par][:, 1:2], sp[par][:, 0:1], 1e-5))
            jv()
            p.op("vector", lambda e, par=par: e.reciprocal(
                sp[par][:, 2:3], sp[par][:, 1:2]))
            p.op("vector", lambda e, par=par, tt=tt: e.tensor_scalar_mul(
                rga_all[:, tt:tt + 1], sp[par][:, 1:2], 1.0 / 127.0))
            p.op("vector", lambda e, par=par: e.tensor_scalar_mul(
                sp[par][:, 3:4], sp[par][:, 2:3], 127.0))
            qs_v = vsp(S["d_sg6"])
            p.wait("scalar", S["d_sg6"], qs_v)
            q1_v = p.op("scalar", lambda e, par=par, xp=xp: e.activation(
                qstage[xp][:], ln2t[xp][:], AF.Identity, scale=sp[par][:, 3:4],
                bias=rcc[:]), S["d_sg4"])
            p.wait("vector", S["d_sg4"], q1_v)
            if tt >= 2:
                p.wait("vector", S["g_xqst"], xqst_v[tt - 2])
            q2_v = p.op("vector", lambda e, xp=xp: e.tensor_scalar(
                xqt[xp][:], qstage[xp][:], -RC, None, ALU.add), S["d_sg6"])
            p.wait("gpsimd", S["d_sg6"], q2_v)
            xqst_v[tt] = p.dma("gpsimd", xqd[tt * 128:(tt + 1) * 128, :],
                               xqt[xp][:], xq_sems[tt % 8])

        d_x1v = [0] * NT
        for tt in range(NT):
            d_x1v[tt] = emitD_A(tt)
            if tt >= 2:
                emitD_B(tt - 2, d_x1v[tt - 2])
        emitD_B(NT - 2, d_x1v[NT - 2])
        emitD_B(NT - 1, d_x1v[NT - 1])
        d_q2_done = S["d_sg6"].n

        p.wait("sync", S["c_yev"], c_y_done)
        p.wait("sync", S["g_prepst"], prepst_done)
        xqtg_v = [0] * NG
        wstr_v = [[0] * 8 for _ in range(NG)]

        def gA_count(g, sl):
            return g * 64 + (sl + 1) * 8

        def gA_count2(g, sp2):
            return g * 64 + (sp2 + 1) * 16

        def emit_stream_slice(g, sp2):
            wssem = S["s_wstr"] if sp2 % 2 == 0 else S["s_wstr1"]
            if sp2 >= 2:
                p.wait("sync", S["e_gmm"], gA_count2(g, sp2 - 2))
            elif g > 0:
                p.wait("sync", S["e_gmm"], gA_count2(g - 1, 2 + sp2))
            for ic in range(IC):
                p.dma("sync", wgTs[sp2 % 2][ic][:],
                      wgq[sp2 * 1024:(sp2 + 1) * 1024, ic * 128:(ic + 1) * 128],
                      wssem, transpose=True)
                wstr_v[g][sp2] = p.dma(
                    "sync", wuTs[sp2 % 2][ic][:],
                    wuq[sp2 * 1024:(sp2 + 1) * 1024, ic * 128:(ic + 1) * 128],
                    wssem, transpose=True)

        xq_sems_all = [S["g_xqst"], S["g_xqst1"], S["g_xqst2"], S["g_xqst3"],
                       S["g_xqst4"], S["g_xqst5"], S["g_xqst6"], S["g_xqst7"]]

        def emit_group_streams_head(g):
            for t4 in range(4):
                tt = 4 * g + t4
                p.wait("sync", xq_sems_all[tt % 8], 16 * (tt // 8 + 1))
            if g >= 2:
                p.wait("sync", S["e_gmm"], gA_count(g - 2, 7))
            xqsem = S["s_xqtg"] if g % 2 == 0 else S["s_xqtg1"]
            for ic in range(IC):
                xqtg_v[g] = p.dma(
                    "sync", xqTg[g % 2][ic][:],
                    xqd[g * 512:(g + 1) * 512, ic * 128:(ic + 1) * 128],
                    xqsem, transpose=True)
            emit_stream_slice(g, 0)

        def emit_group_streams_tail(g):
            for sp2 in range(1, 4):
                emit_stream_slice(g, sp2)

        emit_group_streams_head(0)
        emit_group_streams_tail(0)

        s_d.close()
        s_y.close()
        s_ec = ExitStack()
        EC_ = s_ec.enter_context
        sgb = [EC_(nc.sbuf_tensor(f"sgb{j}", [128, 512], F32)) for j in range(2)]
        hg = [EC_(nc.sbuf_tensor(f"hg{i}", [128, INNER], BF16)) for i in range(GT)]
        hqs = [EC_(nc.sbuf_tensor(f"hqs{j}", [128, 512], F32)) for j in range(2)]
        hqt = EC_(nc.sbuf_tensor("hqt", [128, INNER], BF16))
        hqTgB = EC_(nc.sbuf_tensor("hqTgB", [128, 4 * INNER], BF16))
        ffo = EC_(nc.sbuf_tensor("ffo", [128, C], F32))
        z2b = EC_(nc.sbuf_tensor("z2b", [128, C], F32))
        outb = EC_(nc.sbuf_tensor("outb", [128, C], F32))
        rgag = [EC_(nc.sbuf_tensor(f"rgag{i}", [128, 1], F32)) for i in range(GT)]
        rgau = [EC_(nc.sbuf_tensor(f"rgau{i}", [128, 1], F32)) for i in range(GT)]
        rgad = [EC_(nc.sbuf_tensor(f"rgad{i}", [128, 1], F32)) for i in range(GT)]
        spE = [EC_(nc.sbuf_tensor(f"spE{i}", [128, 6], F32)) for i in range(GT)]

        for g in range(NG):
            p.wait("vector", S["pp_fin"], prep_fin_v[2])
            p.wait("vector", S["d_sg6"], d_q2_done)
            for t4 in range(GT):
                tt = 4 * g + t4
                p.op("vector", lambda e, t4=t4, tt=tt: e.tensor_mul(
                    rgag[t4][:], rga_all[:, tt:tt + 1], rgw_col[0][:]))
                p.op("vector", lambda e, t4=t4, tt=tt: e.tensor_mul(
                    rgau[t4][:], rga_all[:, tt:tt + 1], rgw_col[1][:]))
            rgx_v = vsp(S["e_sgA"])

            h_done_v = [0] * GT
            for sl in range(8):
                sp2, half = sl // 2, sl % 2
                p.wait("tensor", S["s_wstr"] if sp2 % 2 == 0 else S["s_wstr1"],
                       wstr_v[g][sp2])
                if sl == 0:
                    p.wait("tensor",
                           S["s_xqtg"] if g % 2 == 0 else S["s_xqtg1"],
                           xqtg_v[g])
                for t4 in range(GT):
                    gbi = t4 % 2
                    ubi = 2 + t4 % 2
                    pe_writes_bank(gbi)
                    for ic in range(IC):
                        p.op("tensor", lambda e, ic=ic, t4=t4, g=g, gbi=gbi, sp2=sp2, half=half: e.matmul(
                            bank(gbi),
                            xqTg[g % 2][ic][:, t4 * 128:(t4 + 1) * 128],
                            wgTs[sp2 % 2][ic][:, half * 512:(half + 1) * 512],
                            start=(ic == 0), stop=(ic == IC - 1)),
                            S["e_gmm"] if ic == IC - 1 else None)
                    gmm_v = S["e_gmm"].n
                    note_bank_write(gbi, S["e_gmm"], gmm_v)
                    pe_writes_bank(ubi)
                    for ic in range(IC):
                        p.op("tensor", lambda e, ic=ic, t4=t4, g=g, ubi=ubi, sp2=sp2, half=half: e.matmul(
                            bank(ubi),
                            xqTg[g % 2][ic][:, t4 * 128:(t4 + 1) * 128],
                            wuTs[sp2 % 2][ic][:, half * 512:(half + 1) * 512],
                            start=(ic == 0), stop=(ic == IC - 1)),
                            S["e_gmm"] if ic == IC - 1 else None)
                    umm_v = S["e_gmm"].n
                    note_bank_write(ubi, S["e_gmm"], umm_v)
                    spar = t4 % 2
                    reader_waits_bank("scalar", gbi)
                    p.wait("scalar", S["e_sgA"], rgx_v)
                    if sl > 0 or t4 >= 2:
                        p.wait("scalar", S["e_hD"], S["e_hD"].n)
                    sg_v = p.op("scalar", lambda e, spar=spar, gbi=gbi, t4=t4: e.activation(
                        sgb[spar][:], bank(gbi), AF.Silu, scale=rgag[t4][:]),
                        S["e_sgA"])
                    note_bank_read(gbi, S["e_sgA"], sg_v)
                    reader_waits_bank("vector", ubi)
                    p.wait("vector", S["e_sgA"], sg_v)
                    h_v = p.op("vector", lambda e, spar=spar, ubi=ubi, t4=t4, sl=sl: e.scalar_tensor_tensor(
                        hg[t4][:, sl * 512:(sl + 1) * 512], bank(ubi),
                        rgau[t4][:], sgb[spar][:], ALU.mult, ALU.mult),
                        S["e_hD"])
                    note_bank_read(ubi, S["e_hD"], h_v)
                    h_done_v[t4] = h_v

            if g + 1 < NG:
                emit_group_streams_head(g + 1)

            if debug == 1:
                for t4 in range(GT):
                    tt = 4 * g + t4
                    p.wait("gpsimd", S["e_hD"], h_done_v[t4])
                    p.dma("gpsimd", dbgh[tt * 128:(tt + 1) * 128, :],
                          hg[t4][:], S["misc2"])

            quad_i = 0
            pending_tr = None
            tr_ev = [0] * 8
            tr_tile_done = 0
            for t4 in range(GT):
                p.wait("vector", S["e_hD"], h_done_v[t4])
                p.op("vector", lambda e, t4=t4, s=spE[t4]: e.tensor_reduce(
                    s[:, 0:1], hg[t4][:], AX.X, ALU.max,
                    apply_absolute_value=True))
            for t4 in range(GT):
                p.op("vector", lambda e, s=spE[t4]: e.tensor_scalar_max(
                    s[:, 1:2], s[:, 0:1], 1e-5))
            for t4 in range(GT):
                p.op("vector", lambda e, s=spE[t4]: e.reciprocal(
                    s[:, 2:3], s[:, 1:2]))
            for t4 in range(GT):
                p.op("vector", lambda e, s=spE[t4], t4=t4: e.scalar_tensor_tensor(
                    rgad[t4][:], s[:, 1:2], 1.0 / 127.0, rgw_col[2][:],
                    ALU.mult, ALU.mult))
            for t4 in range(GT):
                p.op("vector", lambda e, s=spE[t4]: e.tensor_scalar_mul(
                    s[:, 3:4], s[:, 2:3], 127.0))
            amx_v = vsp(S["e_amx"])
            p.wait("scalar", S["e_amx"], amx_v)

            def emit_tr_evac(pt, gate):
                t4q, chkq, b = pt
                eng = "scalar" if b % 2 == 0 else "vector"
                sem = S["e_trev"] if b % 2 == 0 else S["e_trev2"]
                if gate is not None:
                    p.wait(eng, S["e_trm"], gate)
                else:
                    p.wait(eng, S["e_trm"], S["e_trm"].n)
                    if eng == "scalar":
                        ja(); ja()
                    else:
                        jv(); jv()
                ev = p.op(eng, lambda e, t4q=t4q, chkq=chkq, b=b: e.tensor_copy(
                    hqTgB[:, t4q * 4096 + chkq * 512:t4q * 4096 + chkq * 512 + 512],
                    bank(b)) if b % 2
                    else e.activation(
                    hqTgB[:, t4q * 4096 + chkq * 512:t4q * 4096 + chkq * 512 + 512],
                    bank(b), AF.Identity), sem)
                tr_ev[b] = ev
                note_bank_read(b, sem, ev)

            for t4 in range(GT):
                s = spE[t4]
                if tr_tile_done:
                    p.wait("vector", S["e_trm"], tr_tile_done)
                for chk in range(8):
                    qpar = chk % 2
                    if chk >= 2:
                        p.wait("scalar", S["e_qD"], S["e_qD"].n)
                    qa = p.op("scalar", lambda e, t4=t4, chk=chk, qpar=qpar, s=s: e.activation(
                        hqs[qpar][:], hg[t4][:, chk * 512:(chk + 1) * 512],
                        AF.Identity, scale=s[:, 3:4], bias=rcc[:]), S["e_qA"])
                    p.wait("vector", S["e_qA"], qa)
                    p.op("vector", lambda e, chk=chk, qpar=qpar: e.tensor_scalar(
                        hqt[:, chk * 512:(chk + 1) * 512], hqs[qpar][:],
                        -RC, None, ALU.add), S["e_qD"])
                    chk_v = vsp(S["e_qD"])
                    p.wait("tensor", S["e_qD"], chk_v)
                    b = quad_i % 8
                    if quad_i < 8:
                        pe_writes_bank(b)
                    if tr_ev[b]:
                        p.wait("tensor",
                               S["e_trev"] if b % 2 == 0 else S["e_trev2"],
                               tr_ev[b])
                    tm2 = None
                    for bq in range(4):
                        icn = chk * 4 + bq
                        tm = p.op("tensor", lambda e, icn=icn, b=b, bq=bq: e.matmul(
                            bank(b)[:, bq * 128:(bq + 1) * 128],
                            hqt[:, icn * 128:(icn + 1) * 128], ident[:],
                            start=True, stop=True), S["e_trm"])
                        if bq == 1:
                            tm2 = tm
                    if pending_tr is not None:
                        emit_tr_evac(pending_tr, tm2)
                    pending_tr = (t4, chk, b)
                    quad_i += 1
                    if not (t4 == GT - 1 and chk == 7):
                        bn = quad_i % 8
                        if tr_ev[bn]:
                            p.wait("tensor",
                                   S["e_trev"] if bn % 2 == 0 else S["e_trev2"],
                                   tr_ev[bn])
                        p.wait("tensor", S["e_qA"], S["e_qA"].n + 1)
                        p.op("tensor", lambda e, bn=bn: e.matmul(
                            bank(bn)[0:1, 0:1], ones_row_b[:, 0:1],
                            ones_row_b[:, 0:1], start=True, stop=True))
                tr_tile_done = S["e_trm"].n
                hq_v = S["e_qD"].n
                if debug == 1:
                    tt = 4 * g + t4
                    p.wait("gpsimd", S["e_qD"], hq_v)
                    dmp = p.dma("gpsimd", dbghq[tt * 128:(tt + 1) * 128, :],
                                hqt[:], S["misc2"])
                    p.wait("vector", S["misc2"], dmp)
            emit_tr_evac(pending_tr, None)
            pending_tr = None
            trA_fin = asp(S["e_trev"])
            trD_fin = vsp(S["e_trev2"])

            wdc_sems = [S["s_wdc"], S["s_wdc1"], S["s_wdc2"], S["s_wdc3"]]
            dn_mm_v = 0
            p.wait("tensor", S["e_trev"], trA_fin)
            p.wait("tensor", S["e_trev2"], trD_fin)
            for icn in range(32):
                dpar = icn % 4
                if icn >= 4:
                    p.wait("scalar", S["e_dmm"], S["e_dmm"].n - 24)
                elif g > 0:
                    p.wait("scalar", S["e_dmm"], g * 256)
                wdv = p.dma("scalar", wdTc[dpar][:],
                            wdq[0:C, icn * 128:(icn + 1) * 128],
                            wdc_sems[dpar], transpose=True)
                p.wait("tensor", wdc_sems[dpar], wdv)
                if icn == 0:
                    for b in range(8):
                        pe_writes_bank(b)
                for t4 in range(GT):
                    for oc2 in range(2):
                        dn_mm_v = p.op("tensor", lambda e, icn=icn, t4=t4, oc2=oc2, dpar=dpar: e.matmul(
                            bank(t4 * 2 + oc2),
                            hqTgB[:, t4 * 4096 + icn * 128:t4 * 4096 + icn * 128 + 128],
                            wdTc[dpar][:, oc2 * 512:(oc2 + 1) * 512],
                            start=(icn == 0), stop=(icn == 31)), S["e_dmm"])
            for b in range(8):
                note_bank_write(b, S["e_dmm"], dn_mm_v)

            if g + 1 < NG:
                emit_group_streams_tail(g + 1)

            for t4 in range(GT):
                tt = 4 * g + t4
                par = t4 % 2
                p.wait("scalar", S["e_dmm"], dn_mm_v)
                if g > 0 or t4 > 0:
                    p.wait("scalar", S["e_fin"], S["e_fin"].n)
                    p.wait("scalar", S["d_sg1"], S["d_sg1"].n)
                ev = None
                for oc2 in range(2):
                    reader_waits_bank("scalar", t4 * 2 + oc2)
                    ev = p.op("scalar", lambda e, t4=t4, oc2=oc2: e.activation(
                        ffo[:, oc2 * 512:(oc2 + 1) * 512],
                        bank(t4 * 2 + oc2), AF.Identity, scale=rgad[t4][:]),
                        S["e_deA"])
                    note_bank_read(t4 * 2 + oc2, S["e_deA"], ev)
                if debug == 1:
                    tt2 = 4 * g + t4
                    p.wait("gpsimd", S["e_deA"], ev)
                    dmp2 = p.dma("gpsimd", dbgffo[tt2 * 128:(tt2 + 1) * 128, :],
                                 ffo[:], S["misc2"])
                    p.wait("scalar", S["misc2"], dmp2)
                    p.wait("vector", S["misc2"], dmp2)
                rs_v, nm_v = ln_stats(par, ffo[:], S["d_sg1"], S["d_sg2"],
                                      S["d_sg3"],
                                      pre_waits_v=[("vector", S["e_deA"], ev)])
                p.wait("scalar", S["d_sg3"], nm_v)
                z2_v = p.op("scalar", lambda e, par=par: e.activation(
                    z2b[:], ffo[:], AF.Identity, scale=rstd[par],
                    bias=nmrc[par][:]), S["d_sg4"])
                p.wait("gpsimd", S["g_x1st"], x1st_v[NT - 1])
                p.wait("gpsimd", S["e_fin"], S["e_fin"].n)
                xsem = S["s_xld"] if par == 0 else S["s_xld1"]
                x1ld = p.dma("gpsimd", xt[par][:],
                             x1d[tt * 128:(tt + 1) * 128, :], xsem)
                p.wait("vector", S["d_sg4"], z2_v)
                p.wait("vector", xsem, x1ld)
                if g > 0 or t4 > 0:
                    p.wait("vector", S["g_outst"], S["g_outst"].n)
                o_v = p.op("vector", lambda e, par=par: e.tensor_add(
                    outb[:], z2b[:], xt[par][:]), S["e_fin"])
                p.wait("gpsimd", S["e_fin"], o_v)
                p.dma("gpsimd", out_h[tt * 128:(tt + 1) * 128, :], outb[:],
                      S["g_outst"])

        p.wait("gpsimd", S["g_outst"], 16 * NT)
        s_ec.close()
        s_es.close()

        with nc.Block() as block:
            @block.sync
            def _(eng):
                for f in p.ops["sync"]:
                    f(eng)

            @block.gpsimd
            def _(eng):
                for f in p.ops["gpsimd"]:
                    f(eng)

            @block.scalar
            def _(eng):
                for f in p.ops["scalar"]:
                    f(eng)

            @block.vector
            def _(eng):
                for f in p.ops["vector"]:
                    f(eng)

            @block.tensor
            def _(eng):
                for f in p.ops["tensor"]:
                    f(eng)

    return nc


_NC_CACHE = None


def kernel(L0_attn_x, L0_attn_Wq, L0_attn_Wk, L0_attn_Wv, L0_attn_Wp,
           L0_ffn_w_gate, L0_ffn_w_up, L0_ffn_w_down):
    global _NC_CACHE
    if _NC_CACHE is None:
        _NC_CACHE = build_nc()
    nc = _NC_CACHE
    x = np.ascontiguousarray(np.asarray(L0_attn_x, dtype=np.float32))
    ws = {
        "wq": np.ascontiguousarray(np.asarray(L0_attn_Wq, dtype=np.float32)),
        "wk": np.ascontiguousarray(np.asarray(L0_attn_Wk, dtype=np.float32)),
        "wv": np.ascontiguousarray(np.asarray(L0_attn_Wv, dtype=np.float32)),
        "wp": np.ascontiguousarray(np.asarray(L0_attn_Wp, dtype=np.float32)),
        "wg": np.ascontiguousarray(np.asarray(L0_ffn_w_gate, dtype=np.float32)),
        "wu": np.ascontiguousarray(np.asarray(L0_ffn_w_up, dtype=np.float32)),
        "wd": np.ascontiguousarray(np.asarray(L0_ffn_w_down, dtype=np.float32)),
    }
    in_maps = [dict(ws, x=x[i]) for i in range(8)]
    res = run_bass_kernel_spmd(nc, in_maps, core_ids=list(range(8)))
    return np.stack([res.results[i]["out"] for i in range(8)], axis=0)


# version 18 (final)
# speedup vs baseline: 1.0112x; 1.0112x over previous
import numpy as np
from collections import deque
from contextlib import ExitStack
import concourse.bass as bass
import concourse.mybir as mybir
from concourse.bass_utils import run_bass_kernel_spmd

F32 = mybir.dt.float32
BF16 = mybir.dt.bfloat16
AF = mybir.ActivationFunctionType
ALU = mybir.AluOpType
AX = mybir.AxisListType

T = 2048
C = 1024
H = 16
DH = 64
INNER = 4096
NT = T // 128
IC = C // 128
EPS = 1e-5
ASCALE = 0.125
RC = 12582912.0
NG = 4
GT = NT // NG
BANK = 512


class Sem:
    __slots__ = ("h", "n", "name")

    def __init__(self, h, name):
        self.h = h
        self.n = 0
        self.name = name

    def bump(self, k):
        self.n += k
        return self.n


class Plan:
    def __init__(self):
        self.ops = {"sync": [], "gpsimd": [], "scalar": [], "vector": [],
                    "tensor": []}

    def emit(self, eng, fn):
        self.ops[eng].append(fn)

    def wait(self, eng, sem, val=None):
        v = sem.n if val is None else val
        if v is None or v <= 0:
            return
        self.emit(eng, lambda e, s=sem.h, v=v: e.wait_ge(s, v))

    def dma(self, eng, out, in_, sem, **kw):
        def fn(e, o=out, i=in_, s=sem.h, kw=dict(kw)):
            e.dma_start(o, i, **kw).then_inc(s, 16)
        self.emit(eng, fn)
        return sem.bump(16)

    def op(self, eng, fn, sem=None, k=1):
        if sem is None:
            self.emit(eng, fn)
            return None

        def wrapped(e, fn=fn, s=sem.h, k=k):
            fn(e).then_inc(s, k)
        self.emit(eng, wrapped)
        return sem.bump(k)


def build_nc(debug=False):
    nc = bass.Bass("TRN2", target_bir_lowering=False)

    x_h = nc.dram_tensor("x", [T, C], F32, kind="ExternalInput").ap()
    wq_h = nc.dram_tensor("wq", [C, C], F32, kind="ExternalInput").ap()
    wk_h = nc.dram_tensor("wk", [C, C], F32, kind="ExternalInput").ap()
    wv_h = nc.dram_tensor("wv", [C, C], F32, kind="ExternalInput").ap()
    wp_h = nc.dram_tensor("wp", [C, C], F32, kind="ExternalInput").ap()
    wg_h = nc.dram_tensor("wg", [INNER, C], F32, kind="ExternalInput").ap()
    wu_h = nc.dram_tensor("wu", [INNER, C], F32, kind="ExternalInput").ap()
    wd_h = nc.dram_tensor("wd", [C, INNER], F32, kind="ExternalInput").ap()
    out_h = nc.dram_tensor("out", [T, C], F32, kind="ExternalOutput").ap()

    kd = "ExternalOutput" if debug else "Internal"
    kd2 = "ExternalOutput" if debug == 2 else kd
    wqb = nc.dram_tensor("wqb", [C, C], BF16, kind=kd).ap()
    wkb = nc.dram_tensor("wkb", [C, C], BF16, kind=kd).ap()
    wvb = nc.dram_tensor("wvb", [C, C], BF16, kind=kd).ap()
    wpb = nc.dram_tensor("wpb", [C, C], BF16, kind=kd).ap()
    x1d = nc.dram_tensor("x1d", [T, C], F32, kind=kd2).ap()
    xqd = nc.dram_tensor("xqd", [T, C], BF16, kind=kd2).ap()
    wgq = nc.dram_tensor("wgq", [INNER, C], BF16, kind=kd2).ap()
    wuq = nc.dram_tensor("wuq", [INNER, C], BF16, kind=kd2).ap()
    wdq = nc.dram_tensor("wdq", [C, INNER], BF16, kind=kd2).ap()
    if debug:
        dbgh = nc.dram_tensor("dbgh", [T, INNER], F32, kind=kd).ap()
        dbghq = nc.dram_tensor("dbghq", [T, INNER], BF16, kind=kd).ap()
        dbgffo = nc.dram_tensor("dbgffo", [T, C], F32, kind=kd).ap()
        dbgvA2 = nc.dram_tensor("dbgvA2", [T, H * 65], BF16, kind=kd).ap()
        dbgWv = nc.dram_tensor("dbgWv", [C, C], BF16, kind=kd).ap()
        dbgsc = nc.dram_tensor("dbgsc", [128, 8], F32, kind=kd).ap()
        dbgacc = nc.dram_tensor("dbgacc", [128, 192], F32, kind=kd).ap()
        dbgAT = nc.dram_tensor("dbgAT", [C, T], BF16, kind=kd).ap()
        dbgqT = nc.dram_tensor("dbgqT", [C, T], BF16, kind=kd).ap()
        dbgkT = nc.dram_tensor("dbgkT", [C, T], BF16, kind=kd).ap()
        dbgyT = nc.dram_tensor("dbgyT", [C, T], BF16, kind=kd).ap()
        dbgvA = nc.dram_tensor("dbgvA", [T, H * 65], BF16, kind=kd).ap()

    p = Plan()
    ctx = ExitStack()
    with ctx:
        EG = ctx.enter_context

        S = {}
        for name in [
            "g_cast", "g_cast1", "g_cast2", "g_cast3", "s_xld", "s_xld1",
            "s_xld2", "s_xld3",
            "g_x1st", "g_xqst", "g_xqst1", "g_xqst2", "g_xqst3", "g_xqst4",
            "g_xqst5", "g_xqst6", "g_xqst7", "wb0", "wb1", "wb2", "wb3",
            "s_wstr1", "s_xqtg1", "s_wdc1", "s_wdc2", "s_wdc3",
            "g_prepA", "g_prepB",
            "g_prepst", "g_outst", "s_wtr", "s_xqtg", "s_wstr", "s_wdc",
            "p1_st", "p1_rs", "p1_nm", "p1_ln", "p1_tr", "at_evD", "at_evA",
            "prj_mm", "prj_evA", "prj_evD", "c_smm", "c_expA", "c_msk",
            "c_avmm", "c_lcp", "c_rbf", "c_rmm", "c_rev", "c_yev",
            "d_aomm", "d_sg1", "d_sg2", "d_sg3", "d_sg4", "d_sg5", "d_sg6",
            "pp_abs", "pp_eop", "pp_fin", "pp_q1", "pp_q2", "pp_q3", "e_gmm", "e_sgA",
            "e_hD", "e_amx", "e_qA", "e_qD", "e_trm", "e_trev", "e_trev2",
            "e_dmm",
            "g_hqst", "s_hqt",
            "e_deA", "e_fin", "misc", "misc2",
        ]:
            S[name] = Sem(EG(nc.semaphore(name)), name)

        PS = EG(nc.psum_tensor("PS", [128, 8 * BANK], F32, unsafe=True))

        def bank(i, n=1):
            return PS[:, i * BANK:(i + n) * BANK]

        bank_w = [(None, 0)] * 8
        bank_r = [(None, 0)] * 8

        def pe_writes_bank(bi, n=1):
            for b in range(bi, bi + n):
                s, v = bank_r[b]
                if s is not None:
                    p.wait("tensor", s, v)

        def note_bank_write(bi, sem, val, n=1):
            for b in range(bi, bi + n):
                bank_w[b] = (sem, val)

        def reader_waits_bank(eng, bi, n=1):
            for b in range(bi, bi + n):
                s, v = bank_w[b]
                if s is not None:
                    p.wait(eng, s, v)

        def note_bank_read(bi, sem, val, n=1):
            for b in range(bi, bi + n):
                bank_r[b] = (sem, val)

        ident = EG(nc.sbuf_tensor("ident", [128, 128], BF16))
        ones_row_b = EG(nc.sbuf_tensor("ones_row_b", [1, 128], BF16))
        ones_row_f = EG(nc.sbuf_tensor("ones_row_f", [1, 128], F32))
        ones_col_f = EG(nc.sbuf_tensor("ones_col_f", [128, 1], F32))
        epsc = EG(nc.sbuf_tensor("epsc", [128, 1], F32))
        rcc = EG(nc.sbuf_tensor("rcc", [128, 1], F32))

        xt = [EG(nc.sbuf_tensor(f"xt{i}", [128, C], F32)) for i in range(2)]
        bn6 = [EG(nc.sbuf_tensor(f"bn6_{i}", [128, 12], F32)) for i in range(4)]
        sp = [EG(nc.sbuf_tensor(f"sp{i}", [128, 12], F32)) for i in range(4)]
        nmrc = [EG(nc.sbuf_tensor(f"nmrc{i}", [128, 1], F32)) for i in range(4)]
        rga_all = EG(nc.sbuf_tensor("rga_all", [128, NT], F32))
        scol_w = [EG(nc.sbuf_tensor(f"scolw{i}", [128, 1], F32)) for i in range(3)]
        rgw_col = [EG(nc.sbuf_tensor(f"rgwcol{i}", [128, 1], F32)) for i in range(3)]
        wstat = EG(nc.sbuf_tensor("wstat", [1, 4], F32))

        junkV = EG(nc.sbuf_tensor("junkV", [128, 1], F32))
        junkG = EG(nc.sbuf_tensor("junkG", [128, 1], BF16))
        junkA = EG(nc.sbuf_tensor("junkA", [128, 1], F32))
        fincol = EG(nc.sbuf_tensor("fincol", [128, 1], F32))
        maskb = EG(nc.sbuf_tensor("maskb", [128, 128], BF16))
        masknT = EG(nc.sbuf_tensor("masknT", [128, 128], BF16))
        sel4 = EG(nc.sbuf_tensor("sel4", [4, 256], BF16))
        sel4a = EG(nc.sbuf_tensor("sel4a", [4, 256], F32))
        sel4b = EG(nc.sbuf_tensor("sel4b", [4, 256], F32))

        rstd = [sp[i][:, 11:12] for i in range(4)]

        def jv():
            p.op("vector", lambda e: e.tensor_copy(junkV[:], epsc[:]))

        def ja():
            p.op("scalar", lambda e: e.activation(junkA[:], epsc[:],
                                                  AF.Identity))

        def vsp(sem):
            return p.op("vector", lambda e: e.tensor_copy(junkV[:], epsc[:]), sem)

        def asp(sem):
            return p.op("scalar", lambda e: e.activation(
                junkA[:], epsc[:], AF.Identity), sem)

        def init_consts(e):
            e.memset(ident[:], 1.0)
            e.affine_select(ident[:], ident[:], [[1, 128]], ALU.is_equal, 0.0,
                            channel_multiplier=-1)
            e.memset(ones_row_b[:], 1.0)
            e.memset(ones_row_f[:], 1.0)
            e.memset(epsc[:], EPS)
            e.memset(rcc[:], RC)

            e.memset(maskb[:], 1.0)
            e.affine_select(maskb[:], maskb[:], [[1, 128]], ALU.is_ge, 0.0,
                            channel_multiplier=-1)
            e.memset(masknT[:], -30000.0)
            e.affine_select(masknT[:], masknT[:], [[1, 128]], ALU.is_gt, 0.0,
                            channel_multiplier=-1)
            e.memset(sel4a[:], 1.0)
            e.affine_select(sel4a[:], sel4a[:], [[1, 256]], ALU.is_ge, 0.0,
                            channel_multiplier=-64)
            e.memset(sel4b[:], 1.0)
            e.affine_select(sel4b[:], sel4b[:], [[1, 256]], ALU.is_ge, 0.0,
                            base=-64, channel_multiplier=-64)
            return e.memset(ones_col_f[:], 1.0)
        p.op("gpsimd", init_consts, S["misc"])
        consts_done = S["misc"].n
        for _eng in ["sync", "scalar", "vector", "tensor"]:
            p.wait(_eng, S["misc"], consts_done)
        p.op("vector", lambda e: e.tensor_sub(sel4[:], sel4a[:], sel4b[:]))
        sel4_done = vsp(S["misc2"])
        p.wait("tensor", S["misc2"], sel4_done)

        cast_sems = [S["g_cast"], S["g_cast1"], S["g_cast2"], S["g_cast3"]]
        for ci, (csrc, cdst) in enumerate([(wq_h, wqb), (wk_h, wkb),
                                           (wv_h, wvb), (wp_h, wpb)]):
            p.dma("gpsimd", cdst, csrc, cast_sems[ci])

        def ln_stats(par, src_ap, sg_stats, sg_rstd, sg_nm, pre_waits_v=()):
            s = sp[par]
            b = bn6[par]
            m = [b[:, 1:2], b[:, 4:5], b[:, 7:8], b[:, 10:11]]
            M = [b[:, 2:3], b[:, 5:6], b[:, 8:9], b[:, 11:12]]
            for eng, sem, v in pre_waits_v:
                p.wait(eng, sem, v)
            p.op("vector", lambda e, b=b, src=src_ap: e.bn_stats(
                b[:, 0:6], src[:, 0:512]))
            p.op("vector", lambda e, b=b, src=src_ap: e.bn_stats(
                b[:, 6:12], src[:, 512:1024]))
            p.op("vector", lambda e, s=s, m=m: e.tensor_add(s[:, 0:1], m[0], m[1]))
            p.op("vector", lambda e, s=s, m=m: e.tensor_add(s[:, 1:2], m[2], m[3]))
            p.op("vector", lambda e, s=s, M=M: e.tensor_add(s[:, 4:5], M[0], M[1]))
            p.op("vector", lambda e, s=s, M=M: e.tensor_add(s[:, 5:6], M[2], M[3]))
            p.op("vector", lambda e, s=s, m=m: e.tensor_mul(s[:, 7:8], m[0], m[0]))
            p.op("vector", lambda e, s=s, m=m: e.tensor_mul(s[:, 8:9], m[1], m[1]))
            p.op("vector", lambda e, s=s: e.tensor_add(s[:, 2:3], s[:, 0:1], s[:, 1:2]))
            p.op("vector", lambda e, s=s: e.tensor_add(s[:, 6:7], s[:, 4:5], s[:, 5:6]))
            p.op("vector", lambda e, s=s: e.tensor_add(s[:, 7:8], s[:, 7:8], s[:, 8:9]))
            p.op("vector", lambda e, s=s, m=m: e.tensor_mul(s[:, 8:9], m[2], m[2]))
            p.op("vector", lambda e, s=s: e.tensor_scalar_mul(s[:, 3:4], s[:, 2:3], 0.25))
            p.op("vector", lambda e, s=s: e.tensor_add(s[:, 7:8], s[:, 7:8], s[:, 8:9]))
            p.op("vector", lambda e, s=s, m=m: e.tensor_mul(s[:, 8:9], m[3], m[3]))
            p.op("vector", lambda e, s=s: e.tensor_mul(s[:, 1:2], s[:, 3:4], s[:, 3:4]))
            p.op("vector", lambda e, s=s: e.tensor_add(s[:, 7:8], s[:, 7:8], s[:, 8:9]))
            jv()
            p.op("vector", lambda e, s=s: e.tensor_scalar_mul(s[:, 0:1], s[:, 7:8], 0.25))
            jv()
            p.op("vector", lambda e, s=s: e.scalar_tensor_tensor(
                s[:, 9:10], s[:, 6:7], 1.0 / C, s[:, 0:1], ALU.mult, ALU.add))
            jv()
            p.op("vector", lambda e, s=s: e.tensor_sub(
                s[:, 10:11], s[:, 9:10], s[:, 1:2]))
            st_v = vsp(sg_stats)
            p.wait("scalar", sg_stats, st_v)
            p.op("scalar", lambda e, s=s: e.activation(s[:, 9:10], s[:, 10:11],
                                                       AF.Ln, bias=epsc[:]))
            ja()
            p.op("scalar", lambda e, s=s, par=par: e.activation(
                rstd[par], s[:, 9:10], AF.Exp, scale=-0.5))
            rs_v = asp(sg_rstd)
            p.wait("vector", sg_rstd, rs_v)
            p.op("vector", lambda e, s=s, par=par: e.tensor_scalar(
                nmrc[par][:], s[:, 3:4], rstd[par], -1.0, ALU.mult, ALU.mult))
            nm_v = vsp(sg_nm)
            return rs_v, nm_v

        prep_srcs = []
        for mi, wh in [(0, wg_h), (1, wu_h)]:
            for ch in range(32):
                for hf in range(2):
                    prep_srcs.append(
                        (mi, wh[ch * 128:(ch + 1) * 128, hf * 512:(hf + 1) * 512],
                         None))
        for rr in range(8):
            for cpc in range(8):
                prep_srcs.append(
                    (2, wd_h[rr * 128:(rr + 1) * 128, cpc * 512:(cpc + 1) * 512],
                     (rr, cpc)))

        prep_abs_v = [0] * 192
        prep_q3_v = [0] * 192
        prep_fin_v = [0, 0, 0]
        prep_state = {"st": 0}

        s_prep = ExitStack()
        PE_ = s_prep.enter_context
        wbuf = [PE_(nc.sbuf_tensor(f"wbuf{i}", [128, 512], F32)) for i in range(4)]
        pstage = [PE_(nc.sbuf_tensor(f"pstage{i}", [128, 512], F32)) for i in range(2)]
        pout = [PE_(nc.sbuf_tensor(f"pout{i}", [128, 512], BF16)) for i in range(2)]
        acc_strip = PE_(nc.sbuf_tensor("acc_strip", [128, 192], F32))

        def prep1_thunk(idx):
            def emit():
                par = idx % 4
                wsems = [S["wb0"], S["wb1"], S["wb2"], S["wb3"]]
                if idx == 0:
                    for j in range(4):
                        p.dma("gpsimd", wbuf[j][:], prep_srcs[j][1], wsems[j])
                elif idx + 3 < 192:
                    p.wait("gpsimd", S["pp_abs"], idx)
                    p.dma("gpsimd", wbuf[(idx + 3) % 4][:],
                          prep_srcs[idx + 3][1], wsems[(idx + 3) % 4])
                p.wait("vector", wsems[par], 16 * (idx // 4 + 1))
                prep_abs_v[idx] = p.op("vector", lambda e, par=par, idx=idx: e.tensor_reduce(
                    acc_strip[:, idx:idx + 1], wbuf[par][:], AX.X, ALU.add,
                    apply_absolute_value=True), S["pp_abs"])
            return emit

        def prep2_dst(idx):
            mi, _, wdpos = prep_srcs[idx]
            if mi == 0:
                ch, hf = (idx % 64) // 2, idx % 2
                return wgq[ch * 128:(ch + 1) * 128, hf * 512:(hf + 1) * 512]
            if mi == 1:
                j = idx - 64
                ch, hf = j // 2, j % 2
                return wuq[ch * 128:(ch + 1) * 128, hf * 512:(hf + 1) * 512]
            rr, cpc = wdpos
            return wdq[rr * 128:(rr + 1) * 128, cpc * 512:(cpc + 1) * 512]

        def prep2_thunk(idx):
            def emit():
                mi = prep_srcs[idx][0]
                par = idx % 4
                spar = idx % 2
                wsems = [S["wb0"], S["wb1"], S["wb2"], S["wb3"]]
                if idx == 0:
                    p.wait("gpsimd", S["pp_abs"], 192)
                    for j in range(4):
                        p.dma("gpsimd", wbuf[j][:], prep_srcs[j][1], wsems[j])
                elif idx + 3 < 192:
                    p.wait("gpsimd", S["pp_q1"], idx)
                    p.dma("gpsimd", wbuf[(idx + 3) % 4][:],
                          prep_srcs[idx + 3][1], wsems[(idx + 3) % 4])
                p.wait("scalar", wsems[par], 16 * (48 + idx // 4 + 1))
                p.wait("scalar", S["pp_fin"], prep_fin_v[mi])
                if idx >= 2:
                    p.wait("scalar", S["pp_q3"], idx - 1)
                q1 = p.op("scalar", lambda e, par=par, mi=mi, spar=spar: e.activation(
                    pstage[spar][:], wbuf[par][:], AF.Identity,
                    scale=scol_w[mi][:], bias=rcc[:]), S["pp_q1"])
                p.wait("vector", S["pp_q1"], q1)
                p.op("vector", lambda e, spar=spar: e.tensor_scalar(
                    pstage[spar][:], pstage[spar][:], RC + 1.0, RC - 1.0,
                    ALU.min, ALU.max), S["pp_q2"])
                if idx >= 2:
                    p.wait("vector", S["g_prepst"], 16 * (idx - 1))
                q3 = p.op("vector", lambda e, spar=spar: e.tensor_scalar(
                    pout[spar][:], pstage[spar][:], -RC, None, ALU.add),
                    S["pp_q3"])
                prep_q3_v[idx] = q3
                p.wait("gpsimd", S["pp_q3"], q3)
                prep_state["st"] = p.dma("gpsimd", prep2_dst(idx),
                                         pout[spar][:], S["g_prepst"])
            return emit

        def prep_fin_thunk(mi):
            def emit():
                lo, hi = mi * 64, (mi + 1) * 64
                bi = 6
                p.wait("vector", S["pp_abs"], prep_abs_v[hi - 1])
                p.op("vector", lambda e: e.reduce_sum(
                    fincol[:], acc_strip[:, lo:hi], AX.X))
                jv()
                cs_v = vsp(S["pp_fin"])
                p.wait("tensor", S["pp_fin"], cs_v)
                pe_writes_bank(bi)
                mm_v = p.op("tensor", lambda e, bi=bi: e.matmul(
                    bank(bi)[0:1, 0:1], fincol[:], ones_col_f[:],
                    start=True, stop=True), S["pp_fin"])
                note_bank_write(bi, S["pp_fin"], mm_v)
                reader_waits_bank("vector", bi)
                jv()
                jv()
                p.op("vector", lambda e, bi=bi: e.tensor_copy(wstat[:, 0:1],
                                                              bank(bi)[0:1, 0:1]))
                jv()
                p.op("vector", lambda e: e.tensor_scalar(
                    wstat[:, 1:2], wstat[:, 0:1], 1.0 / (INNER * C), 1e-5,
                    ALU.mult, ALU.max))
                jv()
                p.op("vector", lambda e: e.reciprocal(
                    wstat[:, 2:3], wstat[:, 1:2]))
                rc_v = vsp(S["pp_fin"])
                note_bank_read(bi, S["pp_fin"], rc_v)
                p.wait("tensor", S["pp_fin"], rc_v)
                pe_writes_bank(bi)
                p.op("tensor", lambda e, bi=bi: e.matmul(
                    bank(bi)[:, 0:1], ones_row_f[:], wstat[:, 2:3],
                    start=True, stop=True), S["pp_fin"])
                mm3 = p.op("tensor", lambda e, bi=bi: e.matmul(
                    bank(bi)[:, 1:2], ones_row_f[:], wstat[:, 1:2],
                    start=True, stop=True), S["pp_fin"])
                note_bank_write(bi, S["pp_fin"], mm3)
                reader_waits_bank("vector", bi)
                jv()
                jv()
                p.op("vector", lambda e, mi=mi, bi=bi: e.tensor_copy(
                    scol_w[mi][:], bank(bi)[:, 0:1]))
                p.op("vector", lambda e, mi=mi, bi=bi: e.tensor_copy(
                    rgw_col[mi][:], bank(bi)[:, 1:2]))
                ev_v = vsp(S["pp_fin"])
                note_bank_read(bi, S["pp_fin"], ev_v)
                prep_fin_v[mi] = S["pp_fin"].n
            return emit

        def prep2_dst(idx):
            mi, _, wdpos = prep_srcs[idx]
            if mi == 0:
                ch, hf = (idx % 64) // 2, idx % 2
                return wgq[ch * 128:(ch + 1) * 128, hf * 512:(hf + 1) * 512]
            if mi == 1:
                j = idx - 64
                ch, hf = j // 2, j % 2
                return wuq[ch * 128:(ch + 1) * 128, hf * 512:(hf + 1) * 512]
            rr, cpc = wdpos
            return wdq[rr * 128:(rr + 1) * 128, cpc * 512:(cpc + 1) * 512]

        def prep2_all():
            H0, H1 = slice(0, 256), slice(256, 512)
            for idx in range(192):
                par = idx % 2
                mi = prep_srcs[idx][0]
                if idx >= 2:
                    p.wait("gpsimd", S["pp_q3"], prep_q3_v[idx - 2])
                ld = p.dma("gpsimd", wbuf[par][:], prep_srcs[idx][1],
                           S["g_prepB"])
                p.wait("gpsimd", S["g_prepB"], ld)
                p.wait("gpsimd", S["pp_fin"], prep_fin_v[mi])
                for hs in (H0, H1):
                    p.op("gpsimd", lambda e, par=par, mi=mi, hs=hs: e.tensor_tensor(
                        pstage[par][:, hs], wbuf[par][:, hs], scol256[mi][:],
                        ALU.mult))
                for hs in (H0, H1):
                    p.op("gpsimd", lambda e, par=par, hs=hs: e.tensor_tensor(
                        pst2[par][:, hs], pstage[par][:, hs], rc256[:],
                        ALU.add))
                for hs in (H0, H1):
                    p.op("gpsimd", lambda e, par=par, hs=hs: e.tensor_tensor(
                        pstage[par][:, hs], pst2[par][:, hs], rcp256[:],
                        ALU.min))
                for hs in (H0, H1):
                    p.op("gpsimd", lambda e, par=par, hs=hs: e.tensor_tensor(
                        pst2[par][:, hs], pstage[par][:, hs], rcm256[:],
                        ALU.max))
                if idx >= 2:
                    p.wait("gpsimd", S["g_prepst"], 16 * (idx - 1))
                q3 = None
                for hs in (H0, H1):
                    q3 = p.op("gpsimd", lambda e, par=par, hs=hs: e.tensor_tensor(
                        pout[par][:, hs], pst2[par][:, hs], nrc256[:],
                        ALU.add), S["pp_q3"])
                prep_q3_v[idx] = q3
                prep_state["st"] = p.dma("gpsimd", prep2_dst(idx),
                                         pout[par][:], S["g_prepst"])
            return

        prep_queue = deque()
        for i in range(64):
            prep_queue.append(prep1_thunk(i))
        prep_queue.append(prep_fin_thunk(0))
        for i in range(64, 128):
            prep_queue.append(prep1_thunk(i))
        prep_queue.append(prep_fin_thunk(1))
        for i in range(128, 192):
            prep_queue.append(prep1_thunk(i))
        prep_queue.append(prep_fin_thunk(2))
        for i in range(192):
            prep_queue.append(prep2_thunk(i))

        def drip(n):
            for _ in range(n):
                if prep_queue:
                    prep_queue.popleft()()

        s_qkv = ExitStack()
        Q_ = s_qkv.enter_context
        qT = [Q_(nc.sbuf_tensor(f"qT{i}", [128, T], BF16)) for i in range(IC)]
        kT = [Q_(nc.sbuf_tensor(f"kT{i}", [128, T], BF16)) for i in range(IC)]
        vA = [Q_(nc.sbuf_tensor(f"vA{i}", [128, H * 65], BF16)) for i in range(NT)]

        s_a = ExitStack()
        A_ = s_a.enter_context
        AT = [A_(nc.sbuf_tensor(f"AT{i}", [128, T], BF16)) for i in range(IC)]
        WTa = [[A_(nc.sbuf_tensor(f"WTa{j}_{i}", [128, C], BF16))
                for i in range(IC)] for j in range(2)]
        s_lnt = ExitStack()
        lnt = [s_lnt.enter_context(nc.sbuf_tensor(f"lnt{i}", [128, C], BF16))
               for i in range(2)]

        for i in range(NT):
            p.op("gpsimd", lambda e, t=vA[i]: e.memset(t[:], 1.0), S["misc"])
        vones_done = S["misc"].n

        p1_ln_v = [0] * NT
        for tt in range(NT):
            par = tt % 2
            if tt >= 2:
                p.wait("sync", S["p1_ln"], p1_ln_v[tt - 2])
            xsem = S["s_xld"] if par == 0 else S["s_xld1"]
            xld = p.dma("sync", xt[par][:], x_h[tt * 128:(tt + 1) * 128, :],
                        xsem)
            rs_v, nm_v = ln_stats(par, xt[par][:], S["p1_st"], S["p1_rs"],
                                  S["p1_nm"],
                                  pre_waits_v=[("vector", xsem, xld)])
            p.wait("scalar", S["p1_nm"], nm_v)
            if tt >= 2:
                p.wait("scalar", S["p1_tr"], (tt - 1) * IC)
            ln_v = p.op("scalar", lambda e, par=par: e.activation(
                lnt[par][:], xt[par][:], AF.Identity, bias=nmrc[par][:],
                scale=rstd[par]), S["p1_ln"])
            p1_ln_v[tt] = ln_v
            p.wait("tensor", S["p1_ln"], ln_v)
            if tt == 0:
                p.wait("tensor", S["misc"], consts_done)
            for ic in range(IC):
                bi = ic % 4
                pe_writes_bank(bi)
                mm_v = p.op("tensor", lambda e, par=par, ic=ic, bi=bi: e.matmul(
                    bank(bi)[:, 0:128], lnt[par][:, ic * 128:(ic + 1) * 128],
                    ident[:], start=True, stop=True), S["p1_tr"])
                note_bank_write(bi, S["p1_tr"], mm_v)
                eng, sem = (("vector", S["at_evD"]) if ic % 2 == 0
                            else ("scalar", S["at_evA"]))
                p.wait(eng, S["p1_tr"], mm_v + 1)
                reader_waits_bank(eng, bi)
                if eng == "vector":
                    ev_v = p.op(eng, lambda e, ic=ic, tt=tt, bi=bi: e.tensor_copy(
                        AT[ic][:, tt * 128:(tt + 1) * 128], bank(bi)[:, 0:128]),
                        sem)
                else:
                    ev_v = p.op(eng, lambda e, ic=ic, tt=tt, bi=bi: e.activation(
                        AT[ic][:, tt * 128:(tt + 1) * 128], bank(bi)[:, 0:128],
                        AF.Identity), sem)
                note_bank_read(bi, sem, ev_v)
            pe_writes_bank(6)
            dmy = p.op("tensor", lambda e: e.matmul(
                bank(6)[:, 0:128], ident[:], ident[:], start=True, stop=True),
                S["p1_tr"])
            note_bank_write(6, S["p1_tr"], dmy)
        vsp(S["at_evD"])
        asp(S["at_evA"])
        at_evD_done = S["at_evD"].n
        at_evA_done = S["at_evA"].n
        s_lnt.close()

        def load_wta(j, wb_dram, cast_idx, war_v):
            p.wait("sync", cast_sems[cast_idx], 16)
            if war_v:
                p.wait("sync", S["prj_mm"], war_v)
            last = 0
            for ic in range(IC):
                last = p.dma("sync", WTa[j][ic][:],
                             wb_dram[0:C, ic * 128:(ic + 1) * 128],
                             S["s_wtr"], transpose=True)
            return last

        def proj_fm(j, dstT, wtr_v, scale, evac_eng, evac_sem):
            p.wait("tensor", S["s_wtr"], wtr_v)
            for ts in range(4):
                p.wait("tensor", S["at_evD"], min(4 * (4 * ts + 4) + 1, at_evD_done))
                p.wait("tensor", S["at_evA"], min(4 * (4 * ts + 4) + 1, at_evA_done))
                for oc in range(IC):
                    bi = (oc * 4 + ts) % 4
                    pe_writes_bank(bi)
                    mm_v = None
                    for ic in range(IC):
                        r = p.op("tensor", lambda e, j=j, ic=ic, oc=oc, ts=ts, bi=bi: e.matmul(
                            bank(bi), WTa[j][ic][:, oc * 128:(oc + 1) * 128],
                            AT[ic][:, ts * 512:(ts + 1) * 512],
                            start=(ic == 0), stop=(ic == IC - 1)),
                            S["prj_mm"] if ic == IC - 1 else None)
                        if r is not None:
                            mm_v = r
                    note_bank_write(bi, S["prj_mm"], mm_v)
                    reader_waits_bank(evac_eng, bi)
                    if evac_eng == "scalar":
                        ev_v = p.op(evac_eng, lambda e, oc=oc, ts=ts, bi=bi, d=dstT, sc=scale: e.activation(
                            d[oc][:, ts * 512:(ts + 1) * 512], bank(bi),
                            AF.Identity, scale=sc), evac_sem)
                    else:
                        ev_v = p.op(evac_eng, lambda e, oc=oc, ts=ts, bi=bi, d=dstT: e.tensor_copy(
                            d[oc][:, ts * 512:(ts + 1) * 512], bank(bi)),
                            evac_sem)
                    note_bank_read(bi, evac_sem, ev_v)
                    drip(3)

        wtr_q = load_wta(0, wqb, 0, 0)
        wtr_k = load_wta(1, wkb, 1, 0)
        proj_fm(0, qT, wtr_q, ASCALE, "scalar", S["prj_evA"])
        proj_fm(1, kT, wtr_k, 1.0, "vector", S["prj_evD"])
        wtr_v = load_wta(0, wvb, 2, S["prj_mm"].n - 32)

        p.wait("tensor", S["s_wtr"], wtr_v)
        p.wait("vector", S["misc"], vones_done)
        p.wait("scalar", S["misc"], vones_done)
        for tt in range(NT):
            for oc2 in range(2):
                bi = 4 + (tt * 2 + oc2) % 2
                pe_writes_bank(bi)
                mm_v = None
                for ic in range(IC):
                    r = p.op("tensor", lambda e, ic=ic, tt=tt, oc2=oc2, bi=bi: e.matmul(
                        bank(bi), AT[ic][:, tt * 128:(tt + 1) * 128],
                        WTa[0][ic][:, oc2 * 512:(oc2 + 1) * 512],
                        start=(ic == 0), stop=(ic == IC - 1)),
                        S["prj_mm"] if ic == IC - 1 else None)
                    if r is not None:
                        mm_v = r
                note_bank_write(bi, S["prj_mm"], mm_v)
                eng, sem = (("vector", S["prj_evD"]) if oc2 == 0
                            else ("scalar", S["prj_evA"]))
                reader_waits_bank(eng, bi)
                ev_v = None
                for hh in range(8):
                    h = oc2 * 8 + hh
                    if eng == "vector":
                        ev_v = p.op(eng, lambda e, tt=tt, h=h, hh=hh, bi=bi: e.tensor_copy(
                            vA[tt][:, h * 65:h * 65 + 64],
                            bank(bi)[:, hh * 64:(hh + 1) * 64]), sem)
                    else:
                        ev_v = p.op(eng, lambda e, tt=tt, h=h, hh=hh, bi=bi: e.activation(
                            vA[tt][:, h * 65:h * 65 + 64],
                            bank(bi)[:, hh * 64:(hh + 1) * 64], AF.Identity), sem)
                note_bank_read(bi, sem, ev_v)
                drip(3)
        vsp(S["prj_evD"])
        asp(S["prj_evA"])
        v_doneD = S["prj_evD"].n
        v_doneA = S["prj_evA"].n

        if debug == 1:
            p.wait("gpsimd", S["at_evD"], at_evD_done)
            p.wait("gpsimd", S["at_evA"], at_evA_done)
            for ic in range(IC):
                p.dma("gpsimd", dbgAT[ic * 128:(ic + 1) * 128, :], AT[ic][:],
                      S["misc2"])
            p.wait("gpsimd", S["prj_evD"], v_doneD)
            p.wait("gpsimd", S["prj_evA"], v_doneA)
            for tt in range(NT):
                p.dma("gpsimd", dbgvA2[tt * 128:(tt + 1) * 128, :], vA[tt][:],
                      S["misc2"])
            for ic in range(IC):
                p.dma("gpsimd", dbgWv[ic * 128:(ic + 1) * 128, :],
                      WTa[0][ic][:], S["misc2"])
            p.wait("gpsimd", S["misc2"], S["misc2"].n)

        s_a.close()

        s_y = ExitStack()
        Y_ = s_y.enter_context
        yT = [Y_(nc.sbuf_tensor(f"yT{i}", [128, T], BF16, side="right"))
              for i in range(IC)]
        WTb = [Y_(nc.sbuf_tensor(f"WTb{i}", [128, C], BF16, side="right"))
               for i in range(IC)]

        p.wait("sync", S["g_cast3"], 16)
        p.wait("sync", S["prj_mm"], S["prj_mm"].n)
        wtr_p = 0
        for ic in range(IC):
            wtr_p = p.dma("sync", WTb[ic][:], wpb[0:C, ic * 128:(ic + 1) * 128],
                          S["s_wtr"], transpose=True)

        s_c = ExitStack()
        C_ = s_c.enter_context
        ptile = [C_(nc.sbuf_tensor(f"pt{i}", [128, BANK], BF16)) for i in range(4)]
        yu = [C_(nc.sbuf_tensor(f"yu{i}", [65, BANK], F32)) for i in range(4)]
        lall = C_(nc.sbuf_tensor("lall", [4, BANK], F32))
        rall = C_(nc.sbuf_tensor("rall", [4, BANK], F32))
        rallb = C_(nc.sbuf_tensor("rallb", [4, BANK], BF16))

        p.wait("tensor", S["prj_evA"], v_doneA)
        p.wait("tensor", S["prj_evD"], v_doneD)

        SBK = [0, 1, 6, 7]
        YBK = [2, 3, 4, 5]
        yev_done = [0, 0, 0, 0]
        av_prev = [None] * 4
        rb_prev = 0
        pending = []

        def emit_tail(st):
            si = st["si"]
            sbk = SBK[si]
            pe_writes_bank(sbk)
            if st.get("heat"):
                p.wait("tensor", S["c_lcp"], st["ev"])
                p.op("tensor", lambda e, sbk=sbk: e.matmul(
                    bank(sbk)[0:1, 0:1], ones_row_b[:, 0:1],
                    ones_row_b[:, 0:1], start=True, stop=True))
                p.wait("tensor", S["c_rev"], st["ld"])
                p.op("tensor", lambda e, sbk=sbk: e.matmul(
                    bank(sbk)[0:1, 0:1], ones_row_b[:, 0:1],
                    ones_row_b[:, 0:1], start=True, stop=True))
            p.wait("tensor", S["c_rbf"], st["rb"])
            rmm = p.op("tensor", lambda e, si=si, sbk=sbk: e.matmul(
                bank(sbk)[0:64, :], sel4[:, si * 64:(si + 1) * 64],
                rallb[:], start=True, stop=True), S["c_rmm"])
            note_bank_write(sbk, S["c_rmm"], rmm)
            reader_waits_bank("vector", sbk)
            yv = p.op("vector", lambda e, st=st, si=si, sbk=sbk: e.tensor_tensor(
                yT[st["hpt"]][st["prow"]:st["prow"] + 64,
                              st["tq0"]:st["tq0"] + 512],
                yu[si][0:64, :], bank(sbk)[0:64, 0:512], ALU.mult),
                S["c_yev"])
            note_bank_read(sbk, S["c_yev"], yv)
            yev_done[si] = yv

        for j in range(4):
            for tqs in range(4):
                tq0 = tqs * 512
                nk = 4 * tqs + 4
                avv = [[None] * nk for _ in range(4)]
                exv = [[None] * nk for _ in range(4)]

                for tkc in range(nk):
                    off = max(0, 128 * tkc - tq0)
                    npr = 512 - off
                    diag = tkc >= 4 * tqs
                    for si in range(4):
                        h = 4 * j + si
                        hpt, pr = h // 2, (h % 2) * 64
                        sbi = SBK[si]
                        pe_writes_bank(sbi)
                        if diag:
                            p.op("tensor", lambda e, sbi=sbi: e.matmul(
                                bank(sbi)[:, 0:128], masknT[:], ident[:],
                                start=True, stop=False))
                        smm = p.op("tensor", lambda e, tkc=tkc, off=off, npr=npr, sbi=sbi, diag=diag, hpt=hpt, pr=pr, tq0=tq0: e.matmul(
                            bank(sbi)[:, 0:npr],
                            kT[hpt][pr:pr + 64, tkc * 128:(tkc + 1) * 128],
                            qT[hpt][pr:pr + 64, tq0 + off:tq0 + 512],
                            start=not diag, stop=True), S["c_smm"])
                        note_bank_write(sbi, S["c_smm"], smm)
                        reader_waits_bank("scalar", sbi)
                        if tkc >= 1 and avv[si][tkc - 1] is not None:
                            p.wait("scalar", S["c_avmm"], avv[si][tkc - 1])
                        elif tkc == 0 and av_prev[si] is not None:
                            p.wait("scalar", S["c_avmm"], av_prev[si])
                        if npr <= 128:
                            ja()
                        ex = p.op("scalar", lambda e, si=si, npr=npr, sbi=sbi: e.activation(
                            ptile[si][:, 0:npr], bank(sbi)[:, 0:npr], AF.Exp),
                            S["c_expA"])
                        note_bank_read(sbi, S["c_expA"], ex)
                        exv[si][tkc] = ex
                    for si in range(4):
                        h = 4 * j + si
                        if tkc == 0:
                            pe_writes_bank(YBK[si])
                        p.wait("tensor", S["c_expA"], exv[si][tkc])
                        v = p.op("tensor", lambda e, si=si, tkc=tkc, off=off, npr=npr, h=h, ybi=YBK[si], first=(tkc == 0), last=(tkc == nk - 1): e.matmul(
                            bank(ybi)[0:65, off:off + npr],
                            vA[tkc][:, h * 65:(h + 1) * 65],
                            ptile[si][:, 0:npr],
                            start=first, stop=last), S["c_avmm"])
                        avv[si][tkc] = v
                    if tkc == 2 and pending:
                        for st in pending:
                            emit_tail(st)
                        pending = []

                for si in range(4):
                    eng = "scalar" if si % 2 == 0 else "vector"
                    av_prev[si] = avv[si][nk - 1]
                    note_bank_write(YBK[si], S["c_avmm"], avv[si][nk - 1])
                    reader_waits_bank(eng, YBK[si])
                    if yev_done[si]:
                        p.wait(eng, S["c_yev"], yev_done[si])
                    if si % 2 == 0:
                        ev = p.op("scalar", lambda e, si=si, b=YBK[si]: e.activation(
                            yu[si][:, :], bank(b)[0:65, 0:512], AF.Identity),
                            S["c_lcp"])
                    else:
                        ev = p.op("vector", lambda e, si=si, b=YBK[si]: e.tensor_copy(
                            yu[si][:, :], bank(b)[0:65, 0:512]), S["c_lcp"])
                    note_bank_read(YBK[si], S["c_lcp"], ev)
                asp(S["c_lcp"])
                ev_sp = vsp(S["c_lcp"])
                p.wait("gpsimd", S["c_lcp"], ev_sp)
                if rb_prev:
                    p.wait("gpsimd", S["c_rbf"], rb_prev)
                for si in range(4):
                    ldma = p.dma("gpsimd", lall[si:si + 1, :],
                                 yu[si][64:65, :], S["c_rev"])
                p.wait("vector", S["c_rev"], ldma)
                p.op("vector", lambda e: e.reciprocal(rall[:], lall[:]))
                jv()
                p.op("vector", lambda e: e.tensor_copy(rallb[:], rall[:]))
                rb = vsp(S["c_rbf"])
                rb_prev = rb
                pending = []
                for si in range(4):
                    h = 4 * j + si
                    pending.append(dict(si=si, hpt=h // 2, prow=(h % 2) * 64,
                                        tq0=tq0, rb=rb, ev=ev_sp, ld=ldma,
                                        heat=(si == 0)))
                drip(18)
        for st in pending:
            emit_tail(st)
        while prep_queue:
            drip(1)
        vsp(S["c_yev"])
        c_y_done = S["c_yev"].n
        prepst_done = prep_state["st"]

        s_c.close()
        s_qkv.close()
        s_prep.close()


        s_d = ExitStack()
        D_ = s_d.enter_context
        zt = [D_(nc.sbuf_tensor(f"zt{i}", [128, C], F32, side="right"))
              for i in range(2)]
        x1t = [D_(nc.sbuf_tensor(f"x1t{i}", [128, C], F32, side="right"))
               for i in range(4)]
        ln2t = [D_(nc.sbuf_tensor(f"ln2t{i}", [128, C], F32, side="right"))
                for i in range(2)]
        qstage = [D_(nc.sbuf_tensor(f"qstage{i}", [128, C], F32, side="right"))
                  for i in range(2)]
        xqt = [D_(nc.sbuf_tensor(f"xqt{i}", [128, C], BF16, side="right"))
               for i in range(2)]

        s_es = ExitStack()
        ES_ = s_es.enter_context
        xqTg = [[ES_(nc.sbuf_tensor(f"xqTg{j}_{i}", [128, 512], BF16))
                 for i in range(IC)] for j in range(2)]
        wgTs = [[ES_(nc.sbuf_tensor(f"wgTs{j}_{i}", [128, 1024], BF16))
                 for i in range(IC)] for j in range(2)]
        wuTs = [[ES_(nc.sbuf_tensor(f"wuTs{j}_{i}", [128, 1024], BF16))
                 for i in range(IC)] for j in range(2)]
        wdTc = [ES_(nc.sbuf_tensor(f"wdTc{j}", [128, C], BF16))
                for j in range(4)]

        if debug == 1:
            p.wait("gpsimd", S["c_yev"], c_y_done)
            for ic in range(IC):
                p.dma("gpsimd", dbgyT[ic * 128:(ic + 1) * 128, :], yT[ic][:],
                      S["misc2"])
            p.wait("gpsimd", S["misc2"], S["misc2"].n)

        p.wait("tensor", S["c_yev"], c_y_done)
        p.wait("tensor", S["s_wtr"], wtr_p)
        p.wait("gpsimd", S["p1_ln"], S["p1_ln"].n)
        x1st_v = [0] * NT
        xqst_v = [0] * NT
        xq_sems = [S["g_xqst"], S["g_xqst1"], S["g_xqst2"], S["g_xqst3"],
                   S["g_xqst4"], S["g_xqst5"], S["g_xqst6"], S["g_xqst7"]]
        xsems = [S["s_xld"], S["s_xld1"], S["s_xld2"], S["s_xld3"]]

        def emitD_A(tt):
            par = tt % 4
            aob = 4 * (tt % 2)
            pe_writes_bank(aob, 2)
            mm_v = None
            for oc2 in range(2):
                for ic in range(IC):
                    r = p.op("tensor", lambda e, ic=ic, tt=tt, oc2=oc2, aob=aob: e.matmul(
                        bank(aob + oc2), yT[ic][:, tt * 128:(tt + 1) * 128],
                        WTb[ic][:, oc2 * 512:(oc2 + 1) * 512],
                        start=(ic == 0), stop=(ic == IC - 1)),
                        S["d_aomm"] if ic == IC - 1 else None)
                    if r is not None:
                        mm_v = r
            note_bank_write(aob, S["d_aomm"], mm_v, 2)
            xp = tt % 2
            if tt >= 2:
                p.wait("gpsimd", S["d_sg5"], d_x1v[tt - 2])
            xld = p.dma("gpsimd", xt[xp][:], x_h[tt * 128:(tt + 1) * 128, :],
                        xsems[xp])
            ao_view = PS[:, aob * BANK:(aob + 2) * BANK]
            reader_waits_bank("vector", aob, 2)
            rs_v, nm_v = ln_stats(par, ao_view, S["d_sg1"], S["d_sg2"],
                                  S["d_sg3"])
            p.wait("scalar", S["d_sg3"], nm_v)
            if tt >= 2:
                p.wait("scalar", S["d_sg5"], d_x1v[tt - 2])
            z_v = p.op("scalar", lambda e, par=par, xp=xp, ao=ao_view: e.activation(
                zt[xp][:], ao, AF.Identity, scale=rstd[par], bias=nmrc[par][:]),
                S["d_sg4"])
            note_bank_read(aob, S["d_sg4"], z_v, 2)
            p.wait("vector", S["d_sg4"], z_v)
            p.wait("vector", xsems[xp], xld)
            if tt >= 4:
                p.wait("vector", S["g_x1st"], x1st_v[tt - 4])
            x1_v = p.op("vector", lambda e, par=par, xp=xp: e.tensor_add(
                x1t[par][:], zt[xp][:], xt[xp][:]), S["d_sg5"])
            p.wait("gpsimd", S["d_sg5"], x1_v)
            x1st_v[tt] = p.dma("gpsimd", x1d[tt * 128:(tt + 1) * 128, :],
                               x1t[par][:], S["g_x1st"])
            return x1_v

        def emitD_B(tt, x1_v):
            par = tt % 4
            xp = tt % 2
            rs2_v, nm2_v = ln_stats(par, x1t[par][:], S["d_sg1"], S["d_sg2"],
                                    S["d_sg3"],
                                    pre_waits_v=[("vector", S["d_sg5"], x1_v)])
            p.wait("scalar", S["d_sg3"], nm2_v)
            l2_v = p.op("scalar", lambda e, par=par, xp=xp: e.activation(
                ln2t[xp][:], x1t[par][:], AF.Identity, scale=rstd[par],
                bias=nmrc[par][:]), S["d_sg4"])
            p.wait("vector", S["d_sg4"], l2_v)
            p.op("vector", lambda e, par=par, xp=xp: e.tensor_reduce(
                sp[par][:, 0:1], ln2t[xp][:], AX.X, ALU.max,
                apply_absolute_value=True))
            jv()
            p.op("vector", lambda e, par=par: e.tensor_scalar_max(
                sp[par][:, 1:2], sp[par][:, 0:1], 1e-5))
            jv()
            p.op("vector", lambda e, par=par: e.reciprocal(
                sp[par][:, 2:3], sp[par][:, 1:2]))
            p.op("vector", lambda e, par=par, tt=tt: e.tensor_scalar_mul(
                rga_all[:, tt:tt + 1], sp[par][:, 1:2], 1.0 / 127.0))
            p.op("vector", lambda e, par=par: e.tensor_scalar_mul(
                sp[par][:, 3:4], sp[par][:, 2:3], 127.0))
            qs_v = vsp(S["d_sg6"])
            p.wait("scalar", S["d_sg6"], qs_v)
            q1_v = p.op("scalar", lambda e, par=par, xp=xp: e.activation(
                qstage[xp][:], ln2t[xp][:], AF.Identity, scale=sp[par][:, 3:4],
                bias=rcc[:]), S["d_sg4"])
            p.wait("vector", S["d_sg4"], q1_v)
            if tt >= 2:
                p.wait("vector", S["g_xqst"], xqst_v[tt - 2])
            q2_v = p.op("vector", lambda e, xp=xp: e.tensor_scalar(
                xqt[xp][:], qstage[xp][:], -RC, None, ALU.add), S["d_sg6"])
            p.wait("gpsimd", S["d_sg6"], q2_v)
            xqst_v[tt] = p.dma("gpsimd", xqd[tt * 128:(tt + 1) * 128, :],
                               xqt[xp][:], xq_sems[tt % 8])

        d_x1v = [0] * NT
        for tt in range(NT):
            d_x1v[tt] = emitD_A(tt)
            if tt >= 2:
                emitD_B(tt - 2, d_x1v[tt - 2])
        emitD_B(NT - 2, d_x1v[NT - 2])
        emitD_B(NT - 1, d_x1v[NT - 1])
        d_q2_done = S["d_sg6"].n

        p.wait("sync", S["c_yev"], c_y_done)
        p.wait("sync", S["g_prepst"], prepst_done)
        xqtg_v = [0] * NG
        wstr_v = [[0] * 8 for _ in range(NG)]

        def gA_count(g, sl):
            return g * 64 + (sl + 1) * 8

        def gA_count2(g, sp2):
            return g * 64 + (sp2 + 1) * 16

        def emit_stream_slice(g, sp2):
            wssem = S["s_wstr"] if sp2 % 2 == 0 else S["s_wstr1"]
            if sp2 >= 2:
                p.wait("sync", S["e_gmm"], gA_count2(g, sp2 - 2))
            elif g > 0:
                p.wait("sync", S["e_gmm"], gA_count2(g - 1, 2 + sp2))
            for ic in range(IC):
                p.dma("sync", wgTs[sp2 % 2][ic][:],
                      wgq[sp2 * 1024:(sp2 + 1) * 1024, ic * 128:(ic + 1) * 128],
                      wssem, transpose=True)
                wstr_v[g][sp2] = p.dma(
                    "sync", wuTs[sp2 % 2][ic][:],
                    wuq[sp2 * 1024:(sp2 + 1) * 1024, ic * 128:(ic + 1) * 128],
                    wssem, transpose=True)

        xq_sems_all = [S["g_xqst"], S["g_xqst1"], S["g_xqst2"], S["g_xqst3"],
                       S["g_xqst4"], S["g_xqst5"], S["g_xqst6"], S["g_xqst7"]]

        def emit_group_streams_head(g):
            for t4 in range(4):
                tt = 4 * g + t4
                p.wait("sync", xq_sems_all[tt % 8], 16 * (tt // 8 + 1))
            if g >= 2:
                p.wait("sync", S["e_gmm"], gA_count(g - 2, 7))
            xqsem = S["s_xqtg"] if g % 2 == 0 else S["s_xqtg1"]
            for ic in range(IC):
                xqtg_v[g] = p.dma(
                    "sync", xqTg[g % 2][ic][:],
                    xqd[g * 512:(g + 1) * 512, ic * 128:(ic + 1) * 128],
                    xqsem, transpose=True)
            emit_stream_slice(g, 0)

        def emit_group_streams_tail(g):
            for sp2 in range(1, 4):
                emit_stream_slice(g, sp2)

        emit_group_streams_head(0)
        emit_group_streams_tail(0)

        s_d.close()
        s_y.close()
        s_ec = ExitStack()
        EC_ = s_ec.enter_context
        sgb = [EC_(nc.sbuf_tensor(f"sgb{j}", [128, 512], F32)) for j in range(2)]
        hg = [EC_(nc.sbuf_tensor(f"hg{i}", [128, INNER], BF16)) for i in range(GT)]
        hqs = [EC_(nc.sbuf_tensor(f"hqs{j}", [128, 512], F32)) for j in range(2)]
        hqt = EC_(nc.sbuf_tensor("hqt", [128, INNER], BF16))
        hqTgB = EC_(nc.sbuf_tensor("hqTgB", [128, 4 * INNER], BF16))
        ffo = EC_(nc.sbuf_tensor("ffo", [128, C], F32))
        z2b = EC_(nc.sbuf_tensor("z2b", [128, C], F32))
        outb = EC_(nc.sbuf_tensor("outb", [128, C], F32))
        rgag = [EC_(nc.sbuf_tensor(f"rgag{i}", [128, 1], F32)) for i in range(GT)]
        rgau = [EC_(nc.sbuf_tensor(f"rgau{i}", [128, 1], F32)) for i in range(GT)]
        rgad = [EC_(nc.sbuf_tensor(f"rgad{i}", [128, 1], F32)) for i in range(GT)]
        spE = [EC_(nc.sbuf_tensor(f"spE{i}", [128, 6], F32)) for i in range(GT)]

        for g in range(NG):
            p.wait("vector", S["pp_fin"], prep_fin_v[2])
            p.wait("vector", S["d_sg6"], d_q2_done)
            for t4 in range(GT):
                tt = 4 * g + t4
                p.op("vector", lambda e, t4=t4, tt=tt: e.tensor_mul(
                    rgag[t4][:], rga_all[:, tt:tt + 1], rgw_col[0][:]))
                p.op("vector", lambda e, t4=t4, tt=tt: e.tensor_mul(
                    rgau[t4][:], rga_all[:, tt:tt + 1], rgw_col[1][:]))
            rgx_v = vsp(S["e_sgA"])

            h_done_v = [0] * GT
            for sl in range(8):
                sp2, half = sl // 2, sl % 2
                p.wait("tensor", S["s_wstr"] if sp2 % 2 == 0 else S["s_wstr1"],
                       wstr_v[g][sp2])
                if sl == 0:
                    p.wait("tensor",
                           S["s_xqtg"] if g % 2 == 0 else S["s_xqtg1"],
                           xqtg_v[g])
                for t4 in range(GT):
                    gbi = t4 % 2
                    ubi = 2 + t4 % 2
                    pe_writes_bank(gbi)
                    for ic in range(IC):
                        p.op("tensor", lambda e, ic=ic, t4=t4, g=g, gbi=gbi, sp2=sp2, half=half: e.matmul(
                            bank(gbi),
                            xqTg[g % 2][ic][:, t4 * 128:(t4 + 1) * 128],
                            wgTs[sp2 % 2][ic][:, half * 512:(half + 1) * 512],
                            start=(ic == 0), stop=(ic == IC - 1)),
                            S["e_gmm"] if ic == IC - 1 else None)
                    gmm_v = S["e_gmm"].n
                    note_bank_write(gbi, S["e_gmm"], gmm_v)
                    pe_writes_bank(ubi)
                    for ic in range(IC):
                        p.op("tensor", lambda e, ic=ic, t4=t4, g=g, ubi=ubi, sp2=sp2, half=half: e.matmul(
                            bank(ubi),
                            xqTg[g % 2][ic][:, t4 * 128:(t4 + 1) * 128],
                            wuTs[sp2 % 2][ic][:, half * 512:(half + 1) * 512],
                            start=(ic == 0), stop=(ic == IC - 1)),
                            S["e_gmm"] if ic == IC - 1 else None)
                    umm_v = S["e_gmm"].n
                    note_bank_write(ubi, S["e_gmm"], umm_v)
                    spar = t4 % 2
                    reader_waits_bank("scalar", gbi)
                    p.wait("scalar", S["e_sgA"], rgx_v)
                    if sl > 0 or t4 >= 2:
                        p.wait("scalar", S["e_hD"], S["e_hD"].n)
                    sg_v = p.op("scalar", lambda e, spar=spar, gbi=gbi, t4=t4: e.activation(
                        sgb[spar][:], bank(gbi), AF.Silu, scale=rgag[t4][:]),
                        S["e_sgA"])
                    note_bank_read(gbi, S["e_sgA"], sg_v)
                    reader_waits_bank("vector", ubi)
                    p.wait("vector", S["e_sgA"], sg_v)
                    h_v = p.op("vector", lambda e, spar=spar, ubi=ubi, t4=t4, sl=sl: e.scalar_tensor_tensor(
                        hg[t4][:, sl * 512:(sl + 1) * 512], bank(ubi),
                        rgau[t4][:], sgb[spar][:], ALU.mult, ALU.mult),
                        S["e_hD"])
                    note_bank_read(ubi, S["e_hD"], h_v)
                    h_done_v[t4] = h_v

            if g + 1 < NG:
                emit_group_streams_head(g + 1)

            if debug == 1:
                for t4 in range(GT):
                    tt = 4 * g + t4
                    p.wait("gpsimd", S["e_hD"], h_done_v[t4])
                    p.dma("gpsimd", dbgh[tt * 128:(tt + 1) * 128, :],
                          hg[t4][:], S["misc2"])

            quad_i = 0
            pending_tr = None
            tr_ev = [0] * 8
            tr_tile_done = 0
            for t4 in range(GT):
                p.wait("vector", S["e_hD"], h_done_v[t4])
                p.op("vector", lambda e, t4=t4, s=spE[t4]: e.tensor_reduce(
                    s[:, 0:1], hg[t4][:], AX.X, ALU.max,
                    apply_absolute_value=True))
            for t4 in range(GT):
                p.op("vector", lambda e, s=spE[t4]: e.tensor_scalar_max(
                    s[:, 1:2], s[:, 0:1], 1e-5))
            for t4 in range(GT):
                p.op("vector", lambda e, s=spE[t4]: e.reciprocal(
                    s[:, 2:3], s[:, 1:2]))
            for t4 in range(GT):
                p.op("vector", lambda e, s=spE[t4], t4=t4: e.scalar_tensor_tensor(
                    rgad[t4][:], s[:, 1:2], 1.0 / 127.0, rgw_col[2][:],
                    ALU.mult, ALU.mult))
            for t4 in range(GT):
                p.op("vector", lambda e, s=spE[t4]: e.tensor_scalar_mul(
                    s[:, 3:4], s[:, 2:3], 127.0))
            amx_v = vsp(S["e_amx"])
            p.wait("scalar", S["e_amx"], amx_v)

            def emit_tr_evac(pt, gate):
                t4q, chkq, b = pt
                eng = "scalar" if b % 2 == 0 else "vector"
                sem = S["e_trev"] if b % 2 == 0 else S["e_trev2"]
                if gate is not None:
                    p.wait(eng, S["e_trm"], gate)
                else:
                    p.wait(eng, S["e_trm"], S["e_trm"].n)
                    if eng == "scalar":
                        ja(); ja()
                    else:
                        jv(); jv()
                ev = p.op(eng, lambda e, t4q=t4q, chkq=chkq, b=b: e.tensor_copy(
                    hqTgB[:, t4q * 4096 + chkq * 512:t4q * 4096 + chkq * 512 + 512],
                    bank(b)) if b % 2
                    else e.activation(
                    hqTgB[:, t4q * 4096 + chkq * 512:t4q * 4096 + chkq * 512 + 512],
                    bank(b), AF.Identity), sem)
                tr_ev[b] = ev
                note_bank_read(b, sem, ev)

            for t4 in range(GT):
                s = spE[t4]
                if tr_tile_done:
                    p.wait("vector", S["e_trm"], tr_tile_done)
                for chk in range(8):
                    qpar = chk % 2
                    if chk >= 2:
                        p.wait("scalar", S["e_qD"], S["e_qD"].n)
                    qa = p.op("scalar", lambda e, t4=t4, chk=chk, qpar=qpar, s=s: e.activation(
                        hqs[qpar][:], hg[t4][:, chk * 512:(chk + 1) * 512],
                        AF.Identity, scale=s[:, 3:4], bias=rcc[:]), S["e_qA"])
                    p.wait("vector", S["e_qA"], qa)
                    p.op("vector", lambda e, chk=chk, qpar=qpar: e.tensor_scalar(
                        hqt[:, chk * 512:(chk + 1) * 512], hqs[qpar][:],
                        -RC, None, ALU.add), S["e_qD"])
                    chk_v = vsp(S["e_qD"])
                    p.wait("tensor", S["e_qD"], chk_v)
                    b = quad_i % 8
                    if quad_i < 8:
                        pe_writes_bank(b)
                    if tr_ev[b]:
                        p.wait("tensor",
                               S["e_trev"] if b % 2 == 0 else S["e_trev2"],
                               tr_ev[b])
                    tm2 = None
                    for bq in range(4):
                        icn = chk * 4 + bq
                        tm = p.op("tensor", lambda e, icn=icn, b=b, bq=bq: e.matmul(
                            bank(b)[:, bq * 128:(bq + 1) * 128],
                            hqt[:, icn * 128:(icn + 1) * 128], ident[:],
                            start=True, stop=True), S["e_trm"])
                        if bq == 1:
                            tm2 = tm
                    if pending_tr is not None:
                        emit_tr_evac(pending_tr, tm2)
                    pending_tr = (t4, chk, b)
                    quad_i += 1
                    if not (t4 == GT - 1 and chk == 7):
                        bn = quad_i % 8
                        if tr_ev[bn]:
                            p.wait("tensor",
                                   S["e_trev"] if bn % 2 == 0 else S["e_trev2"],
                                   tr_ev[bn])
                        p.wait("tensor", S["e_qA"], S["e_qA"].n + 1)
                        p.op("tensor", lambda e, bn=bn: e.matmul(
                            bank(bn)[0:1, 0:1], ones_row_b[:, 0:1],
                            ones_row_b[:, 0:1], start=True, stop=True))
                tr_tile_done = S["e_trm"].n
                hq_v = S["e_qD"].n
                if debug == 1:
                    tt = 4 * g + t4
                    p.wait("gpsimd", S["e_qD"], hq_v)
                    dmp = p.dma("gpsimd", dbghq[tt * 128:(tt + 1) * 128, :],
                                hqt[:], S["misc2"])
                    p.wait("vector", S["misc2"], dmp)
            emit_tr_evac(pending_tr, None)
            pending_tr = None
            trA_fin = asp(S["e_trev"])
            trD_fin = vsp(S["e_trev2"])

            wdc_sems = [S["s_wdc"], S["s_wdc1"], S["s_wdc2"], S["s_wdc3"]]
            dn_mm_v = 0
            p.wait("tensor", S["e_trev"], trA_fin)
            p.wait("tensor", S["e_trev2"], trD_fin)
            for icn in range(32):
                dpar = icn % 4
                if icn >= 4:
                    p.wait("scalar", S["e_dmm"], S["e_dmm"].n - 24)
                elif g > 0:
                    p.wait("scalar", S["e_dmm"], g * 256)
                wdv = p.dma("scalar", wdTc[dpar][:],
                            wdq[0:C, icn * 128:(icn + 1) * 128],
                            wdc_sems[dpar], transpose=True)
                p.wait("tensor", wdc_sems[dpar], wdv)
                if icn == 0:
                    for b in range(8):
                        pe_writes_bank(b)
                for t4 in range(GT):
                    for oc2 in range(2):
                        dn_mm_v = p.op("tensor", lambda e, icn=icn, t4=t4, oc2=oc2, dpar=dpar: e.matmul(
                            bank(t4 * 2 + oc2),
                            hqTgB[:, t4 * 4096 + icn * 128:t4 * 4096 + icn * 128 + 128],
                            wdTc[dpar][:, oc2 * 512:(oc2 + 1) * 512],
                            start=(icn == 0), stop=(icn == 31)), S["e_dmm"])
            for b in range(8):
                note_bank_write(b, S["e_dmm"], dn_mm_v)

            if g + 1 < NG:
                emit_group_streams_tail(g + 1)

            for t4 in range(GT):
                tt = 4 * g + t4
                par = t4 % 2
                p.wait("scalar", S["e_dmm"], dn_mm_v)
                if g > 0 or t4 > 0:
                    p.wait("scalar", S["e_fin"], S["e_fin"].n)
                    p.wait("scalar", S["d_sg1"], S["d_sg1"].n)
                reader_waits_bank("scalar", t4 * 2)
                ev = p.op("scalar", lambda e, t4=t4: e.activation(
                    ffo[:, 0:512],
                    bank(t4 * 2), AF.Identity, scale=rgad[t4][:]),
                    S["e_deA"])
                note_bank_read(t4 * 2, S["e_deA"], ev)
                reader_waits_bank("vector", t4 * 2 + 1)
                if g > 0 or t4 > 0:
                    p.wait("vector", S["d_sg4"], S["d_sg4"].n)
                evD = p.op("vector", lambda e, t4=t4: e.tensor_scalar(
                    ffo[:, 512:1024], bank(t4 * 2 + 1), rgad[t4][:],
                    None, ALU.mult), S["e_trev2"])
                note_bank_read(t4 * 2 + 1, S["e_trev2"], evD)
                if debug == 1:
                    tt2 = 4 * g + t4
                    p.wait("gpsimd", S["e_deA"], ev)
                    dmp2 = p.dma("gpsimd", dbgffo[tt2 * 128:(tt2 + 1) * 128, :],
                                 ffo[:], S["misc2"])
                    p.wait("scalar", S["misc2"], dmp2)
                    p.wait("vector", S["misc2"], dmp2)
                rs_v, nm_v = ln_stats(par, ffo[:], S["d_sg1"], S["d_sg2"],
                                      S["d_sg3"],
                                      pre_waits_v=[("vector", S["e_deA"], ev)])
                p.wait("scalar", S["d_sg3"], nm_v)
                z2_v = p.op("scalar", lambda e, par=par: e.activation(
                    z2b[:], ffo[:], AF.Identity, scale=rstd[par],
                    bias=nmrc[par][:]), S["d_sg4"])
                p.wait("gpsimd", S["g_x1st"], x1st_v[NT - 1])
                p.wait("gpsimd", S["e_fin"], S["e_fin"].n)
                xsem = S["s_xld"] if par == 0 else S["s_xld1"]
                x1ld = p.dma("gpsimd", xt[par][:],
                             x1d[tt * 128:(tt + 1) * 128, :], xsem)
                p.wait("vector", S["d_sg4"], z2_v)
                p.wait("vector", xsem, x1ld)
                if g > 0 or t4 > 0:
                    p.wait("vector", S["g_outst"], S["g_outst"].n)
                o_v = p.op("vector", lambda e, par=par: e.tensor_add(
                    outb[:], z2b[:], xt[par][:]), S["e_fin"])
                p.wait("gpsimd", S["e_fin"], o_v)
                p.dma("gpsimd", out_h[tt * 128:(tt + 1) * 128, :], outb[:],
                      S["g_outst"])

        p.wait("gpsimd", S["g_outst"], 16 * NT)
        s_ec.close()
        s_es.close()

        with nc.Block() as block:
            @block.sync
            def _(eng):
                for f in p.ops["sync"]:
                    f(eng)

            @block.gpsimd
            def _(eng):
                for f in p.ops["gpsimd"]:
                    f(eng)

            @block.scalar
            def _(eng):
                for f in p.ops["scalar"]:
                    f(eng)

            @block.vector
            def _(eng):
                for f in p.ops["vector"]:
                    f(eng)

            @block.tensor
            def _(eng):
                for f in p.ops["tensor"]:
                    f(eng)

    return nc


_NC_CACHE = None


def kernel(L0_attn_x, L0_attn_Wq, L0_attn_Wk, L0_attn_Wv, L0_attn_Wp,
           L0_ffn_w_gate, L0_ffn_w_up, L0_ffn_w_down):
    global _NC_CACHE
    if _NC_CACHE is None:
        _NC_CACHE = build_nc()
    nc = _NC_CACHE
    x = np.ascontiguousarray(np.asarray(L0_attn_x, dtype=np.float32))
    ws = {
        "wq": np.ascontiguousarray(np.asarray(L0_attn_Wq, dtype=np.float32)),
        "wk": np.ascontiguousarray(np.asarray(L0_attn_Wk, dtype=np.float32)),
        "wv": np.ascontiguousarray(np.asarray(L0_attn_Wv, dtype=np.float32)),
        "wp": np.ascontiguousarray(np.asarray(L0_attn_Wp, dtype=np.float32)),
        "wg": np.ascontiguousarray(np.asarray(L0_ffn_w_gate, dtype=np.float32)),
        "wu": np.ascontiguousarray(np.asarray(L0_ffn_w_up, dtype=np.float32)),
        "wd": np.ascontiguousarray(np.asarray(L0_ffn_w_down, dtype=np.float32)),
    }
    in_maps = [dict(ws, x=x[i]) for i in range(8)]
    res = run_bass_kernel_spmd(nc, in_maps, core_ids=list(range(8)))
    return np.stack([res.results[i]["out"] for i in range(8)], axis=0)


# version 19
# speedup vs baseline: 1.0237x; 1.0123x over previous
import numpy as np
from collections import deque
from contextlib import ExitStack
import concourse.bass as bass
import concourse.mybir as mybir
from concourse.bass_utils import run_bass_kernel_spmd

F32 = mybir.dt.float32
BF16 = mybir.dt.bfloat16
AF = mybir.ActivationFunctionType
ALU = mybir.AluOpType
AX = mybir.AxisListType

T = 2048
C = 1024
H = 16
DH = 64
INNER = 4096
NT = T // 128
IC = C // 128
EPS = 1e-5
ASCALE = 0.125
RC = 12582912.0
NG = 4
GT = NT // NG
BANK = 512


class Sem:
    __slots__ = ("h", "n", "name")

    def __init__(self, h, name):
        self.h = h
        self.n = 0
        self.name = name

    def bump(self, k):
        self.n += k
        return self.n


class Plan:
    def __init__(self):
        self.ops = {"sync": [], "gpsimd": [], "scalar": [], "vector": [],
                    "tensor": []}

    def emit(self, eng, fn):
        self.ops[eng].append(fn)

    def wait(self, eng, sem, val=None):
        v = sem.n if val is None else val
        if v is None or v <= 0:
            return
        self.emit(eng, lambda e, s=sem.h, v=v: e.wait_ge(s, v))

    def dma(self, eng, out, in_, sem, **kw):
        def fn(e, o=out, i=in_, s=sem.h, kw=dict(kw)):
            e.dma_start(o, i, **kw).then_inc(s, 16)
        self.emit(eng, fn)
        return sem.bump(16)

    def op(self, eng, fn, sem=None, k=1):
        if sem is None:
            self.emit(eng, fn)
            return None

        def wrapped(e, fn=fn, s=sem.h, k=k):
            fn(e).then_inc(s, k)
        self.emit(eng, wrapped)
        return sem.bump(k)


def build_nc(debug=False):
    nc = bass.Bass("TRN2", target_bir_lowering=False)

    x_h = nc.dram_tensor("x", [T, C], F32, kind="ExternalInput").ap()
    wq_h = nc.dram_tensor("wq", [C, C], F32, kind="ExternalInput").ap()
    wk_h = nc.dram_tensor("wk", [C, C], F32, kind="ExternalInput").ap()
    wv_h = nc.dram_tensor("wv", [C, C], F32, kind="ExternalInput").ap()
    wp_h = nc.dram_tensor("wp", [C, C], F32, kind="ExternalInput").ap()
    wg_h = nc.dram_tensor("wg", [INNER, C], F32, kind="ExternalInput").ap()
    wu_h = nc.dram_tensor("wu", [INNER, C], F32, kind="ExternalInput").ap()
    wd_h = nc.dram_tensor("wd", [C, INNER], F32, kind="ExternalInput").ap()
    out_h = nc.dram_tensor("out", [T, C], F32, kind="ExternalOutput").ap()

    kd = "ExternalOutput" if debug else "Internal"
    kd2 = "ExternalOutput" if debug == 2 else kd
    wqb = nc.dram_tensor("wqb", [C, C], BF16, kind=kd).ap()
    wkb = nc.dram_tensor("wkb", [C, C], BF16, kind=kd).ap()
    wvb = nc.dram_tensor("wvb", [C, C], BF16, kind=kd).ap()
    wpb = nc.dram_tensor("wpb", [C, C], BF16, kind=kd).ap()
    x1d = nc.dram_tensor("x1d", [T, C], F32, kind=kd2).ap()
    xqd = nc.dram_tensor("xqd", [T, C], BF16, kind=kd2).ap()
    wgq = nc.dram_tensor("wgq", [INNER, C], BF16, kind=kd2).ap()
    wuq = nc.dram_tensor("wuq", [INNER, C], BF16, kind=kd2).ap()
    wdq = nc.dram_tensor("wdq", [C, INNER], BF16, kind=kd2).ap()
    if debug:
        dbgh = nc.dram_tensor("dbgh", [T, INNER], F32, kind=kd).ap()
        dbghq = nc.dram_tensor("dbghq", [T, INNER], BF16, kind=kd).ap()
        dbgffo = nc.dram_tensor("dbgffo", [T, C], F32, kind=kd).ap()
        dbgvA2 = nc.dram_tensor("dbgvA2", [T, H * 65], BF16, kind=kd).ap()
        dbgWv = nc.dram_tensor("dbgWv", [C, C], BF16, kind=kd).ap()
        dbgsc = nc.dram_tensor("dbgsc", [128, 8], F32, kind=kd).ap()
        dbgacc = nc.dram_tensor("dbgacc", [128, 192], F32, kind=kd).ap()
        dbgAT = nc.dram_tensor("dbgAT", [C, T], BF16, kind=kd).ap()
        dbgqT = nc.dram_tensor("dbgqT", [C, T], BF16, kind=kd).ap()
        dbgkT = nc.dram_tensor("dbgkT", [C, T], BF16, kind=kd).ap()
        dbgyT = nc.dram_tensor("dbgyT", [C, T], BF16, kind=kd).ap()
        dbgvA = nc.dram_tensor("dbgvA", [T, H * 65], BF16, kind=kd).ap()

    p = Plan()
    ctx = ExitStack()
    with ctx:
        EG = ctx.enter_context

        S = {}
        for name in [
            "g_cast", "g_cast1", "g_cast2", "g_cast3", "s_xld", "s_xld1",
            "s_xld2", "s_xld3",
            "g_x1st", "g_xqst", "g_xqst1", "g_xqst2", "g_xqst3", "g_xqst4",
            "g_xqst5", "g_xqst6", "g_xqst7", "wb0", "wb1", "wb2", "wb3",
            "s_wstr1", "s_xqtg1", "s_wdc1", "s_wdc2", "s_wdc3",
            "g_prepA", "g_prepB",
            "g_prepst", "g_outst", "s_wtr", "s_xqtg", "s_wstr", "s_wdc",
            "p1_st", "p1_rs", "p1_nm", "p1_ln", "p1_tr", "at_evD", "at_evA",
            "prj_mm", "prj_evA", "prj_evD", "c_smm", "c_expA", "c_msk",
            "c_avmm", "c_lcp", "c_rbf", "c_rmm", "c_rev", "c_yev",
            "d_aomm", "d_sg1", "d_sg2", "d_sg3", "d_sg4", "d_sg5", "d_sg6",
            "pp_abs", "pp_eop", "pp_fin", "pp_q1", "pp_q2", "pp_q3", "e_gmm", "e_sgA",
            "e_hD", "e_amx", "e_qA", "e_qD", "e_trm", "e_trev", "e_trev2",
            "e_dmm",
            "g_hqst", "s_hqt",
            "e_deA", "e_fin", "misc", "misc2",
        ]:
            S[name] = Sem(EG(nc.semaphore(name)), name)

        PS = EG(nc.psum_tensor("PS", [128, 8 * BANK], F32, unsafe=True))

        def bank(i, n=1):
            return PS[:, i * BANK:(i + n) * BANK]

        bank_w = [(None, 0)] * 8
        bank_r = [(None, 0)] * 8

        def pe_writes_bank(bi, n=1):
            for b in range(bi, bi + n):
                s, v = bank_r[b]
                if s is not None:
                    p.wait("tensor", s, v)

        def note_bank_write(bi, sem, val, n=1):
            for b in range(bi, bi + n):
                bank_w[b] = (sem, val)

        def reader_waits_bank(eng, bi, n=1):
            for b in range(bi, bi + n):
                s, v = bank_w[b]
                if s is not None:
                    p.wait(eng, s, v)

        def note_bank_read(bi, sem, val, n=1):
            for b in range(bi, bi + n):
                bank_r[b] = (sem, val)

        ident = EG(nc.sbuf_tensor("ident", [128, 128], BF16))
        ones_row_b = EG(nc.sbuf_tensor("ones_row_b", [1, 128], BF16))
        ones_row_f = EG(nc.sbuf_tensor("ones_row_f", [1, 128], F32))
        ones_col_f = EG(nc.sbuf_tensor("ones_col_f", [128, 1], F32))
        epsc = EG(nc.sbuf_tensor("epsc", [128, 1], F32))
        rcc = EG(nc.sbuf_tensor("rcc", [128, 1], F32))

        xt = [EG(nc.sbuf_tensor(f"xt{i}", [128, C], F32)) for i in range(2)]
        bn6 = [EG(nc.sbuf_tensor(f"bn6_{i}", [128, 12], F32)) for i in range(4)]
        sp = [EG(nc.sbuf_tensor(f"sp{i}", [128, 12], F32)) for i in range(4)]
        nmrc = [EG(nc.sbuf_tensor(f"nmrc{i}", [128, 1], F32)) for i in range(4)]
        rga_all = EG(nc.sbuf_tensor("rga_all", [128, NT], F32))
        scol_w = [EG(nc.sbuf_tensor(f"scolw{i}", [128, 1], F32)) for i in range(3)]
        rgw_col = [EG(nc.sbuf_tensor(f"rgwcol{i}", [128, 1], F32)) for i in range(3)]
        wstat = EG(nc.sbuf_tensor("wstat", [1, 4], F32))

        junkV = EG(nc.sbuf_tensor("junkV", [128, 1], F32))
        junkG = EG(nc.sbuf_tensor("junkG", [128, 1], BF16))
        junkA = EG(nc.sbuf_tensor("junkA", [128, 1], F32))
        fincol = EG(nc.sbuf_tensor("fincol", [128, 1], F32))
        maskb = EG(nc.sbuf_tensor("maskb", [128, 128], BF16))
        masknT = EG(nc.sbuf_tensor("masknT", [128, 128], BF16))
        sel4 = EG(nc.sbuf_tensor("sel4", [4, 256], BF16))
        sel4a = EG(nc.sbuf_tensor("sel4a", [4, 256], F32))
        sel4b = EG(nc.sbuf_tensor("sel4b", [4, 256], F32))

        rstd = [sp[i][:, 11:12] for i in range(4)]

        def jv():
            p.op("vector", lambda e: e.tensor_copy(junkV[:], epsc[:]))

        def ja():
            p.op("scalar", lambda e: e.activation(junkA[:], epsc[:],
                                                  AF.Identity))

        def vsp(sem):
            return p.op("vector", lambda e: e.tensor_copy(junkV[:], epsc[:]), sem)

        def asp(sem):
            return p.op("scalar", lambda e: e.activation(
                junkA[:], epsc[:], AF.Identity), sem)

        def init_consts(e):
            e.memset(ident[:], 1.0)
            e.affine_select(ident[:], ident[:], [[1, 128]], ALU.is_equal, 0.0,
                            channel_multiplier=-1)
            e.memset(ones_row_b[:], 1.0)
            e.memset(ones_row_f[:], 1.0)
            e.memset(epsc[:], EPS)
            e.memset(rcc[:], RC)

            e.memset(maskb[:], 1.0)
            e.affine_select(maskb[:], maskb[:], [[1, 128]], ALU.is_ge, 0.0,
                            channel_multiplier=-1)
            e.memset(masknT[:], -30000.0)
            e.affine_select(masknT[:], masknT[:], [[1, 128]], ALU.is_gt, 0.0,
                            channel_multiplier=-1)
            e.memset(sel4a[:], 1.0)
            e.affine_select(sel4a[:], sel4a[:], [[1, 256]], ALU.is_ge, 0.0,
                            channel_multiplier=-64)
            e.memset(sel4b[:], 1.0)
            e.affine_select(sel4b[:], sel4b[:], [[1, 256]], ALU.is_ge, 0.0,
                            base=-64, channel_multiplier=-64)
            return e.memset(ones_col_f[:], 1.0)
        p.op("gpsimd", init_consts, S["misc"])
        consts_done = S["misc"].n
        for _eng in ["sync", "scalar", "vector", "tensor"]:
            p.wait(_eng, S["misc"], consts_done)
        p.op("vector", lambda e: e.tensor_sub(sel4[:], sel4a[:], sel4b[:]))
        sel4_done = vsp(S["misc2"])
        p.wait("tensor", S["misc2"], sel4_done)

        cast_sems = [S["g_cast"], S["g_cast1"], S["g_cast2"], S["g_cast3"]]
        for ci, (csrc, cdst) in enumerate([(wq_h, wqb), (wk_h, wkb),
                                           (wv_h, wvb), (wp_h, wpb)]):
            p.dma("gpsimd", cdst, csrc, cast_sems[ci])

        def ln_stats(par, src_ap, sg_stats, sg_rstd, sg_nm, pre_waits_v=()):
            s = sp[par]
            b = bn6[par]
            m = [b[:, 1:2], b[:, 4:5], b[:, 7:8], b[:, 10:11]]
            M = [b[:, 2:3], b[:, 5:6], b[:, 8:9], b[:, 11:12]]
            for eng, sem, v in pre_waits_v:
                p.wait(eng, sem, v)
            p.op("vector", lambda e, b=b, src=src_ap: e.bn_stats(
                b[:, 0:6], src[:, 0:512]))
            p.op("vector", lambda e, b=b, src=src_ap: e.bn_stats(
                b[:, 6:12], src[:, 512:1024]))
            p.op("vector", lambda e, s=s, m=m: e.tensor_add(s[:, 0:1], m[0], m[1]))
            p.op("vector", lambda e, s=s, m=m: e.tensor_add(s[:, 1:2], m[2], m[3]))
            p.op("vector", lambda e, s=s, M=M: e.tensor_add(s[:, 4:5], M[0], M[1]))
            p.op("vector", lambda e, s=s, M=M: e.tensor_add(s[:, 5:6], M[2], M[3]))
            p.op("vector", lambda e, s=s, m=m: e.tensor_mul(s[:, 7:8], m[0], m[0]))
            p.op("vector", lambda e, s=s, m=m: e.tensor_mul(s[:, 8:9], m[1], m[1]))
            p.op("vector", lambda e, s=s: e.tensor_add(s[:, 2:3], s[:, 0:1], s[:, 1:2]))
            p.op("vector", lambda e, s=s: e.tensor_add(s[:, 6:7], s[:, 4:5], s[:, 5:6]))
            p.op("vector", lambda e, s=s: e.tensor_add(s[:, 7:8], s[:, 7:8], s[:, 8:9]))
            p.op("vector", lambda e, s=s, m=m: e.tensor_mul(s[:, 8:9], m[2], m[2]))
            p.op("vector", lambda e, s=s: e.tensor_scalar_mul(s[:, 3:4], s[:, 2:3], 0.25))
            p.op("vector", lambda e, s=s: e.tensor_add(s[:, 7:8], s[:, 7:8], s[:, 8:9]))
            p.op("vector", lambda e, s=s, m=m: e.tensor_mul(s[:, 8:9], m[3], m[3]))
            p.op("vector", lambda e, s=s: e.tensor_mul(s[:, 1:2], s[:, 3:4], s[:, 3:4]))
            p.op("vector", lambda e, s=s: e.tensor_add(s[:, 7:8], s[:, 7:8], s[:, 8:9]))
            jv()
            p.op("vector", lambda e, s=s: e.tensor_scalar_mul(s[:, 0:1], s[:, 7:8], 0.25))
            jv()
            p.op("vector", lambda e, s=s: e.scalar_tensor_tensor(
                s[:, 9:10], s[:, 6:7], 1.0 / C, s[:, 0:1], ALU.mult, ALU.add))
            jv()
            p.op("vector", lambda e, s=s: e.tensor_sub(
                s[:, 10:11], s[:, 9:10], s[:, 1:2]))
            st_v = vsp(sg_stats)
            p.wait("scalar", sg_stats, st_v)
            p.op("scalar", lambda e, s=s: e.activation(s[:, 9:10], s[:, 10:11],
                                                       AF.Ln, bias=epsc[:]))
            ja()
            p.op("scalar", lambda e, s=s, par=par: e.activation(
                rstd[par], s[:, 9:10], AF.Exp, scale=-0.5))
            rs_v = asp(sg_rstd)
            p.wait("vector", sg_rstd, rs_v)
            p.op("vector", lambda e, s=s, par=par: e.tensor_scalar(
                nmrc[par][:], s[:, 3:4], rstd[par], -1.0, ALU.mult, ALU.mult))
            nm_v = vsp(sg_nm)
            return rs_v, nm_v

        prep_srcs = []
        for mi, wh in [(0, wg_h), (1, wu_h)]:
            for ch in range(32):
                for hf in range(2):
                    prep_srcs.append(
                        (mi, wh[ch * 128:(ch + 1) * 128, hf * 512:(hf + 1) * 512],
                         None))
        for rr in range(8):
            for cpc in range(8):
                prep_srcs.append(
                    (2, wd_h[rr * 128:(rr + 1) * 128, cpc * 512:(cpc + 1) * 512],
                     (rr, cpc)))

        prep_abs_v = [0] * 192
        prep_q3_v = [0] * 192
        prep_fin_v = [0, 0, 0]
        prep_state = {"st": 0}

        s_prep = ExitStack()
        PE_ = s_prep.enter_context
        wbuf = [PE_(nc.sbuf_tensor(f"wbuf{i}", [128, 512], F32)) for i in range(4)]
        pstage = [PE_(nc.sbuf_tensor(f"pstage{i}", [128, 512], F32)) for i in range(2)]
        pout = [PE_(nc.sbuf_tensor(f"pout{i}", [128, 512], BF16)) for i in range(2)]
        acc_strip = PE_(nc.sbuf_tensor("acc_strip", [128, 192], F32))

        def prep1_thunk(idx):
            def emit():
                par = idx % 4
                wsems = [S["wb0"], S["wb1"], S["wb2"], S["wb3"]]
                if idx == 0:
                    for j in range(4):
                        p.dma("gpsimd", wbuf[j][:], prep_srcs[j][1], wsems[j])
                elif idx + 3 < 192:
                    p.wait("gpsimd", S["pp_abs"], idx)
                    p.dma("gpsimd", wbuf[(idx + 3) % 4][:],
                          prep_srcs[idx + 3][1], wsems[(idx + 3) % 4])
                p.wait("vector", wsems[par], 16 * (idx // 4 + 1))
                prep_abs_v[idx] = p.op("vector", lambda e, par=par, idx=idx: e.tensor_reduce(
                    acc_strip[:, idx:idx + 1], wbuf[par][:], AX.X, ALU.add,
                    apply_absolute_value=True), S["pp_abs"])
            return emit

        def prep2_dst(idx):
            mi, _, wdpos = prep_srcs[idx]
            if mi == 0:
                ch, hf = (idx % 64) // 2, idx % 2
                return wgq[ch * 128:(ch + 1) * 128, hf * 512:(hf + 1) * 512]
            if mi == 1:
                j = idx - 64
                ch, hf = j // 2, j % 2
                return wuq[ch * 128:(ch + 1) * 128, hf * 512:(hf + 1) * 512]
            rr, cpc = wdpos
            return wdq[rr * 128:(rr + 1) * 128, cpc * 512:(cpc + 1) * 512]

        def prep2_thunk(idx):
            def emit():
                mi = prep_srcs[idx][0]
                par = idx % 4
                spar = idx % 2
                wsems = [S["wb0"], S["wb1"], S["wb2"], S["wb3"]]
                if idx == 0:
                    p.wait("gpsimd", S["pp_abs"], 192)
                    for j in range(4):
                        p.dma("gpsimd", wbuf[j][:], prep_srcs[j][1], wsems[j])
                elif idx + 3 < 192:
                    p.wait("gpsimd", S["pp_q1"], idx)
                    p.dma("gpsimd", wbuf[(idx + 3) % 4][:],
                          prep_srcs[idx + 3][1], wsems[(idx + 3) % 4])
                p.wait("scalar", wsems[par], 16 * (48 + idx // 4 + 1))
                p.wait("scalar", S["pp_fin"], prep_fin_v[mi])
                if idx >= 2:
                    p.wait("scalar", S["pp_q3"], idx - 1)
                q1 = p.op("scalar", lambda e, par=par, mi=mi, spar=spar: e.activation(
                    pstage[spar][:], wbuf[par][:], AF.Identity,
                    scale=scol_w[mi][:], bias=rcc[:]), S["pp_q1"])
                p.wait("vector", S["pp_q1"], q1)
                p.op("vector", lambda e, spar=spar: e.tensor_scalar(
                    pstage[spar][:], pstage[spar][:], RC + 1.0, RC - 1.0,
                    ALU.min, ALU.max), S["pp_q2"])
                if idx >= 2:
                    p.wait("vector", S["g_prepst"], 16 * (idx - 1))
                q3 = p.op("vector", lambda e, spar=spar: e.tensor_scalar(
                    pout[spar][:], pstage[spar][:], -RC, None, ALU.add),
                    S["pp_q3"])
                prep_q3_v[idx] = q3
                p.wait("gpsimd", S["pp_q3"], q3)
                prep_state["st"] = p.dma("gpsimd", prep2_dst(idx),
                                         pout[spar][:], S["g_prepst"])
            return emit

        def prep_fin_thunk(mi):
            def emit():
                lo, hi = mi * 64, (mi + 1) * 64
                bi = 6
                p.wait("vector", S["pp_abs"], prep_abs_v[hi - 1])
                p.op("vector", lambda e: e.reduce_sum(
                    fincol[:], acc_strip[:, lo:hi], AX.X))
                jv()
                cs_v = vsp(S["pp_fin"])
                p.wait("tensor", S["pp_fin"], cs_v)
                pe_writes_bank(bi)
                mm_v = p.op("tensor", lambda e, bi=bi: e.matmul(
                    bank(bi)[0:1, 0:1], fincol[:], ones_col_f[:],
                    start=True, stop=True), S["pp_fin"])
                note_bank_write(bi, S["pp_fin"], mm_v)
                reader_waits_bank("vector", bi)
                jv()
                jv()
                p.op("vector", lambda e, bi=bi: e.tensor_copy(wstat[:, 0:1],
                                                              bank(bi)[0:1, 0:1]))
                jv()
                p.op("vector", lambda e: e.tensor_scalar(
                    wstat[:, 1:2], wstat[:, 0:1], 1.0 / (INNER * C), 1e-5,
                    ALU.mult, ALU.max))
                jv()
                p.op("vector", lambda e: e.reciprocal(
                    wstat[:, 2:3], wstat[:, 1:2]))
                rc_v = vsp(S["pp_fin"])
                note_bank_read(bi, S["pp_fin"], rc_v)
                p.wait("tensor", S["pp_fin"], rc_v)
                pe_writes_bank(bi)
                p.op("tensor", lambda e, bi=bi: e.matmul(
                    bank(bi)[:, 0:1], ones_row_f[:], wstat[:, 2:3],
                    start=True, stop=True), S["pp_fin"])
                mm3 = p.op("tensor", lambda e, bi=bi: e.matmul(
                    bank(bi)[:, 1:2], ones_row_f[:], wstat[:, 1:2],
                    start=True, stop=True), S["pp_fin"])
                note_bank_write(bi, S["pp_fin"], mm3)
                reader_waits_bank("vector", bi)
                jv()
                jv()
                p.op("vector", lambda e, mi=mi, bi=bi: e.tensor_copy(
                    scol_w[mi][:], bank(bi)[:, 0:1]))
                p.op("vector", lambda e, mi=mi, bi=bi: e.tensor_copy(
                    rgw_col[mi][:], bank(bi)[:, 1:2]))
                ev_v = vsp(S["pp_fin"])
                note_bank_read(bi, S["pp_fin"], ev_v)
                prep_fin_v[mi] = S["pp_fin"].n
            return emit

        def prep2_dst(idx):
            mi, _, wdpos = prep_srcs[idx]
            if mi == 0:
                ch, hf = (idx % 64) // 2, idx % 2
                return wgq[ch * 128:(ch + 1) * 128, hf * 512:(hf + 1) * 512]
            if mi == 1:
                j = idx - 64
                ch, hf = j // 2, j % 2
                return wuq[ch * 128:(ch + 1) * 128, hf * 512:(hf + 1) * 512]
            rr, cpc = wdpos
            return wdq[rr * 128:(rr + 1) * 128, cpc * 512:(cpc + 1) * 512]

        def prep2_all():
            H0, H1 = slice(0, 256), slice(256, 512)
            for idx in range(192):
                par = idx % 2
                mi = prep_srcs[idx][0]
                if idx >= 2:
                    p.wait("gpsimd", S["pp_q3"], prep_q3_v[idx - 2])
                ld = p.dma("gpsimd", wbuf[par][:], prep_srcs[idx][1],
                           S["g_prepB"])
                p.wait("gpsimd", S["g_prepB"], ld)
                p.wait("gpsimd", S["pp_fin"], prep_fin_v[mi])
                for hs in (H0, H1):
                    p.op("gpsimd", lambda e, par=par, mi=mi, hs=hs: e.tensor_tensor(
                        pstage[par][:, hs], wbuf[par][:, hs], scol256[mi][:],
                        ALU.mult))
                for hs in (H0, H1):
                    p.op("gpsimd", lambda e, par=par, hs=hs: e.tensor_tensor(
                        pst2[par][:, hs], pstage[par][:, hs], rc256[:],
                        ALU.add))
                for hs in (H0, H1):
                    p.op("gpsimd", lambda e, par=par, hs=hs: e.tensor_tensor(
                        pstage[par][:, hs], pst2[par][:, hs], rcp256[:],
                        ALU.min))
                for hs in (H0, H1):
                    p.op("gpsimd", lambda e, par=par, hs=hs: e.tensor_tensor(
                        pst2[par][:, hs], pstage[par][:, hs], rcm256[:],
                        ALU.max))
                if idx >= 2:
                    p.wait("gpsimd", S["g_prepst"], 16 * (idx - 1))
                q3 = None
                for hs in (H0, H1):
                    q3 = p.op("gpsimd", lambda e, par=par, hs=hs: e.tensor_tensor(
                        pout[par][:, hs], pst2[par][:, hs], nrc256[:],
                        ALU.add), S["pp_q3"])
                prep_q3_v[idx] = q3
                prep_state["st"] = p.dma("gpsimd", prep2_dst(idx),
                                         pout[par][:], S["g_prepst"])
            return

        prep_queue = deque()
        for i in range(64):
            prep_queue.append(prep1_thunk(i))
        prep_queue.append(prep_fin_thunk(0))
        for i in range(64, 128):
            prep_queue.append(prep1_thunk(i))
        prep_queue.append(prep_fin_thunk(1))
        for i in range(128, 192):
            prep_queue.append(prep1_thunk(i))
        prep_queue.append(prep_fin_thunk(2))
        for i in range(192):
            prep_queue.append(prep2_thunk(i))

        def drip(n):
            for _ in range(n):
                if prep_queue:
                    prep_queue.popleft()()

        s_qkv = ExitStack()
        Q_ = s_qkv.enter_context
        qT = [Q_(nc.sbuf_tensor(f"qT{i}", [128, T], BF16)) for i in range(IC)]
        kT = [Q_(nc.sbuf_tensor(f"kT{i}", [128, T], BF16)) for i in range(IC)]
        vA = [Q_(nc.sbuf_tensor(f"vA{i}", [128, H * 65], BF16)) for i in range(NT)]

        s_a = ExitStack()
        A_ = s_a.enter_context
        AT = [A_(nc.sbuf_tensor(f"AT{i}", [128, T], BF16)) for i in range(IC)]
        WTa = [[A_(nc.sbuf_tensor(f"WTa{j}_{i}", [128, C], BF16))
                for i in range(IC)] for j in range(2)]
        s_lnt = ExitStack()
        lnt = [s_lnt.enter_context(nc.sbuf_tensor(f"lnt{i}", [128, C], BF16))
               for i in range(2)]

        for i in range(NT):
            p.op("gpsimd", lambda e, t=vA[i]: e.memset(t[:], 1.0), S["misc"])
        vones_done = S["misc"].n

        p1_ln_v = [0] * NT
        for tt in range(NT):
            par = tt % 2
            if tt >= 2:
                p.wait("sync", S["p1_ln"], p1_ln_v[tt - 2])
            xsem = S["s_xld"] if par == 0 else S["s_xld1"]
            xld = p.dma("sync", xt[par][:], x_h[tt * 128:(tt + 1) * 128, :],
                        xsem)
            rs_v, nm_v = ln_stats(par, xt[par][:], S["p1_st"], S["p1_rs"],
                                  S["p1_nm"],
                                  pre_waits_v=[("vector", xsem, xld)])
            p.wait("scalar", S["p1_nm"], nm_v)
            if tt >= 2:
                p.wait("scalar", S["p1_tr"], (tt - 1) * IC)
            ln_v = p.op("scalar", lambda e, par=par: e.activation(
                lnt[par][:], xt[par][:], AF.Identity, bias=nmrc[par][:],
                scale=rstd[par]), S["p1_ln"])
            p1_ln_v[tt] = ln_v
            p.wait("tensor", S["p1_ln"], ln_v)
            if tt == 0:
                p.wait("tensor", S["misc"], consts_done)
            for ic in range(IC):
                bi = ic % 4
                pe_writes_bank(bi)
                mm_v = p.op("tensor", lambda e, par=par, ic=ic, bi=bi: e.matmul(
                    bank(bi)[:, 0:128], lnt[par][:, ic * 128:(ic + 1) * 128],
                    ident[:], start=True, stop=True), S["p1_tr"])
                note_bank_write(bi, S["p1_tr"], mm_v)
                eng, sem = (("vector", S["at_evD"]) if ic % 2 == 0
                            else ("scalar", S["at_evA"]))
                p.wait(eng, S["p1_tr"], mm_v + 1)
                reader_waits_bank(eng, bi)
                if eng == "vector":
                    ev_v = p.op(eng, lambda e, ic=ic, tt=tt, bi=bi: e.tensor_copy(
                        AT[ic][:, tt * 128:(tt + 1) * 128], bank(bi)[:, 0:128]),
                        sem)
                else:
                    ev_v = p.op(eng, lambda e, ic=ic, tt=tt, bi=bi: e.activation(
                        AT[ic][:, tt * 128:(tt + 1) * 128], bank(bi)[:, 0:128],
                        AF.Identity), sem)
                note_bank_read(bi, sem, ev_v)
            pe_writes_bank(6)
            dmy = p.op("tensor", lambda e: e.matmul(
                bank(6)[:, 0:128], ident[:], ident[:], start=True, stop=True),
                S["p1_tr"])
            note_bank_write(6, S["p1_tr"], dmy)
        vsp(S["at_evD"])
        asp(S["at_evA"])
        at_evD_done = S["at_evD"].n
        at_evA_done = S["at_evA"].n
        s_lnt.close()

        def load_wta(j, wb_dram, cast_idx, war_v):
            p.wait("sync", cast_sems[cast_idx], 16)
            if war_v:
                p.wait("sync", S["prj_mm"], war_v)
            last = 0
            for ic in range(IC):
                last = p.dma("sync", WTa[j][ic][:],
                             wb_dram[0:C, ic * 128:(ic + 1) * 128],
                             S["s_wtr"], transpose=True)
            return last

        def proj_fm(j, dstT, wtr_v, scale, evac_eng, evac_sem):
            p.wait("tensor", S["s_wtr"], wtr_v)
            for ts in range(4):
                p.wait("tensor", S["at_evD"], min(4 * (4 * ts + 4) + 1, at_evD_done))
                p.wait("tensor", S["at_evA"], min(4 * (4 * ts + 4) + 1, at_evA_done))
                for oc in range(IC):
                    bi = (oc * 4 + ts) % 4
                    pe_writes_bank(bi)
                    mm_v = None
                    for ic in range(IC):
                        r = p.op("tensor", lambda e, j=j, ic=ic, oc=oc, ts=ts, bi=bi: e.matmul(
                            bank(bi), WTa[j][ic][:, oc * 128:(oc + 1) * 128],
                            AT[ic][:, ts * 512:(ts + 1) * 512],
                            start=(ic == 0), stop=(ic == IC - 1)),
                            S["prj_mm"] if ic == IC - 1 else None)
                        if r is not None:
                            mm_v = r
                    note_bank_write(bi, S["prj_mm"], mm_v)
                    reader_waits_bank(evac_eng, bi)
                    if evac_eng == "scalar":
                        ev_v = p.op(evac_eng, lambda e, oc=oc, ts=ts, bi=bi, d=dstT, sc=scale: e.activation(
                            d[oc][:, ts * 512:(ts + 1) * 512], bank(bi),
                            AF.Identity, scale=sc), evac_sem)
                    else:
                        ev_v = p.op(evac_eng, lambda e, oc=oc, ts=ts, bi=bi, d=dstT: e.tensor_copy(
                            d[oc][:, ts * 512:(ts + 1) * 512], bank(bi)),
                            evac_sem)
                    note_bank_read(bi, evac_sem, ev_v)
                    drip(3)

        wtr_q = load_wta(0, wqb, 0, 0)
        wtr_k = load_wta(1, wkb, 1, 0)
        proj_fm(0, qT, wtr_q, ASCALE, "scalar", S["prj_evA"])
        proj_fm(1, kT, wtr_k, 1.0, "vector", S["prj_evD"])
        wtr_v = load_wta(0, wvb, 2, S["prj_mm"].n - 32)

        p.wait("tensor", S["s_wtr"], wtr_v)
        p.wait("vector", S["misc"], vones_done)
        p.wait("scalar", S["misc"], vones_done)
        for tt in range(NT):
            for oc2 in range(2):
                bi = 4 + (tt * 2 + oc2) % 2
                pe_writes_bank(bi)
                mm_v = None
                for ic in range(IC):
                    r = p.op("tensor", lambda e, ic=ic, tt=tt, oc2=oc2, bi=bi: e.matmul(
                        bank(bi), AT[ic][:, tt * 128:(tt + 1) * 128],
                        WTa[0][ic][:, oc2 * 512:(oc2 + 1) * 512],
                        start=(ic == 0), stop=(ic == IC - 1)),
                        S["prj_mm"] if ic == IC - 1 else None)
                    if r is not None:
                        mm_v = r
                note_bank_write(bi, S["prj_mm"], mm_v)
                eng, sem = (("vector", S["prj_evD"]) if oc2 == 0
                            else ("scalar", S["prj_evA"]))
                reader_waits_bank(eng, bi)
                ev_v = None
                for hh in range(8):
                    h = oc2 * 8 + hh
                    if eng == "vector":
                        ev_v = p.op(eng, lambda e, tt=tt, h=h, hh=hh, bi=bi: e.tensor_copy(
                            vA[tt][:, h * 65:h * 65 + 64],
                            bank(bi)[:, hh * 64:(hh + 1) * 64]), sem)
                    else:
                        ev_v = p.op(eng, lambda e, tt=tt, h=h, hh=hh, bi=bi: e.activation(
                            vA[tt][:, h * 65:h * 65 + 64],
                            bank(bi)[:, hh * 64:(hh + 1) * 64], AF.Identity), sem)
                note_bank_read(bi, sem, ev_v)
                drip(3)
        vsp(S["prj_evD"])
        asp(S["prj_evA"])
        v_doneD = S["prj_evD"].n
        v_doneA = S["prj_evA"].n

        if debug == 1:
            p.wait("gpsimd", S["at_evD"], at_evD_done)
            p.wait("gpsimd", S["at_evA"], at_evA_done)
            for ic in range(IC):
                p.dma("gpsimd", dbgAT[ic * 128:(ic + 1) * 128, :], AT[ic][:],
                      S["misc2"])
            p.wait("gpsimd", S["prj_evD"], v_doneD)
            p.wait("gpsimd", S["prj_evA"], v_doneA)
            for tt in range(NT):
                p.dma("gpsimd", dbgvA2[tt * 128:(tt + 1) * 128, :], vA[tt][:],
                      S["misc2"])
            for ic in range(IC):
                p.dma("gpsimd", dbgWv[ic * 128:(ic + 1) * 128, :],
                      WTa[0][ic][:], S["misc2"])
            p.wait("gpsimd", S["misc2"], S["misc2"].n)

        s_a.close()

        s_y = ExitStack()
        Y_ = s_y.enter_context
        yT = [Y_(nc.sbuf_tensor(f"yT{i}", [128, T], BF16, side="right"))
              for i in range(IC)]
        WTb = [Y_(nc.sbuf_tensor(f"WTb{i}", [128, C], BF16, side="right"))
               for i in range(IC)]

        p.wait("sync", S["g_cast3"], 16)
        p.wait("sync", S["prj_mm"], S["prj_mm"].n)
        wtr_p = 0
        for ic in range(IC):
            wtr_p = p.dma("sync", WTb[ic][:], wpb[0:C, ic * 128:(ic + 1) * 128],
                          S["s_wtr"], transpose=True)

        s_c = ExitStack()
        C_ = s_c.enter_context
        ptile = [C_(nc.sbuf_tensor(f"pt{i}", [128, BANK], BF16)) for i in range(4)]
        yu = [C_(nc.sbuf_tensor(f"yu{i}", [65, BANK], F32)) for i in range(4)]
        lall = C_(nc.sbuf_tensor("lall", [4, BANK], F32))
        rall = C_(nc.sbuf_tensor("rall", [4, BANK], F32))
        rallb = C_(nc.sbuf_tensor("rallb", [4, BANK], BF16))

        p.wait("tensor", S["prj_evA"], v_doneA)
        p.wait("tensor", S["prj_evD"], v_doneD)

        SBK = [0, 1, 6, 7]
        YBK = [2, 3, 4, 5]
        yev_done = [0, 0, 0, 0]
        av_prev = [None] * 4
        rb_prev = 0
        pending = []

        def emit_tail(st):
            si = st["si"]
            sbk = SBK[si]
            pe_writes_bank(sbk)
            if st.get("heat"):
                p.wait("tensor", S["c_lcp"], st["ev"])
                p.op("tensor", lambda e, sbk=sbk: e.matmul(
                    bank(sbk)[0:1, 0:1], ones_row_b[:, 0:1],
                    ones_row_b[:, 0:1], start=True, stop=True))
                p.wait("tensor", S["c_rev"], st["ld"])
                p.op("tensor", lambda e, sbk=sbk: e.matmul(
                    bank(sbk)[0:1, 0:1], ones_row_b[:, 0:1],
                    ones_row_b[:, 0:1], start=True, stop=True))
            p.wait("tensor", S["c_rbf"], st["rb"])
            rmm = p.op("tensor", lambda e, si=si, sbk=sbk: e.matmul(
                bank(sbk)[0:64, :], sel4[:, si * 64:(si + 1) * 64],
                rallb[:], start=True, stop=True), S["c_rmm"])
            note_bank_write(sbk, S["c_rmm"], rmm)
            reader_waits_bank("vector", sbk)
            yv = p.op("vector", lambda e, st=st, si=si, sbk=sbk: e.tensor_tensor(
                yT[st["hpt"]][st["prow"]:st["prow"] + 64,
                              st["tq0"]:st["tq0"] + 512],
                yu[si][0:64, :], bank(sbk)[0:64, 0:512], ALU.mult),
                S["c_yev"])
            note_bank_read(sbk, S["c_yev"], yv)
            yev_done[si] = yv

        for j in range(4):
            for tqs in range(4):
                tq0 = tqs * 512
                nk = 4 * tqs + 4
                avv = [[None] * nk for _ in range(4)]
                exv = [[None] * nk for _ in range(4)]

                for tkc in range(nk):
                    off = max(0, 128 * tkc - tq0)
                    npr = 512 - off
                    diag = tkc >= 4 * tqs
                    for si in range(4):
                        h = 4 * j + si
                        hpt, pr = h // 2, (h % 2) * 64
                        sbi = SBK[si]
                        pe_writes_bank(sbi)
                        if diag:
                            p.op("tensor", lambda e, sbi=sbi: e.matmul(
                                bank(sbi)[:, 0:128], masknT[:], ident[:],
                                start=True, stop=False))
                        smm = p.op("tensor", lambda e, tkc=tkc, off=off, npr=npr, sbi=sbi, diag=diag, hpt=hpt, pr=pr, tq0=tq0: e.matmul(
                            bank(sbi)[:, 0:npr],
                            kT[hpt][pr:pr + 64, tkc * 128:(tkc + 1) * 128],
                            qT[hpt][pr:pr + 64, tq0 + off:tq0 + 512],
                            start=not diag, stop=True), S["c_smm"])
                        note_bank_write(sbi, S["c_smm"], smm)
                        reader_waits_bank("scalar", sbi)
                        if tkc >= 1 and avv[si][tkc - 1] is not None:
                            p.wait("scalar", S["c_avmm"], avv[si][tkc - 1])
                        elif tkc == 0 and av_prev[si] is not None:
                            p.wait("scalar", S["c_avmm"], av_prev[si])
                        if npr <= 128:
                            ja()
                        ex = p.op("scalar", lambda e, si=si, npr=npr, sbi=sbi: e.activation(
                            ptile[si][:, 0:npr], bank(sbi)[:, 0:npr], AF.Exp),
                            S["c_expA"])
                        note_bank_read(sbi, S["c_expA"], ex)
                        exv[si][tkc] = ex
                    for si in range(4):
                        h = 4 * j + si
                        if tkc == 0:
                            pe_writes_bank(YBK[si])
                        p.wait("tensor", S["c_expA"], exv[si][tkc])
                        v = p.op("tensor", lambda e, si=si, tkc=tkc, off=off, npr=npr, h=h, ybi=YBK[si], first=(tkc == 0), last=(tkc == nk - 1): e.matmul(
                            bank(ybi)[0:65, off:off + npr],
                            vA[tkc][:, h * 65:(h + 1) * 65],
                            ptile[si][:, 0:npr],
                            start=first, stop=last), S["c_avmm"])
                        avv[si][tkc] = v
                    if tkc == 3 and pending:
                        for st in pending:
                            emit_tail(st)
                        pending = []

                for si in range(4):
                    eng = "scalar" if si % 2 == 0 else "vector"
                    av_prev[si] = avv[si][nk - 1]
                    note_bank_write(YBK[si], S["c_avmm"], avv[si][nk - 1])
                    reader_waits_bank(eng, YBK[si])
                    if yev_done[si]:
                        p.wait(eng, S["c_yev"], yev_done[si])
                    if si % 2 == 0:
                        ev = p.op("scalar", lambda e, si=si, b=YBK[si]: e.activation(
                            yu[si][:, :], bank(b)[0:65, 0:512], AF.Identity),
                            S["c_lcp"])
                    else:
                        ev = p.op("vector", lambda e, si=si, b=YBK[si]: e.tensor_copy(
                            yu[si][:, :], bank(b)[0:65, 0:512]), S["c_lcp"])
                    note_bank_read(YBK[si], S["c_lcp"], ev)
                asp(S["c_lcp"])
                ev_sp = vsp(S["c_lcp"])
                p.wait("gpsimd", S["c_lcp"], ev_sp)
                if rb_prev:
                    p.wait("gpsimd", S["c_rbf"], rb_prev)
                for si in range(4):
                    ldma = p.dma("gpsimd", lall[si:si + 1, :],
                                 yu[si][64:65, :], S["c_rev"])
                p.wait("vector", S["c_rev"], ldma)
                p.op("vector", lambda e: e.reciprocal(rall[:], lall[:]))
                jv()
                p.op("vector", lambda e: e.tensor_copy(rallb[:], rall[:]))
                rb = vsp(S["c_rbf"])
                rb_prev = rb
                pending = []
                for si in range(4):
                    h = 4 * j + si
                    pending.append(dict(si=si, hpt=h // 2, prow=(h % 2) * 64,
                                        tq0=tq0, rb=rb, ev=ev_sp, ld=ldma,
                                        heat=(si == 0)))
                drip(18)
        for st in pending:
            emit_tail(st)
        while prep_queue:
            drip(1)
        vsp(S["c_yev"])
        c_y_done = S["c_yev"].n
        prepst_done = prep_state["st"]

        s_c.close()
        s_qkv.close()
        s_prep.close()


        s_d = ExitStack()
        D_ = s_d.enter_context
        zt = [D_(nc.sbuf_tensor(f"zt{i}", [128, C], F32, side="right"))
              for i in range(2)]
        x1t = [D_(nc.sbuf_tensor(f"x1t{i}", [128, C], F32, side="right"))
               for i in range(4)]
        ln2t = [D_(nc.sbuf_tensor(f"ln2t{i}", [128, C], F32, side="right"))
                for i in range(2)]
        qstage = [D_(nc.sbuf_tensor(f"qstage{i}", [128, C], F32, side="right"))
                  for i in range(2)]
        xqt = [D_(nc.sbuf_tensor(f"xqt{i}", [128, C], BF16, side="right"))
               for i in range(2)]

        s_es = ExitStack()
        ES_ = s_es.enter_context
        xqTg = [[ES_(nc.sbuf_tensor(f"xqTg{j}_{i}", [128, 512], BF16))
                 for i in range(IC)] for j in range(2)]
        wgTs = [[ES_(nc.sbuf_tensor(f"wgTs{j}_{i}", [128, 1024], BF16))
                 for i in range(IC)] for j in range(2)]
        wuTs = [[ES_(nc.sbuf_tensor(f"wuTs{j}_{i}", [128, 1024], BF16))
                 for i in range(IC)] for j in range(2)]
        wdTc = [ES_(nc.sbuf_tensor(f"wdTc{j}", [128, C], BF16))
                for j in range(4)]

        if debug == 1:
            p.wait("gpsimd", S["c_yev"], c_y_done)
            for ic in range(IC):
                p.dma("gpsimd", dbgyT[ic * 128:(ic + 1) * 128, :], yT[ic][:],
                      S["misc2"])
            p.wait("gpsimd", S["misc2"], S["misc2"].n)

        p.wait("tensor", S["c_yev"], c_y_done)
        p.wait("tensor", S["s_wtr"], wtr_p)
        p.wait("gpsimd", S["p1_ln"], S["p1_ln"].n)
        x1st_v = [0] * NT
        xqst_v = [0] * NT
        xq_sems = [S["g_xqst"], S["g_xqst1"], S["g_xqst2"], S["g_xqst3"],
                   S["g_xqst4"], S["g_xqst5"], S["g_xqst6"], S["g_xqst7"]]
        xsems = [S["s_xld"], S["s_xld1"], S["s_xld2"], S["s_xld3"]]

        def emitD_A(tt):
            par = tt % 4
            aob = 4 * (tt % 2)
            pe_writes_bank(aob, 2)
            mm_v = None
            for oc2 in range(2):
                for ic in range(IC):
                    r = p.op("tensor", lambda e, ic=ic, tt=tt, oc2=oc2, aob=aob: e.matmul(
                        bank(aob + oc2), yT[ic][:, tt * 128:(tt + 1) * 128],
                        WTb[ic][:, oc2 * 512:(oc2 + 1) * 512],
                        start=(ic == 0), stop=(ic == IC - 1)),
                        S["d_aomm"] if ic == IC - 1 else None)
                    if r is not None:
                        mm_v = r
            note_bank_write(aob, S["d_aomm"], mm_v, 2)
            xp = tt % 2
            if tt >= 2:
                p.wait("gpsimd", S["d_sg5"], d_x1v[tt - 2])
            xld = p.dma("gpsimd", xt[xp][:], x_h[tt * 128:(tt + 1) * 128, :],
                        xsems[xp])
            ao_view = PS[:, aob * BANK:(aob + 2) * BANK]
            reader_waits_bank("vector", aob, 2)
            rs_v, nm_v = ln_stats(par, ao_view, S["d_sg1"], S["d_sg2"],
                                  S["d_sg3"])
            p.wait("scalar", S["d_sg3"], nm_v)
            if tt >= 2:
                p.wait("scalar", S["d_sg5"], d_x1v[tt - 2])
            z_v = p.op("scalar", lambda e, par=par, xp=xp, ao=ao_view: e.activation(
                zt[xp][:], ao, AF.Identity, scale=rstd[par], bias=nmrc[par][:]),
                S["d_sg4"])
            note_bank_read(aob, S["d_sg4"], z_v, 2)
            p.wait("vector", S["d_sg4"], z_v)
            p.wait("vector", xsems[xp], xld)
            if tt >= 4:
                p.wait("vector", S["g_x1st"], x1st_v[tt - 4])
            x1_v = p.op("vector", lambda e, par=par, xp=xp: e.tensor_add(
                x1t[par][:], zt[xp][:], xt[xp][:]), S["d_sg5"])
            p.wait("gpsimd", S["d_sg5"], x1_v)
            x1st_v[tt] = p.dma("gpsimd", x1d[tt * 128:(tt + 1) * 128, :],
                               x1t[par][:], S["g_x1st"])
            return x1_v

        def emitD_B(tt, x1_v):
            par = tt % 4
            xp = tt % 2
            rs2_v, nm2_v = ln_stats(par, x1t[par][:], S["d_sg1"], S["d_sg2"],
                                    S["d_sg3"],
                                    pre_waits_v=[("vector", S["d_sg5"], x1_v)])
            p.wait("scalar", S["d_sg3"], nm2_v)
            l2_v = p.op("scalar", lambda e, par=par, xp=xp: e.activation(
                ln2t[xp][:], x1t[par][:], AF.Identity, scale=rstd[par],
                bias=nmrc[par][:]), S["d_sg4"])
            p.wait("vector", S["d_sg4"], l2_v)
            p.op("vector", lambda e, par=par, xp=xp: e.tensor_reduce(
                sp[par][:, 0:1], ln2t[xp][:], AX.X, ALU.max,
                apply_absolute_value=True))
            jv()
            p.op("vector", lambda e, par=par: e.tensor_scalar_max(
                sp[par][:, 1:2], sp[par][:, 0:1], 1e-5))
            jv()
            p.op("vector", lambda e, par=par: e.reciprocal(
                sp[par][:, 2:3], sp[par][:, 1:2]))
            p.op("vector", lambda e, par=par, tt=tt: e.tensor_scalar_mul(
                rga_all[:, tt:tt + 1], sp[par][:, 1:2], 1.0 / 127.0))
            p.op("vector", lambda e, par=par: e.tensor_scalar_mul(
                sp[par][:, 3:4], sp[par][:, 2:3], 127.0))
            qs_v = vsp(S["d_sg6"])
            p.wait("scalar", S["d_sg6"], qs_v)
            q1_v = p.op("scalar", lambda e, par=par, xp=xp: e.activation(
                qstage[xp][:], ln2t[xp][:], AF.Identity, scale=sp[par][:, 3:4],
                bias=rcc[:]), S["d_sg4"])
            p.wait("vector", S["d_sg4"], q1_v)
            if tt >= 2:
                p.wait("vector", S["g_xqst"], xqst_v[tt - 2])
            q2_v = p.op("vector", lambda e, xp=xp: e.tensor_scalar(
                xqt[xp][:], qstage[xp][:], -RC, None, ALU.add), S["d_sg6"])
            p.wait("gpsimd", S["d_sg6"], q2_v)
            xqst_v[tt] = p.dma("gpsimd", xqd[tt * 128:(tt + 1) * 128, :],
                               xqt[xp][:], xq_sems[tt % 8])

        d_x1v = [0] * NT
        for tt in range(NT):
            d_x1v[tt] = emitD_A(tt)
            if tt >= 2:
                emitD_B(tt - 2, d_x1v[tt - 2])
        emitD_B(NT - 2, d_x1v[NT - 2])
        emitD_B(NT - 1, d_x1v[NT - 1])
        d_q2_done = S["d_sg6"].n

        p.wait("sync", S["c_yev"], c_y_done)
        p.wait("sync", S["g_prepst"], prepst_done)
        xqtg_v = [0] * NG
        wstr_v = [[0] * 8 for _ in range(NG)]

        def gA_count(g, sl):
            return g * 64 + (sl + 1) * 8

        def gA_count2(g, sp2):
            return g * 64 + (sp2 + 1) * 16

        def emit_stream_slice(g, sp2):
            wssem = S["s_wstr"] if sp2 % 2 == 0 else S["s_wstr1"]
            if sp2 >= 2:
                p.wait("sync", S["e_gmm"], gA_count2(g, sp2 - 2))
            elif g > 0:
                p.wait("sync", S["e_gmm"], gA_count2(g - 1, 2 + sp2))
            for ic in range(IC):
                p.dma("sync", wgTs[sp2 % 2][ic][:],
                      wgq[sp2 * 1024:(sp2 + 1) * 1024, ic * 128:(ic + 1) * 128],
                      wssem, transpose=True)
                wstr_v[g][sp2] = p.dma(
                    "sync", wuTs[sp2 % 2][ic][:],
                    wuq[sp2 * 1024:(sp2 + 1) * 1024, ic * 128:(ic + 1) * 128],
                    wssem, transpose=True)

        xq_sems_all = [S["g_xqst"], S["g_xqst1"], S["g_xqst2"], S["g_xqst3"],
                       S["g_xqst4"], S["g_xqst5"], S["g_xqst6"], S["g_xqst7"]]

        def emit_group_streams_head(g):
            for t4 in range(4):
                tt = 4 * g + t4
                p.wait("sync", xq_sems_all[tt % 8], 16 * (tt // 8 + 1))
            if g >= 2:
                p.wait("sync", S["e_gmm"], gA_count(g - 2, 7))
            xqsem = S["s_xqtg"] if g % 2 == 0 else S["s_xqtg1"]
            for ic in range(IC):
                xqtg_v[g] = p.dma(
                    "sync", xqTg[g % 2][ic][:],
                    xqd[g * 512:(g + 1) * 512, ic * 128:(ic + 1) * 128],
                    xqsem, transpose=True)
            emit_stream_slice(g, 0)

        def emit_group_streams_tail(g):
            for sp2 in range(1, 4):
                emit_stream_slice(g, sp2)

        emit_group_streams_head(0)
        emit_group_streams_tail(0)

        s_d.close()
        s_y.close()
        s_ec = ExitStack()
        EC_ = s_ec.enter_context
        sgb = [EC_(nc.sbuf_tensor(f"sgb{j}", [128, 512], F32)) for j in range(2)]
        hg = [EC_(nc.sbuf_tensor(f"hg{i}", [128, INNER], BF16)) for i in range(GT)]
        hqs = [EC_(nc.sbuf_tensor(f"hqs{j}", [128, 512], F32)) for j in range(2)]
        hqt = EC_(nc.sbuf_tensor("hqt", [128, INNER], BF16))
        hqTgB = EC_(nc.sbuf_tensor("hqTgB", [128, 4 * INNER], BF16))
        ffo = EC_(nc.sbuf_tensor("ffo", [128, C], F32))
        z2b = EC_(nc.sbuf_tensor("z2b", [128, C], F32))
        outb = EC_(nc.sbuf_tensor("outb", [128, C], F32))
        rgag = [EC_(nc.sbuf_tensor(f"rgag{i}", [128, 1], F32)) for i in range(GT)]
        rgau = [EC_(nc.sbuf_tensor(f"rgau{i}", [128, 1], F32)) for i in range(GT)]
        rgad = [EC_(nc.sbuf_tensor(f"rgad{i}", [128, 1], F32)) for i in range(GT)]
        spE = [EC_(nc.sbuf_tensor(f"spE{i}", [128, 6], F32)) for i in range(GT)]

        for g in range(NG):
            p.wait("vector", S["pp_fin"], prep_fin_v[2])
            p.wait("vector", S["d_sg6"], d_q2_done)
            for t4 in range(GT):
                tt = 4 * g + t4
                p.op("vector", lambda e, t4=t4, tt=tt: e.tensor_mul(
                    rgag[t4][:], rga_all[:, tt:tt + 1], rgw_col[0][:]))
                p.op("vector", lambda e, t4=t4, tt=tt: e.tensor_mul(
                    rgau[t4][:], rga_all[:, tt:tt + 1], rgw_col[1][:]))
            rgx_v = vsp(S["e_sgA"])

            h_done_v = [0] * GT
            for sl in range(8):
                sp2, half = sl // 2, sl % 2
                p.wait("tensor", S["s_wstr"] if sp2 % 2 == 0 else S["s_wstr1"],
                       wstr_v[g][sp2])
                if sl == 0:
                    p.wait("tensor",
                           S["s_xqtg"] if g % 2 == 0 else S["s_xqtg1"],
                           xqtg_v[g])
                for t4 in range(GT):
                    gbi = t4 % 2
                    ubi = 2 + t4 % 2
                    pe_writes_bank(gbi)
                    for ic in range(IC):
                        p.op("tensor", lambda e, ic=ic, t4=t4, g=g, gbi=gbi, sp2=sp2, half=half: e.matmul(
                            bank(gbi),
                            xqTg[g % 2][ic][:, t4 * 128:(t4 + 1) * 128],
                            wgTs[sp2 % 2][ic][:, half * 512:(half + 1) * 512],
                            start=(ic == 0), stop=(ic == IC - 1)),
                            S["e_gmm"] if ic == IC - 1 else None)
                    gmm_v = S["e_gmm"].n
                    note_bank_write(gbi, S["e_gmm"], gmm_v)
                    pe_writes_bank(ubi)
                    for ic in range(IC):
                        p.op("tensor", lambda e, ic=ic, t4=t4, g=g, ubi=ubi, sp2=sp2, half=half: e.matmul(
                            bank(ubi),
                            xqTg[g % 2][ic][:, t4 * 128:(t4 + 1) * 128],
                            wuTs[sp2 % 2][ic][:, half * 512:(half + 1) * 512],
                            start=(ic == 0), stop=(ic == IC - 1)),
                            S["e_gmm"] if ic == IC - 1 else None)
                    umm_v = S["e_gmm"].n
                    note_bank_write(ubi, S["e_gmm"], umm_v)
                    spar = t4 % 2
                    reader_waits_bank("scalar", gbi)
                    p.wait("scalar", S["e_sgA"], rgx_v)
                    if sl > 0 or t4 >= 2:
                        p.wait("scalar", S["e_hD"], S["e_hD"].n)
                    sg_v = p.op("scalar", lambda e, spar=spar, gbi=gbi, t4=t4: e.activation(
                        sgb[spar][:], bank(gbi), AF.Silu, scale=rgag[t4][:]),
                        S["e_sgA"])
                    note_bank_read(gbi, S["e_sgA"], sg_v)
                    reader_waits_bank("vector", ubi)
                    p.wait("vector", S["e_sgA"], sg_v)
                    h_v = p.op("vector", lambda e, spar=spar, ubi=ubi, t4=t4, sl=sl: e.scalar_tensor_tensor(
                        hg[t4][:, sl * 512:(sl + 1) * 512], bank(ubi),
                        rgau[t4][:], sgb[spar][:], ALU.mult, ALU.mult),
                        S["e_hD"])
                    note_bank_read(ubi, S["e_hD"], h_v)
                    h_done_v[t4] = h_v

            if g + 1 < NG:
                emit_group_streams_head(g + 1)

            if debug == 1:
                for t4 in range(GT):
                    tt = 4 * g + t4
                    p.wait("gpsimd", S["e_hD"], h_done_v[t4])
                    p.dma("gpsimd", dbgh[tt * 128:(tt + 1) * 128, :],
                          hg[t4][:], S["misc2"])

            quad_i = 0
            pending_tr = None
            tr_ev = [0] * 8
            tr_tile_done = 0
            for t4 in range(GT):
                p.wait("vector", S["e_hD"], h_done_v[t4])
                p.op("vector", lambda e, t4=t4, s=spE[t4]: e.tensor_reduce(
                    s[:, 0:1], hg[t4][:], AX.X, ALU.max,
                    apply_absolute_value=True))
            for t4 in range(GT):
                p.op("vector", lambda e, s=spE[t4]: e.tensor_scalar_max(
                    s[:, 1:2], s[:, 0:1], 1e-5))
            for t4 in range(GT):
                p.op("vector", lambda e, s=spE[t4]: e.reciprocal(
                    s[:, 2:3], s[:, 1:2]))
            for t4 in range(GT):
                p.op("vector", lambda e, s=spE[t4], t4=t4: e.scalar_tensor_tensor(
                    rgad[t4][:], s[:, 1:2], 1.0 / 127.0, rgw_col[2][:],
                    ALU.mult, ALU.mult))
            for t4 in range(GT):
                p.op("vector", lambda e, s=spE[t4]: e.tensor_scalar_mul(
                    s[:, 3:4], s[:, 2:3], 127.0))
            amx_v = vsp(S["e_amx"])
            p.wait("scalar", S["e_amx"], amx_v)

            def emit_tr_evac(pt, gate):
                t4q, chkq, b = pt
                eng = "scalar" if b % 2 == 0 else "vector"
                sem = S["e_trev"] if b % 2 == 0 else S["e_trev2"]
                if gate is not None:
                    p.wait(eng, S["e_trm"], gate)
                else:
                    p.wait(eng, S["e_trm"], S["e_trm"].n)
                    if eng == "scalar":
                        ja(); ja()
                    else:
                        jv(); jv()
                ev = p.op(eng, lambda e, t4q=t4q, chkq=chkq, b=b: e.tensor_copy(
                    hqTgB[:, t4q * 4096 + chkq * 512:t4q * 4096 + chkq * 512 + 512],
                    bank(b)) if b % 2
                    else e.activation(
                    hqTgB[:, t4q * 4096 + chkq * 512:t4q * 4096 + chkq * 512 + 512],
                    bank(b), AF.Identity), sem)
                tr_ev[b] = ev
                note_bank_read(b, sem, ev)

            for t4 in range(GT):
                s = spE[t4]
                if tr_tile_done:
                    p.wait("vector", S["e_trm"], tr_tile_done)
                for chk in range(8):
                    qpar = chk % 2
                    if chk >= 2:
                        p.wait("scalar", S["e_qD"], S["e_qD"].n)
                    qa = p.op("scalar", lambda e, t4=t4, chk=chk, qpar=qpar, s=s: e.activation(
                        hqs[qpar][:], hg[t4][:, chk * 512:(chk + 1) * 512],
                        AF.Identity, scale=s[:, 3:4], bias=rcc[:]), S["e_qA"])
                    p.wait("vector", S["e_qA"], qa)
                    p.op("vector", lambda e, chk=chk, qpar=qpar: e.tensor_scalar(
                        hqt[:, chk * 512:(chk + 1) * 512], hqs[qpar][:],
                        -RC, None, ALU.add), S["e_qD"])
                    chk_v = vsp(S["e_qD"])
                    p.wait("tensor", S["e_qD"], chk_v)
                    b = quad_i % 8
                    if quad_i < 8:
                        pe_writes_bank(b)
                    if tr_ev[b]:
                        p.wait("tensor",
                               S["e_trev"] if b % 2 == 0 else S["e_trev2"],
                               tr_ev[b])
                    tm2 = None
                    for bq in range(4):
                        icn = chk * 4 + bq
                        tm = p.op("tensor", lambda e, icn=icn, b=b, bq=bq: e.matmul(
                            bank(b)[:, bq * 128:(bq + 1) * 128],
                            hqt[:, icn * 128:(icn + 1) * 128], ident[:],
                            start=True, stop=True), S["e_trm"])
                        if bq == 1:
                            tm2 = tm
                    if pending_tr is not None:
                        emit_tr_evac(pending_tr, tm2)
                    pending_tr = (t4, chk, b)
                    quad_i += 1
                    if not (t4 == GT - 1 and chk == 7):
                        bn = quad_i % 8
                        if tr_ev[bn]:
                            p.wait("tensor",
                                   S["e_trev"] if bn % 2 == 0 else S["e_trev2"],
                                   tr_ev[bn])
                        p.wait("tensor", S["e_qA"], S["e_qA"].n + 1)
                        p.op("tensor", lambda e, bn=bn: e.matmul(
                            bank(bn)[0:1, 0:1], ones_row_b[:, 0:1],
                            ones_row_b[:, 0:1], start=True, stop=True))
                tr_tile_done = S["e_trm"].n
                hq_v = S["e_qD"].n
                if debug == 1:
                    tt = 4 * g + t4
                    p.wait("gpsimd", S["e_qD"], hq_v)
                    dmp = p.dma("gpsimd", dbghq[tt * 128:(tt + 1) * 128, :],
                                hqt[:], S["misc2"])
                    p.wait("vector", S["misc2"], dmp)
            emit_tr_evac(pending_tr, None)
            pending_tr = None
            trA_fin = asp(S["e_trev"])
            trD_fin = vsp(S["e_trev2"])

            wdc_sems = [S["s_wdc"], S["s_wdc1"], S["s_wdc2"], S["s_wdc3"]]
            dn_mm_v = 0
            p.wait("tensor", S["e_trev"], trA_fin)
            p.wait("tensor", S["e_trev2"], trD_fin)
            for icn in range(32):
                dpar = icn % 4
                if icn >= 4:
                    p.wait("scalar", S["e_dmm"], S["e_dmm"].n - 24)
                elif g > 0:
                    p.wait("scalar", S["e_dmm"], g * 256)
                wdv = p.dma("scalar", wdTc[dpar][:],
                            wdq[0:C, icn * 128:(icn + 1) * 128],
                            wdc_sems[dpar], transpose=True)
                p.wait("tensor", wdc_sems[dpar], wdv)
                if icn == 0:
                    for b in range(8):
                        pe_writes_bank(b)
                for t4 in range(GT):
                    for oc2 in range(2):
                        dn_mm_v = p.op("tensor", lambda e, icn=icn, t4=t4, oc2=oc2, dpar=dpar: e.matmul(
                            bank(t4 * 2 + oc2),
                            hqTgB[:, t4 * 4096 + icn * 128:t4 * 4096 + icn * 128 + 128],
                            wdTc[dpar][:, oc2 * 512:(oc2 + 1) * 512],
                            start=(icn == 0), stop=(icn == 31)), S["e_dmm"])
            for b in range(8):
                note_bank_write(b, S["e_dmm"], dn_mm_v)

            if g + 1 < NG:
                emit_group_streams_tail(g + 1)

            for t4 in range(GT):
                tt = 4 * g + t4
                par = t4 % 2
                p.wait("scalar", S["e_dmm"], dn_mm_v)
                if g > 0 or t4 > 0:
                    p.wait("scalar", S["e_fin"], S["e_fin"].n)
                    p.wait("scalar", S["d_sg1"], S["d_sg1"].n)
                reader_waits_bank("scalar", t4 * 2)
                ev = p.op("scalar", lambda e, t4=t4: e.activation(
                    ffo[:, 0:512],
                    bank(t4 * 2), AF.Identity, scale=rgad[t4][:]),
                    S["e_deA"])
                note_bank_read(t4 * 2, S["e_deA"], ev)
                reader_waits_bank("vector", t4 * 2 + 1)
                if g > 0 or t4 > 0:
                    p.wait("vector", S["d_sg4"], S["d_sg4"].n)
                evD = p.op("vector", lambda e, t4=t4: e.tensor_scalar(
                    ffo[:, 512:1024], bank(t4 * 2 + 1), rgad[t4][:],
                    None, ALU.mult), S["e_trev2"])
                note_bank_read(t4 * 2 + 1, S["e_trev2"], evD)
                if debug == 1:
                    tt2 = 4 * g + t4
                    p.wait("gpsimd", S["e_deA"], ev)
                    dmp2 = p.dma("gpsimd", dbgffo[tt2 * 128:(tt2 + 1) * 128, :],
                                 ffo[:], S["misc2"])
                    p.wait("scalar", S["misc2"], dmp2)
                    p.wait("vector", S["misc2"], dmp2)
                rs_v, nm_v = ln_stats(par, ffo[:], S["d_sg1"], S["d_sg2"],
                                      S["d_sg3"],
                                      pre_waits_v=[("vector", S["e_deA"], ev)])
                p.wait("scalar", S["d_sg3"], nm_v)
                z2_v = p.op("scalar", lambda e, par=par: e.activation(
                    z2b[:], ffo[:], AF.Identity, scale=rstd[par],
                    bias=nmrc[par][:]), S["d_sg4"])
                p.wait("gpsimd", S["g_x1st"], x1st_v[NT - 1])
                p.wait("gpsimd", S["e_fin"], S["e_fin"].n)
                xsem = S["s_xld"] if par == 0 else S["s_xld1"]
                x1ld = p.dma("gpsimd", xt[par][:],
                             x1d[tt * 128:(tt + 1) * 128, :], xsem)
                p.wait("vector", S["d_sg4"], z2_v)
                p.wait("vector", xsem, x1ld)
                if g > 0 or t4 > 0:
                    p.wait("vector", S["g_outst"], S["g_outst"].n)
                o_v = p.op("vector", lambda e, par=par: e.tensor_add(
                    outb[:], z2b[:], xt[par][:]), S["e_fin"])
                p.wait("gpsimd", S["e_fin"], o_v)
                p.dma("gpsimd", out_h[tt * 128:(tt + 1) * 128, :], outb[:],
                      S["g_outst"])

        p.wait("gpsimd", S["g_outst"], 16 * NT)
        s_ec.close()
        s_es.close()

        with nc.Block() as block:
            @block.sync
            def _(eng):
                for f in p.ops["sync"]:
                    f(eng)

            @block.gpsimd
            def _(eng):
                for f in p.ops["gpsimd"]:
                    f(eng)

            @block.scalar
            def _(eng):
                for f in p.ops["scalar"]:
                    f(eng)

            @block.vector
            def _(eng):
                for f in p.ops["vector"]:
                    f(eng)

            @block.tensor
            def _(eng):
                for f in p.ops["tensor"]:
                    f(eng)

    return nc


_NC_CACHE = None


def kernel(L0_attn_x, L0_attn_Wq, L0_attn_Wk, L0_attn_Wv, L0_attn_Wp,
           L0_ffn_w_gate, L0_ffn_w_up, L0_ffn_w_down):
    global _NC_CACHE
    if _NC_CACHE is None:
        _NC_CACHE = build_nc()
    nc = _NC_CACHE
    x = np.ascontiguousarray(np.asarray(L0_attn_x, dtype=np.float32))
    ws = {
        "wq": np.ascontiguousarray(np.asarray(L0_attn_Wq, dtype=np.float32)),
        "wk": np.ascontiguousarray(np.asarray(L0_attn_Wk, dtype=np.float32)),
        "wv": np.ascontiguousarray(np.asarray(L0_attn_Wv, dtype=np.float32)),
        "wp": np.ascontiguousarray(np.asarray(L0_attn_Wp, dtype=np.float32)),
        "wg": np.ascontiguousarray(np.asarray(L0_ffn_w_gate, dtype=np.float32)),
        "wu": np.ascontiguousarray(np.asarray(L0_ffn_w_up, dtype=np.float32)),
        "wd": np.ascontiguousarray(np.asarray(L0_ffn_w_down, dtype=np.float32)),
    }
    in_maps = [dict(ws, x=x[i]) for i in range(8)]
    res = run_bass_kernel_spmd(nc, in_maps, core_ids=list(range(8)))
    return np.stack([res.results[i]["out"] for i in range(8)], axis=0)
